# version 21
# baseline (speedup 1.0000x reference)
import numpy as np
import ml_dtypes
from contextlib import ExitStack

import concourse.bass as bass
import concourse.mybir as mybir
from concourse.bass_utils import run_bass_kernel_spmd

F32 = mybir.dt.float32
BF16 = mybir.dt.bfloat16
AF = mybir.ActivationFunctionType
ALU = mybir.AluOpType
AX = mybir.AxisListType
BFD = ml_dtypes.bfloat16

D = 1024
SQ = 4096
NHEAD = 16
DT = 8
NT = 32
OT = 16
NCH = 8
NJ = 4
FCH = [(0, 4), (4, 4), (8, 4), (12, 4), (16, 4), (20, 2)]
CAP = 768
CT = 6
THETA = 10000.0

SEMS = ("T", "V", "S", "G", "dC", "dT0", "dT1", "dXa", "dXb", "dXc", "dXd", "dXQa", "dXQb", "dQW", "dQR",
        "dXR", "dX2W", "dX2X", "dHW", "dWgt", "dMA", "dMB", "dMD", "dME", "dHR", "dX2R", "dX2S", "dOut", "dOu2")


class Builder:
    def __init__(self, nc, st):
        self.nc = nc
        self.ops = {k: [] for k in "TVSGY"}
        self.cnt = {}
        self.sem = {}
        self.lastwait = {k: {} for k in "TVSGY"}
        for n in SEMS:
            self.sem[n] = st.enter_context(nc.semaphore(n))
            self.cnt[n] = 0

    def emit(self, eng, fn, waits=(), inc=None, k=1):
        lw = self.lastwait[eng]
        ws = []
        dedup = eng not in "YG"
        for s, v in waits:
            if v > 0 and (not dedup or lw.get(s, 0) < v):
                ws.append((s, v))
                if lw.get(s, 0) < v:
                    lw[s] = v
        if inc is not None:
            self.cnt[inc] += k
        sems = self.sem

        def run(e, ws=tuple(ws), fn=fn, inc=inc, k=k):
            for s, v in ws:
                e.wait_ge(sems[s], v)
            ins = fn(e)
            if inc is not None and ins is not None:
                ins.then_inc(sems[inc], k)
        self.ops[eng].append(run)

    def wait_only(self, eng, waits):
        self.emit(eng, lambda e: None, waits=waits)

    def barrier(self):
        snap = [(n, self.cnt[n]) for n in SEMS if self.cnt[n] > 0]
        for eng in "TVSGY":
            self.wait_only(eng, [(n, v) for n, v in snap if n != eng])

    def c(self, n):
        return self.cnt[n]

    def vv(self):
        return ("V", self.cnt["V"])

    def ss(self):
        return ("S", self.cnt["S"])

    def tt(self):
        return ("T", self.cnt["T"])


def build_program():
    nc = bass.Bass("TRN2", target_bir_lowering=False)

    x_d = nc.dram_tensor("x", [SQ, D], F32, kind="ExternalInput")
    xown_d = nc.dram_tensor("xown", [2048, D], F32, kind="ExternalInput")
    wqT_d = nc.dram_tensor("wqT", [128, DT, D], F32, kind="ExternalInput")
    wkT_d = nc.dram_tensor("wkT", [128, DT, D], F32, kind="ExternalInput")
    wvT_d = nc.dram_tensor("wvT", [128, DT, D], F32, kind="ExternalInput")
    woT_d = nc.dram_tensor("woT", [128, DT, D], F32, kind="ExternalInput")
    bq_d = nc.dram_tensor("bq128", [128, 8], F32, kind="ExternalInput")
    bk_d = nc.dram_tensor("bk128", [128, 8], F32, kind="ExternalInput")
    bv_d = nc.dram_tensor("bvbc", [128, D], F32, kind="ExternalInput")
    bo_d = nc.dram_tensor("bobc", [128, D], F32, kind="ExternalInput")
    rw_d = nc.dram_tensor("rw", [128, DT, 8], F32, kind="ExternalInput")
    gate6_d = nc.dram_tensor("gate6", [8, 5, 128, DT, 512], F32, kind="ExternalInput")
    gate4_d = nc.dram_tensor("gate4", [8, 128, DT, 256], F32, kind="ExternalInput")
    up6_d = nc.dram_tensor("up6", [8, 5, 128, DT, 512], F32, kind="ExternalInput")
    up4_d = nc.dram_tensor("up4", [8, 128, DT, 256], F32, kind="ExternalInput")
    down6_d = nc.dram_tensor("down6", [8, 5, 128, 4, D], F32, kind="ExternalInput")
    down4_d = nc.dram_tensor("down4", [8, 128, 2, D], F32, kind="ExternalInput")
    iota_d = nc.dram_tensor("iota768", [128, 768], F32, kind="ExternalInput")
    trils_d = nc.dram_tensor("trils", [128, 128], BF16, kind="ExternalInput")
    onesf_d = nc.dram_tensor("onesf", [128, 128], BF16, kind="ExternalInput")
    cosK_d = nc.dram_tensor("cosK", [128, SQ], F32, kind="ExternalInput")
    sinK_d = nc.dram_tensor("sinK", [128, SQ], F32, kind="ExternalInput")
    cosQ_d = nc.dram_tensor("cosQ", [128, 2048], F32, kind="ExternalInput")
    sinQ_d = nc.dram_tensor("sinQ", [128, 2048], F32, kind="ExternalInput")
    masks_d = nc.dram_tensor("masks", [128, 8, 512], BF16, kind="ExternalInput")
    perm_d = nc.dram_tensor("perm128", [128, 128], BF16, kind="ExternalInput")
    idb_d = nc.dram_tensor("id128", [128, 128], BF16, kind="ExternalInput")
    sel0_d = nc.dram_tensor("sel0", [128, 128], BF16, kind="ExternalInput")
    out_d = nc.dram_tensor("out", [2048, D], F32, kind="ExternalOutput")

    qT_s = nc.dram_tensor("qTs", [DT, 128, 2048], BF16, kind="Internal")
    hTok_s = nc.dram_tensor("hToks", [2048, D], BF16, kind="Internal")
    x2_s = nc.dram_tensor("x2s", [2048, D], F32, kind="Internal")

    st = ExitStack()
    with st:
        b = Builder(nc, st)
        E = st.enter_context

        perm_sb = E(nc.sbuf_tensor("perm_sb", [128, 128], BF16))
        idb_sb = E(nc.sbuf_tensor("idb_sb", [128, 128], BF16))
        bq_sb = E(nc.sbuf_tensor("bq_sb", [128, 8], F32))
        bk_sb = E(nc.sbuf_tensor("bk_sb", [128, 8], F32))
        w_sb = E(nc.sbuf_tensor("w_sb", [128, OT, 8], F32))
        msel = E(nc.sbuf_tensor("msel", [128, OT, 8], BF16))
        st1 = E(nc.sbuf_tensor("st1", [128, 8], F32))
        st2 = E(nc.sbuf_tensor("st2", [128, 8], F32))
        nwv = E(nc.sbuf_tensor("nwv", [128, 8], F32))
        nwy = E(nc.sbuf_tensor("nwy", [128, 8], F32))
        nwu = E(nc.sbuf_tensor("nwu", [128, 8], F32))
        nmu = E(nc.sbuf_tensor("nmu", [128, 8], F32))

        sKV = ExitStack()
        kT_sb = sKV.enter_context(nc.sbuf_tensor("kT_sb", [128, DT, SQ], BF16))
        v_sb = sKV.enter_context(nc.sbuf_tensor("v_sb", [128, NT, NHEAD * 65], BF16))

        b.emit("Y", lambda e: e.dma_start(perm_sb[:], perm_d[:, :]), inc="dC", k=16)
        b.emit("Y", lambda e: e.dma_start(idb_sb[:], idb_d[:, :]), inc="dC", k=16)
        b.emit("Y", lambda e: e.dma_start(bq_sb[:], bq_d[:, :]), inc="dC", k=16)
        b.emit("Y", lambda e: e.dma_start(bk_sb[:], bk_d[:, :]), inc="dC", k=16)
        b.emit("G", lambda e: e.memset(v_sb[:, :, 64:NHEAD * 65:65], 1.0), inc="G")
        b.emit("S", lambda e: e.preload_act_table(AF.Exp), inc="S")
        dC0 = b.c("dC")

        def ln_stats_newton(ncols, co=0, extra=()):
            sV0 = b.c("V")
            sS0 = b.c("S")
            def SL(t):
                return t[:, co:co + ncols]
            b.emit("V", lambda e: e.tensor_single_scalar(
                SL(nmu), SL(st1), 1.0 / D, ALU.mult),
                waits=[("V", sV0)] + list(extra), inc="V")
            b.emit("V", lambda e: e.tensor_mul(SL(nwu), SL(nmu), SL(nmu)),
                   waits=[b.vv()], inc="V")
            b.emit("V", lambda e: e.scalar_tensor_tensor(
                SL(nwv), SL(st2), 1.0 / D, SL(nwu),
                ALU.mult, ALU.subtract),
                waits=[("S", sS0), b.vv()], inc="V")
            b.emit("V", lambda e: e.tensor_single_scalar(
                SL(nwv), SL(nwv), 1e-5, ALU.add),
                waits=[b.vv()], inc="V")
            b.emit("V", lambda e: e.tensor_single_scalar(
                SL(nwu), SL(nwv), 1.0, ALU.add),
                waits=[b.vv()], inc="V")
            b.emit("V", lambda e: e.reciprocal(SL(nwu), SL(nwu)),
                   waits=[b.vv()], inc="V")
            b.emit("V", lambda e: e.tensor_single_scalar(
                SL(nwy), SL(nwu), 2.0, ALU.mult),
                waits=[b.vv()], inc="V")
            for _ in range(4):
                b.emit("V", lambda e: e.tensor_mul(SL(nwu), SL(nwy), SL(nwy)),
                       waits=[b.vv()], inc="V")
                b.emit("V", lambda e: e.tensor_mul(SL(nwu), SL(nwu), SL(nwv)),
                       waits=[b.vv()], inc="V")
                b.emit("V", lambda e: e.tensor_scalar(
                    SL(nwu), SL(nwu), -0.5, 1.5, ALU.mult, ALU.add),
                    waits=[b.vv()], inc="V")
                b.emit("V", lambda e: e.tensor_mul(SL(nwy), SL(nwy), SL(nwu)),
                       waits=[b.vv()], inc="V")
            return b.c("V")

        with ExitStack() as sA:
            EA = sA.enter_context
            wk_sb = EA(nc.sbuf_tensor("wk_sb", [128, DT, D], BF16))
            wv_sb = EA(nc.sbuf_tensor("wv_sb", [128, DT, D], BF16))
            cosS = EA(nc.sbuf_tensor("cosS", [128, 1, 512], F32))
            sinS = EA(nc.sbuf_tensor("sinS", [128, 1, 512], F32))
            ln1T = EA(nc.sbuf_tensor("ln1T", [128, DT, 512], BF16))
            xbuf = EA(nc.sbuf_tensor("xbuf", [128, 4, D], F32))
            lnb = EA(nc.sbuf_tensor("lnb", [128, D], BF16))
            tmp1 = EA(nc.sbuf_tensor("tmp1", [128, 512], BF16))
            tmp2 = EA(nc.sbuf_tensor("tmp2", [128, 512], BF16))
            bv_sb = EA(nc.sbuf_tensor("bv_sb", [128, D], BF16))
            psT = [EA(nc.psum_tensor(f"psT{i}", [128, 1024], BF16)) for i in range(2)]
            kqv = [EA(nc.psum_tensor(f"kqv{i}", [128, 512], F32)) for i in range(2)]
            shf = EA(nc.psum_tensor("shf", [128, 512], F32))
            dump = EA(nc.psum_tensor("dump", [128, 1024], F32))

            b.emit("G", lambda e: e.dma_start(
                wk_sb[:], wkT_d[:, :, :]), inc="dWgt", k=16)
            b.emit("G", lambda e: e.dma_start(
                wv_sb[:], wvT_d[:, :, :]), inc="dWgt", k=16)
            b.emit("G", lambda e: e.dma_start(bv_sb[:], bv_d[:, :]), inc="dWgt", k=16)
            dC1 = b.c("dC")
            dW1 = b.c("dWgt")

            xfree = [None] * 4
            cfree = [None, None]
            stats_pipe = {}

            xw_pipe = {}

            def ph1_loads(c):
                for t in range(4):
                    tt = 4 * c + t
                    s8 = t
                    xsem = ("dXa", "dXb", "dXc", "dXd")[t]
                    lw = [x for x in [xfree[s8]] if x]
                    b.emit("Y", lambda e, tt=tt, s8=s8: e.dma_start(
                        xbuf[:, s8, :], x_d[tt * 128:(tt + 1) * 128, :]),
                        waits=lw, inc=xsem, k=16)
                    xw_pipe[(c, t)] = (xsem, b.c(xsem))

            def ph1_stats(c):
                nonlocal dump_free
                co = 4 * (c % 2)
                for t in range(4):
                    s8 = t
                    xw = xw_pipe[(c, t)]
                    b.emit("V", lambda e, s8=s8, co=co, t=t: e.reduce_sum(
                        st1[:, co + t:co + t + 1], xbuf[:, s8, :], axis=AX.X, negate=True),
                        waits=[xw], inc="V")
                    dw = [x for x in [dump_free] if x]
                    b.emit("S", lambda e, s8=s8, co=co, t=t: e.activation(
                        dump[:, :], xbuf[:, s8, :], AF.Square,
                        accum_out=st2[:, co + t:co + t + 1]),
                        waits=[xw] + dw, inc="S")
                    dump_free = b.ss()
                stats_pipe[c] = ln_stats_newton(4, co=co)

            def ph1_load_stats(c):
                ph1_loads(c)
                ph1_stats(c)
            lnb_free = None
            psT_free = [None, None]
            kqv_free = [None, None]
            shf_free = None
            ln1T_free = None
            dump_free = None

            ph1_load_stats(0)
            for c in range(NCH):
                cslot = 0
                tsem = "dT" + str(c % 2)
                cw = [x for x in [cfree[cslot]] if x]
                b.emit("Y", lambda e, c=c, cslot=cslot: e.dma_start(
                    cosS[:, cslot, :], cosK_d[:, c * 512:(c + 1) * 512]),
                    waits=cw, inc=tsem, k=16)
                b.emit("Y", lambda e, c=c, cslot=cslot: e.dma_start(
                    sinS[:, cslot, :], sinK_d[:, c * 512:(c + 1) * 512]),
                    inc=tsem, k=16)
                tab_w = (tsem, b.c(tsem))

                co = 4 * (c % 2)
                stats_done = stats_pipe[c]

                for t in range(4):
                    s8 = t
                    lw = [("V", stats_done)] + ([lnb_free] if lnb_free else [])
                    b.emit("V", lambda e, s8=s8, co=co, t=t: e.tensor_scalar(
                        lnb[:, :], xbuf[:, s8, :], nmu[:, co + t:co + t + 1],
                        nwy[:, co + t:co + t + 1], ALU.add, ALU.mult),
                        waits=lw, inc="V")
                    lnw = b.vv()
                    xfree[s8] = lnw
                    bk2 = t % 2
                    wl = [lnw, ("dC", dC1)]
                    if psT_free[bk2]:
                        wl.append(psT_free[bk2])
                    for dt in range(DT):
                        b.emit("T", lambda e, bk2=bk2, dt=dt: e.transpose(
                            psT[bk2][:, dt * 128:(dt + 1) * 128],
                            lnb[:, dt * 128:(dt + 1) * 128], idb_sb[:]),
                            waits=(wl if dt == 0 else ()), inc="T" if dt == DT - 1 else None)
                    trw = b.tt()
                    lnb_free = trw
                    wl = [trw, b.vv()]
                    if t == 0 and ln1T_free:
                        wl.append(ln1T_free)
                    b.emit("V", lambda e, bk2=bk2, t=t: e.tensor_copy(
                        ln1T[:, :, t * 128:(t + 1) * 128],
                        psT[bk2][:, :].rearrange("p (a q) -> p a q", a=DT)),
                        waits=wl, inc="V")
                    psT_free[bk2] = b.vv()
                lnT_ready = b.vv()
                if c + 1 < NCH:
                    ph1_loads(c + 1)

                for ft in range(8):
                    bank = ft % 2
                    wl = [lnT_ready, ("dWgt", dW1)]
                    if kqv_free[bank]:
                        wl.append(kqv_free[bank])
                    for dt in range(DT):
                        b.emit("T", lambda e, bank=bank, ft=ft, dt=dt: e.matmul(
                            kqv[bank][:, :], wk_sb[:, dt, ft * 128:(ft + 1) * 128],
                            ln1T[:, dt, :], start=(dt == 0), stop=(dt == DT - 1)),
                            waits=(wl if dt == 0 else ()), inc=("T" if dt == DT - 1 else None))
                    mmw = b.tt()
                    b.emit("S", lambda e, bank=bank, ft=ft, c=c: e.activation(
                        kT_sb[:, ft, c * 512:(c + 1) * 512], kqv[bank][:, :], AF.Identity,
                        bias=bk_sb[:, ft:ft + 1]),
                        waits=[mmw, ("dC", dC1)], inc="S")
                    kqv_free[bank] = b.ss()
                kevac = b.c("S")

                for t in range(4):
                    tt = 4 * c + t
                    for nh in range(2):
                        bank = (t * 2 + nh) % 2
                        wl = [lnT_ready]
                        if kqv_free[bank]:
                            wl.append(kqv_free[bank])
                        for dt in range(DT):
                            b.emit("T", lambda e, bank=bank, t=t, nh=nh, dt=dt: e.matmul(
                                kqv[bank][:, :], ln1T[:, dt, t * 128:(t + 1) * 128],
                                wv_sb[:, dt, nh * 512:(nh + 1) * 512],
                                start=(dt == 0), stop=(dt == DT - 1)),
                                waits=(wl if dt == 0 else ()), inc=("T" if dt == DT - 1 else None))
                        mmw = b.tt()
                        b.emit("V", lambda e, bank=bank, tt=tt, nh=nh: e.tensor_add(
                            v_sb[:, tt, nh * 520:(nh + 1) * 520]
                            .rearrange("p (h q) -> p h q", h=8)[:, :, 0:64],
                            kqv[bank][:, :].rearrange("p (h q) -> p h q", h=8),
                            bv_sb[:, nh * 512:(nh + 1) * 512]
                            .rearrange("p (h q) -> p h q", h=8)),
                            waits=[mmw, ("dC", dC1), ("G", 1)], inc="V")
                        kqv_free[bank] = b.vv()
                if c + 1 < NCH:
                    ph1_stats(c + 1)

                for ft in range(8):
                    wl = [("S", kevac), ("dC", dC0)]
                    if shf_free:
                        wl.append(shf_free)
                    b.emit("T", lambda e, ft=ft, c=c: e.matmul(
                        shf[:, :], perm_sb[:, :], kT_sb[:, ft, c * 512:(c + 1) * 512],
                        start=True, stop=True),
                        waits=wl, inc="T")
                    shw = b.tt()
                    b.emit("V", lambda e, ft=ft, c=c, cslot=cslot: e.tensor_mul(
                        tmp1[:, :], kT_sb[:, ft, c * 512:(c + 1) * 512], cosS[:, cslot, :]),
                        waits=[("S", kevac), tab_w, b.vv()], inc="V")
                    b.emit("V", lambda e, cslot=cslot: e.tensor_mul(
                        tmp2[:, :], shf[:, :], sinS[:, cslot, :]),
                        waits=[shw, b.vv()], inc="V")
                    shf_free = b.vv()
                    b.emit("V", lambda e, ft=ft, c=c: e.tensor_add(
                        kT_sb[:, ft, c * 512:(c + 1) * 512], tmp1[:, :], tmp2[:, :]),
                        waits=[b.vv()], inc="V")
                cfree[cslot] = b.vv()
                ln1T_free = b.tt()

        b.barrier()

        with ExitStack() as sB:
            EB = sB.enter_context
            wq_sb = EB(nc.sbuf_tensor("wq_sb", [128, DT, D], BF16))
            cosQs = EB(nc.sbuf_tensor("cosQs", [128, 2048], F32))
            sinQs = EB(nc.sbuf_tensor("sinQs", [128, 2048], F32))
            lnq = EB(nc.sbuf_tensor("lnq", [128, DT, 256], BF16))
            xq = EB(nc.sbuf_tensor("xq", [128, 2, D], F32))
            lnb2 = EB(nc.sbuf_tensor("lnb2", [128, D], BF16))
            qTq = EB(nc.sbuf_tensor("qTq", [128, DT, 256], BF16))
            tq1 = EB(nc.sbuf_tensor("tq1", [128, 256], F32))
            tq2 = EB(nc.sbuf_tensor("tq2", [128, 256], F32))
            psT2 = [EB(nc.psum_tensor(f"psTa{i}", [128, 1024], BF16)) for i in range(2)]
            qps = [EB(nc.psum_tensor(f"qps{i}", [128, 256], F32)) for i in range(2)]
            shq = EB(nc.psum_tensor("shq", [128, 256], F32))
            dump2 = EB(nc.psum_tensor("dump2", [128, 1024], F32))

            b.emit("Y", lambda e: e.dma_start(cosQs[:], cosQ_d[:, :]), inc="dC", k=16)
            b.emit("Y", lambda e: e.dma_start(sinQs[:], sinQ_d[:, :]), inc="dC", k=16)
            b.emit("G", lambda e: e.dma_start(
                wq_sb[:], wqT_d[:, :, :]), inc="dWgt", k=16)
            dC2 = b.c("dC")
            dW2 = b.c("dWgt")

            xqfree = [None, None]
            lnb2_free = None
            psT2_free = [None, None]
            qps_free = [None, None]
            shq_free = None
            lnq_free = None
            qTq_free = None
            dump2_free = None

            def ph2_front(pr):
                nonlocal dump2_free
                for t in range(2):
                    slot = 2 * pr + t
                    xsem = ("dXQa", "dXQb")[t]
                    lw = [x for x in [xqfree[t]] if x]
                    b.emit("Y", lambda e, slot=slot, t=t: e.dma_start(
                        xq[:, t, :], xown_d[slot * 128:(slot + 1) * 128, :]),
                        waits=lw, inc=xsem, k=16)
                    xw = (xsem, b.c(xsem))
                    b.emit("V", lambda e, t=t: e.reduce_sum(
                        st1[:, t:t + 1], xq[:, t, :], axis=AX.X, negate=True),
                        waits=[xw], inc="V")
                    dw = [x for x in [dump2_free] if x]
                    b.emit("S", lambda e, t=t: e.activation(
                        dump2[:, :], xq[:, t, :], AF.Square, accum_out=st2[:, t:t + 1]),
                        waits=[xw] + dw, inc="S")
                    dump2_free = b.ss()
                return ln_stats_newton(2)

            stats_next = ph2_front(0)
            for pr in range(8):
                stats_done = stats_next

                for t in range(2):
                    lw = [("V", stats_done)] + ([lnb2_free] if lnb2_free else [])
                    b.emit("V", lambda e, t=t: e.tensor_scalar(
                        lnb2[:, :], xq[:, t, :], nmu[:, t:t + 1], nwy[:, t:t + 1],
                        ALU.add, ALU.mult),
                        waits=lw, inc="V")
                    lnw = b.vv()
                    xqfree[t] = lnw
                    wl = [lnw, ("dC", dC2)]
                    if psT2_free[t]:
                        wl.append(psT2_free[t])
                    for dt in range(DT):
                        b.emit("T", lambda e, t=t, dt=dt: e.transpose(
                            psT2[t][:, dt * 128:(dt + 1) * 128],
                            lnb2[:, dt * 128:(dt + 1) * 128], idb_sb[:]),
                            waits=(wl if dt == 0 else ()), inc="T" if dt == DT - 1 else None)
                    trw = b.tt()
                    lnb2_free = trw
                    wl = [trw, b.vv()]
                    if t == 0 and lnq_free:
                        wl.append(lnq_free)
                    b.emit("V", lambda e, t=t: e.tensor_copy(
                        lnq[:, :, t * 128:(t + 1) * 128],
                        psT2[t][:, :].rearrange("p (a q) -> p a q", a=DT)),
                        waits=wl, inc="V")
                    psT2_free[t] = b.vv()
                lnq_ready = b.vv()
                if pr + 1 < 8:
                    stats_next = ph2_front(pr + 1)

                for ft in range(8):
                    bank = ft % 2
                    wl = [lnq_ready, ("dWgt", dW2)]
                    if qps_free[bank]:
                        wl.append(qps_free[bank])
                    if ft < 2 and qTq_free:
                        wl.append(qTq_free)
                    for dt in range(DT):
                        b.emit("T", lambda e, bank=bank, ft=ft, dt=dt: e.matmul(
                            qps[bank][:, :], wq_sb[:, dt, ft * 128:(ft + 1) * 128],
                            lnq[:, dt, :], start=(dt == 0), stop=(dt == DT - 1)),
                            waits=(wl if dt == 0 else ()), inc=("T" if dt == DT - 1 else None))
                    mmw = b.tt()
                    b.emit("S", lambda e, bank=bank, ft=ft: e.activation(
                        qTq[:, ft, :], qps[bank][:, :], AF.Identity, bias=bq_sb[:, ft:ft + 1]),
                        waits=[mmw, ("dC", dC2)] + ([qTq_free] if qTq_free else []), inc="S")
                    qps_free[bank] = b.ss()
                qevac = b.c("S")
                lnq_free = b.tt()

                for ft in range(8):
                    wl = [("S", qevac)] + ([shq_free] if shq_free else [])
                    b.emit("T", lambda e, ft=ft: e.matmul(
                        shq[:, :], perm_sb[:, :], qTq[:, ft, :], start=True, stop=True),
                        waits=wl + [("dC", dC2)], inc="T")
                    shw = b.tt()
                    b.emit("V", lambda e, ft=ft, pr=pr: e.tensor_mul(
                        tq1[:, :], qTq[:, ft, :], cosQs[:, pr * 256:(pr + 1) * 256]),
                        waits=[("S", qevac), ("dC", dC2), b.vv()], inc="V")
                    b.emit("V", lambda e, pr=pr: e.tensor_mul(
                        tq2[:, :], shq[:, :], sinQs[:, pr * 256:(pr + 1) * 256]),
                        waits=[shw, b.vv()], inc="V")
                    shq_free = b.vv()
                    b.emit("V", lambda e, ft=ft: e.tensor_add(
                        qTq[:, ft, :], tq1[:, :], tq2[:, :]),
                        waits=[b.vv()], inc="V")
                ropeq = b.vv()
                b.emit("Y", lambda e, pr=pr: e.dma_start(
                    qT_s[:, :, pr * 256:(pr + 1) * 256].rearrange("dt p q -> p dt q"),
                    qTq[:, :, :]),
                    waits=[ropeq], inc="dQW", k=16)
                qTq_free = ("dQW", b.c("dQW"))

        b.barrier()
        dQW_all = b.c("dQW")

        with ExitStack() as sC:
            EC = sC.enter_context
            wo_sb = EC(nc.sbuf_tensor("wo_sb", [128, DT, D], BF16))
            qTc = EC(nc.sbuf_tensor("qTc", [128, DT, 512], BF16))
            masks_sb = EC(nc.sbuf_tensor("masks_sb", [128, 8, 512], BF16))
            p_sb = EC(nc.sbuf_tensor("p_sb", [128, 8, 512], BF16))
            aoT = EC(nc.sbuf_tensor("aoT", [128, DT, 512], BF16))
            rcs = EC(nc.sbuf_tensor("rcs", [128, 512], BF16))
            bc_sb = EC(nc.sbuf_tensor("bc_sb", [128, 512], BF16))
            sel_sb = EC(nc.sbuf_tensor("sel_sb", [128, 128], BF16))
            xres = EC(nc.sbuf_tensor("xres", [128, D], F32))
            x2_sb = EC(nc.sbuf_tensor("x2_sb", [128, 2, D], F32))
            ln2b4 = EC(nc.sbuf_tensor("ln2b4", [128, 4, D], BF16))
            hTt = EC(nc.sbuf_tensor("hTt", [128, DT, 128], BF16))
            bo_sb = EC(nc.sbuf_tensor("bo_sb", [128, D], F32))
            rw_sb = EC(nc.sbuf_tensor("rw_sb", [128, DT, 8], BF16))
            pr_sb = EC(nc.sbuf_tensor("pr_sb", [128, 8], F32))
            p2_sb = EC(nc.sbuf_tensor("p2_sb", [128, 8], F32))
            mk1 = EC(nc.sbuf_tensor("mk1", [128, 8], F32))
            mk2 = EC(nc.sbuf_tensor("mk2", [128, 8], F32))
            rr1 = EC(nc.sbuf_tensor("rr1", [128, 1], F32))
            rr2 = EC(nc.sbuf_tensor("rr2", [128, 1], F32))
            rr3 = EC(nc.sbuf_tensor("rr3", [128, 1], F32))
            sps = [EC(nc.psum_tensor(f"sps{i}", [128, 1024], F32)) for i in range(2)]
            aops = [EC(nc.psum_tensor(f"aops{i}", [128, 512], F32)) for i in range(2)]
            bcps = EC(nc.psum_tensor("bcps", [128, 512], F32))
            psT3 = EC(nc.psum_tensor("psT3", [128, 512], BF16))
            ops_ = [sps[0], sps[1]]

            b.emit("Y", lambda e: e.dma_start(masks_sb[:], masks_d[:, :, :]), inc="dC", k=16)
            b.emit("Y", lambda e: e.dma_start(sel_sb[:], sel0_d[:, :]), inc="dC", k=16)
            b.emit("Y", lambda e: e.dma_start(bo_sb[:], bo_d[:, :]), inc="dC", k=16)
            b.emit("G", lambda e: e.dma_start(
                wo_sb[:], woT_d[:, :, :]), inc="dWgt", k=16)
            b.emit("G", lambda e: e.dma_start(
                rw_sb[:], rw_d[:, :, :]), inc="dWgt", k=16)
            b.emit("G", lambda e: e.memset(rcs[:, :], 0.0), inc="G")
            dC3 = b.c("dC")
            dW3 = b.c("dWgt")
            sG3 = b.c("G")

            exp_war = {}
            av_war = {}
            sps_war = {}
            lnfreeY = None
            lnwarT = [None] * 4
            ao_free = [None, None]
            aoT_reuse = None
            rcs_free = None
            ops_free = [None, None]
            x2free = {0: None, 1: None}
            hTt_free = None
            psT3_free = None
            qtc_war = None
            sq_s = None

            for j in range(NJ):
                nk = 8 * j + 8
                bc_war_v = b.vv()
                bc_war_s = b.ss()
                b.emit("Y", lambda e, j=j: e.dma_start(
                    qTc[:, :, :],
                    qT_s[:, :, j * 512:(j + 1) * 512].rearrange("dt p q -> p dt q")),
                    waits=[("dQW", dQW_all), qtc_war if qtc_war else b.tt()],
                    inc="dQR", k=16)
                qw = ("dQR", b.c("dQR"))

                epi_pending = None
                for h in range(NHEAD):
                    po = (h % 2) * 64
                    fth = h // 2
                    ab = h % 2
                    LEAD = 4 if j == 0 else (3 if j == 1 else 2)
                    npr = nk // 2
                    pready = {}
                    for pi in range(npr + LEAD):
                        if pi == 2 and epi_pending is not None:
                            epi_pending()
                            epi_pending = None
                        if pi < npr:
                            kp = pi
                            pb = kp % 2
                            ps4 = kp % 4
                            wl = [qw, ("dC", dC3)]
                            if exp_war.get(pb) is not None:
                                wl.append(("S", exp_war[pb]))
                            if sps_war.get(pb) is not None:
                                wl.append(sps_war[pb])
                                sps_war[pb] = None
                            for kk2 in range(2):
                                k = 2 * kp + kk2
                                b.emit("T", lambda e, k=k, pb=pb, kk2=kk2, po=po, fth=fth: e.matmul(
                                    sps[pb][:, kk2 * 512:(kk2 + 1) * 512],
                                    kT_sb[po:po + 64, fth, k * 128:(k + 1) * 128],
                                    qTc[po:po + 64, fth, :], start=True, stop=True),
                                    waits=(wl if kk2 == 0 else ()), inc=("T" if kk2 == 1 else None))
                            scw = b.tt()
                            wl = [scw]
                            if av_war.get(ps4) is not None:
                                wl.append(("T", av_war[ps4]))
                            b.emit("S", lambda e, pb=pb, ps4=ps4: e.activation(
                                p_sb[:, 2 * ps4:2 * ps4 + 2, :], sps[pb][:, :],
                                AF.Exp, scale=0.125),
                                waits=wl, inc="S")
                            exp_war[pb] = b.c("S")
                            pw = b.ss()
                            if kp >= 4 * j:
                                r0 = 2 * (kp - 4 * j)
                                b.emit("V", lambda e, ps4=ps4, r0=r0: e.tensor_mul(
                                    p_sb[:, 2 * ps4:2 * ps4 + 2, :],
                                    p_sb[:, 2 * ps4:2 * ps4 + 2, :],
                                    masks_sb[:, r0:r0 + 2, :]),
                                    waits=[pw], inc="V")
                                pw = b.vv()
                            pready[kp] = pw
                        if pi >= LEAD:
                            kp2 = pi - LEAD
                            ps4 = kp2 % 4
                            wl = [pready.pop(kp2), ("G", 1)]
                            if kp2 == 0 and ao_free[ab] is not None:
                                wl.append(ao_free[ab])
                            for kk2 in range(2):
                                k = 2 * kp2 + kk2
                                b.emit("T", lambda e, k=k, h=h, ps4=ps4, kk2=kk2, ab=ab, nk=nk: e.matmul(
                                    aops[ab][0:65, :], v_sb[:, k, h * 65:(h + 1) * 65],
                                    p_sb[:, 2 * ps4 + kk2, :], start=(k == 0), stop=(k == nk - 1)),
                                    waits=(wl if kk2 == 0 else ()), inc=("T" if kk2 == 1 else None))
                            av_war[ps4] = b.c("T")
                    av_last = b.tt()

                    def _epi(av_last=av_last, ab=ab, po=po, fth=fth):
                        nonlocal rcs_free
                        wl = [av_last, ("G", sG3)] + ([rcs_free] if rcs_free else [])
                        def _recip(e, ab=ab):
                            with nc.allow_low_precision(reason="softmax denominator reciprocal cast to bf16 for the selector broadcast matmul"):
                                return e.reciprocal(rcs[0:1, :], aops[ab][64:65, :])
                        b.emit("V", _recip, waits=wl, inc="V")
                        rw_ = b.vv()
                        b.emit("T", lambda e: e.matmul(
                            bcps[:, :], sel_sb[:, :], rcs[:, :], start=True, stop=True),
                            waits=[rw_, ("dC", dC3), bc_war_v, bc_war_s, b.ss()], inc="T")
                        bcw = b.tt()
                        rcs_free = bcw
                        b.emit("S", lambda e: e.activation(
                            bc_sb[0:64, :], bcps[0:64, :], AF.Identity),
                            waits=[bcw, b.ss()], inc="S")
                        bcev = b.ss()
                        wl = [bcev] + ([aoT_reuse] if aoT_reuse else [])
                        b.emit("V", lambda e, ab=ab, po=po, fth=fth: e.tensor_mul(
                            aoT[po:po + 64, fth, :], aops[ab][0:64, :], bc_sb[0:64, :]),
                            waits=wl, inc="V")
                        ao_free[ab] = b.vv()

                    if h == NHEAD - 1:
                        _epi()
                    else:
                        epi_pending = _epi
                aoT_ready = b.vv()
                qtc_war = b.tt()

                lnws = []
                for i in range(4):
                    slot = 4 * j + i
                    xp = i % 2
                    b.emit("Y", lambda e, slot=slot: e.dma_start(
                        xres[:, :], xown_d[slot * 128:(slot + 1) * 128, :]),
                        waits=[b.vv()] + ([sq_s] if sq_s else []), inc="dXR", k=16)
                    xw = ("dXR", b.c("dXR"))
                    wl = [xw, ("dC", dC3), b.vv()]
                    if x2free[xp] is not None:
                        wl.append(x2free[xp])
                    b.emit("V", lambda e, xp=xp: e.tensor_add(
                        x2_sb[:, xp, :], xres[:, :], bo_sb[:, :]),
                        waits=wl, inc="V")
                    xbw = b.vv()
                    for oc in range(2):
                        wl = [aoT_ready, ("dWgt", dW3)]
                        if ops_free[oc]:
                            wl.append(ops_free[oc])
                        if exp_war.get(oc) is not None:
                            wl.append(("S", exp_war[oc]))
                        for dt in range(DT):
                            b.emit("T", lambda e, oc=oc, i=i, dt=dt: e.matmul(
                                ops_[oc][:, 0:512], aoT[:, dt, i * 128:(i + 1) * 128],
                                wo_sb[:, dt, oc * 512:(oc + 1) * 512],
                                start=(dt == 0), stop=(dt == DT - 1)),
                                waits=(wl if dt == 0 else ()), inc=("T" if dt == DT - 1 else None))
                        mmw = b.tt()
                        b.emit("V", lambda e, oc=oc, xp=xp: e.tensor_add(
                            x2_sb[:, xp, oc * 512:(oc + 1) * 512], ops_[oc][:, 0:512],
                            x2_sb[:, xp, oc * 512:(oc + 1) * 512]),
                            waits=[mmw, xbw], inc="V")
                        ops_free[oc] = b.vv()
                    x2w = b.vv()
                    b.emit("V", lambda e, i=i, xp=xp: e.reduce_sum(
                        st1[:, i:i + 1], x2_sb[:, xp, :], axis=AX.X, negate=True),
                        waits=[x2w], inc="V")
                    b.emit("S", lambda e, i=i, xp=xp: e.activation(
                        xres[:, :], x2_sb[:, xp, :], AF.Square, accum_out=st2[:, i:i + 1]),
                        waits=[x2w, b.ss()], inc="S")
                    sq_s = b.ss()
                    stats_done = ln_stats_newton(1, co=i)
                    wl = [("V", stats_done)]
                    if lnwarT[i] is not None:
                        wl.append(lnwarT[i])
                    if lnfreeY is not None:
                        wl.append(lnfreeY)
                    b.emit("V", lambda e, i=i, xp=xp: e.tensor_scalar(
                        ln2b4[:, i, :], x2_sb[:, xp, :], nmu[:, i:i + 1], nwy[:, i:i + 1],
                        ALU.add, ALU.mult),
                        waits=wl, inc="V")
                    lnw = b.vv()
                    lnws.append(lnw)
                    b.emit("Y", lambda e, slot=slot, i=i: e.dma_start(
                        hTok_s[slot * 128:(slot + 1) * 128, :], ln2b4[:, i, :]),
                        waits=[lnw], inc="dHW", k=16)
                    xsm = "dX2W" if xp == 0 else "dX2X"
                    b.emit("Y", lambda e, slot=slot, xp=xp: e.dma_start(
                        x2_s[slot * 128:(slot + 1) * 128, :], x2_sb[:, xp, :]),
                        waits=[lnw], inc=xsm, k=16)
                    x2free[xp] = (xsm, b.c(xsm))
                lnfreeY = ("dHW", b.c("dHW"))
                aoT_reuse = b.tt()
                sps_war[0] = ops_free[0]
                sps_war[1] = ops_free[1]

                for i in range(4):
                    slot = 4 * j + i
                    for half in range(2):
                        for q in range(4):
                            dt = 4 * half + q
                            wl = [lnws[i]]
                            if q == 0 and psT3_free:
                                wl.append(psT3_free)
                            wl = wl + [("dC", dC3)]
                            b.emit("T", lambda e, q=q, dt=dt, i=i: e.transpose(
                                psT3[:, q * 128:(q + 1) * 128],
                                ln2b4[:, i, dt * 128:(dt + 1) * 128], idb_sb[:]),
                                waits=wl, inc="T" if q == 3 else None)
                        trw = b.tt()
                        wl = [trw] + (list(hTt_free) if (half == 0 and hTt_free) else [])
                        b.emit("V", lambda e, half=half: e.tensor_copy(
                            hTt[:, 4 * half:4 * half + 4, :],
                            psT3[:, :].rearrange("p (a q) -> p a q", a=4)),
                            waits=wl, inc="V")
                        psT3_free = b.vv()
                    lnwarT[i] = b.tt()
                    httw = b.vv()
                    for dt in range(DT):
                        b.emit("T", lambda e, dt=dt: e.matmul(
                            bcps[:, 0:8], hTt[:, dt, :], rw_sb[:, dt, :],
                            start=(dt == 0), stop=(dt == DT - 1)),
                            waits=([httw, ("dWgt", dW3), b.vv(), b.ss()] if dt == 0 else ()),
                            inc=("T" if dt == DT - 1 else None))
                    rmm = b.tt()
                    hTt_free = [rmm]
                    b.emit("V", lambda e: e.reduce_max(
                        rr1[:, 0:1], bcps[:, 0:8], axis=AX.X, negate=True),
                        waits=[rmm], inc="V")
                    b.emit("S", lambda e: e.activation(
                        pr_sb[:, :], bcps[:, 0:8], AF.Exp, bias=rr1[:, 0:1]),
                        waits=[b.vv(), b.ss()], inc="S")
                    spr = b.ss()
                    b.emit("V", lambda e: e.reduce_max(rr1[:, 0:1], pr_sb[:, :], axis=AX.X),
                           waits=[spr, b.vv()], inc="V")
                    b.emit("V", lambda e: e.tensor_single_scalar(
                        mk1[:, :], pr_sb[:, :], rr1[:, 0:1], ALU.is_ge),
                        waits=[b.vv()], inc="V")
                    b.emit("V", lambda e: e.scalar_tensor_tensor(
                        p2_sb[:, :], mk1[:, :], -4.0, pr_sb[:, :], ALU.mult, ALU.add),
                        waits=[b.vv()], inc="V")
                    b.emit("V", lambda e: e.reduce_max(rr2[:, 0:1], p2_sb[:, :], axis=AX.X),
                           waits=[b.vv()], inc="V")
                    b.emit("V", lambda e: e.tensor_single_scalar(
                        mk2[:, :], p2_sb[:, :], rr2[:, 0:1], ALU.is_ge),
                        waits=[b.vv()], inc="V")
                    b.emit("V", lambda e: e.tensor_add(rr3[:, 0:1], rr1[:, 0:1], rr2[:, 0:1]),
                           waits=[b.vv()], inc="V")
                    b.emit("V", lambda e: e.tensor_single_scalar(
                        rr3[:, 0:1], rr3[:, 0:1], 2.0, ALU.mult),
                        waits=[b.vv()], inc="V")
                    b.emit("V", lambda e: e.reciprocal(rr3[:, 0:1], rr3[:, 0:1]),
                           waits=[b.vv()], inc="V")
                    b.emit("V", lambda e, slot=slot: e.tensor_add(
                        msel[:, slot, :], mk1[:, :], mk2[:, :]),
                        waits=[b.vv()], inc="V")
                    b.emit("V", lambda e, slot=slot: e.tensor_mul(
                        mk1[:, :], msel[:, slot, :], pr_sb[:, :]),
                        waits=[b.vv()], inc="V")
                    b.emit("V", lambda e, slot=slot: e.tensor_single_scalar(
                        w_sb[:, slot, :], mk1[:, :], rr3[:, 0:1], ALU.mult),
                        waits=[b.vv()], inc="V")
            dX2W_all = b.c("dX2W")
            dX2X_all = b.c("dX2X")
            dHW_all = b.c("dHW")
            router_done = b.c("V")

        sKV.close()
        b.barrier()

        with ExitStack() as sD:
            ED = sD.enter_context
            hTok = ED(nc.sbuf_tensor("hTok", [128, OT, D], BF16))
            acc = ED(nc.sbuf_tensor("acc", [128, OT, D], BF16))
            gbuf = [ED(nc.sbuf_tensor(f"gbuf{i}", [128, DT, 512], BF16)) for i in range(2)]
            ubuf = [ED(nc.sbuf_tensor(f"ubuf{i}", [128, DT, 512], BF16)) for i in range(2)]
            dbuf = [ED(nc.sbuf_tensor(f"dbuf{i}", [128, 4, D], BF16)) for i in range(2)]
            gT = ED(nc.sbuf_tensor("gT", [128, OT, CAP], BF16))
            gS = ED(nc.sbuf_tensor("gS", [128, CT, 2048], BF16))
            hgT = ED(nc.sbuf_tensor("hgT", [128, DT, CAP], BF16))
            act_sb = ED(nc.sbuf_tensor("act_sb", [128, 4, CAP], BF16))
            dwacc = ED(nc.sbuf_tensor("dwacc", [128, CT, D], BF16))
            t_sb = ED(nc.sbuf_tensor("t_sb", [128, CAP], BF16))
            gp_sb = ED(nc.sbuf_tensor("gp_sb", [128, CAP], BF16))
            x2t = [ED(nc.sbuf_tensor(f"x2t{i}", [128, D], F32)) for i in range(2)]
            iota_sb = ED(nc.sbuf_tensor("iota_sb", [128, CAP], F32))
            trils_sb = ED(nc.sbuf_tensor("trils_sb", [128, 128], BF16))
            onesf_sb = ED(nc.sbuf_tensor("onesf_sb", [128, 128], BF16))
            rank_sb = ED(nc.sbuf_tensor("rank_sb", [128, OT, 8], F32))
            pA = [ED(nc.psum_tensor(f"pA{i}", [128, 512], F32)) for i in range(4)]
            pB = [ED(nc.psum_tensor(f"pB{i}", [128, 512], F32)) for i in range(3)]

            for nm, dd in ((iota_sb, iota_d),
                           (trils_sb, trils_d), (onesf_sb, onesf_d)):
                b.emit("Y", lambda e, nm=nm, dd=dd: e.dma_start(nm[:], dd[:]), inc="dC", k=16)
            b.emit("Y", lambda e: e.dma_start(
                hTok[:, :, :], hTok_s[:, :].rearrange("(t p) d -> p t d", p=128)),
                waits=[("dHW", dHW_all)], inc="dHR", k=16)
            dC4 = b.c("dC")
            hTw = ("dHR", b.c("dHR"))
            sG4 = b.c("G")

            rk_done = []
            for i in range(OT):
                bank = pA[i % 2]
                wl = [("V", router_done), ("dC", dC4)]
                if i >= 2:
                    wl.append(rk_done[i - 2])
                for jj in range(i + 1):
                    b.emit("T", lambda e, i=i, jj=jj, bank=bank: e.matmul(
                        bank[:, 0:8], trils_sb[:, :] if jj == i else onesf_sb[:, :],
                        msel[:, jj, :], start=(jj == 0), stop=(jj == i)),
                        waits=(wl if jj == 0 else ()), inc=("T" if jj == i else None))
                mmw = b.tt()
                b.emit("V", lambda e, i=i, bank=bank: e.scalar_tensor_tensor(
                    rank_sb[:, i, :], msel[:, i, :], 4096.0, bank[:, 0:8],
                    ALU.mult, ALU.add),
                    waits=[mmw, b.vv()], inc="V")
                b.emit("V", lambda e, i=i: e.tensor_single_scalar(
                    rank_sb[:, i, :], rank_sb[:, i, :], -4096.0, ALU.add),
                    waits=[b.vv()], inc="V")
                rk_done.append(b.vv())
            rank_ready = b.vv()


            wsem = ["dMA", "dMB"]
            wload = {}
            wfree = {0: None, 1: None}

            def load_chunk(stp):
                ei, ci = stp // 6, stp % 6
                f0, nf = FCH[ci]
                buf = stp % 2
                sname = wsem[buf]
                wl = [x for x in [wfree[buf]] if x]
                if ci < 5:
                    gsrc = lambda e, ei=ei, ci=ci: gate6_d[ei, ci, :, :, :]
                    usrc = lambda e, ei=ei, ci=ci: up6_d[ei, ci, :, :, :]
                    dsrc = lambda e, ei=ei, ci=ci: down6_d[ei, ci, :, :, :]
                else:
                    gsrc = lambda e, ei=ei: gate4_d[ei, :, :, :]
                    usrc = lambda e, ei=ei: up4_d[ei, :, :, :]
                    dsrc = lambda e, ei=ei: down4_d[ei, :, :, :]
                b.emit("G", lambda e, nf=nf, buf=buf, gsrc=gsrc: e.dma_start(
                    gbuf[buf][:, :, 0:nf * 128], gsrc(e)), waits=wl, inc=sname, k=16)
                b.emit("G", lambda e, nf=nf, buf=buf, usrc=usrc: e.dma_start(
                    ubuf[buf][:, :, 0:nf * 128], usrc(e)), inc=sname, k=16)
                wload[stp] = (sname, b.c(sname))

            dfree = {0: None, 1: None}

            def load_dchunk(stp):
                ei, ci = stp // 6, stp % 6
                f0, nf = FCH[ci]
                dbf = stp % 2
                if ci < 5:
                    dsrc = lambda e, ei=ei, ci=ci: down6_d[ei, ci, :, :, :]
                else:
                    dsrc = lambda e, ei=ei: down4_d[ei, :, :, :]
                dsem = "dMD" if dbf == 0 else "dME"
                wl = [x for x in [dfree[dbf]] if x]
                b.emit("G", lambda e, nf=nf, dbf=dbf, dsrc=dsrc: e.dma_start(
                    dbuf[dbf][:, 0:nf, :], dsrc(e)), waits=wl, inc=dsem, k=16)
                wload[("d", stp)] = (dsem, b.c(dsem))

            load_chunk(0)
            load_chunk(1)
            load_dchunk(0)
            load_dchunk(1)

            gather_T = None
            hg_S = {}
            gu_war = {}
            dw_war = {}
            act_T = None
            t_free = None
            rflush = {}
            s_ffn = None
            xld = {}

            gT_pipe = {}

            def build_gT(ei):
                wl = [rank_ready, ("dC", dC4)]
                if gather_T is not None:
                    wl.append(gather_T)
                for ti in range(OT):
                    b.emit("V", lambda e, ti=ti, ei=ei: e.tensor_single_scalar(
                        gT[:, ti, :], iota_sb[:, :], rank_sb[:, ti, ei:ei + 1],
                        ALU.is_equal),
                        waits=wl + [b.vv()], inc="V")
                gT_pipe[ei] = b.vv()

            build_gT(0)
            for ei in range(8):
                gT_ready = gT_pipe[ei]

                with nc.psum_tensor(f"psT4_{ei}", [128, 1024], BF16) as psT4:
                    psT4_free = None
                    gidx = 0
                    for ct in range(CT):
                        for tp in range(2):
                            if gidx == 0:
                                if ei == 0:
                                    wl2 = [gT_ready, b.vv(), b.ss(), ("dC", dC4)]
                                else:
                                    wl2 = [gT_ready, rflush[6], s_ffn, ("dC", dC4)]
                            else:
                                wl2 = [gT_ready, psT4_free]
                            for q8 in range(8):
                                ti = tp * 8 + q8
                                b.emit("T", lambda e, ti=ti, ct=ct, q8=q8, psT4=psT4: e.transpose(
                                    psT4[:, q8 * 128:(q8 + 1) * 128],
                                    gT[:, ti, ct * 128:(ct + 1) * 128], idb_sb[:]),
                                    waits=(wl2 if q8 == 0 else ()),
                                    inc=("T" if q8 == 7 else None))
                            trw = b.tt()
                            b.emit("V", lambda e, ct=ct, tp=tp, psT4=psT4: e.tensor_copy(
                                gS[:, ct, tp * 1024:(tp + 1) * 1024], psT4[:, :]),
                                waits=[trw, b.vv()], inc="V")
                            psT4_free = b.vv()
                            gidx += 1
                gs_wl = [psT4_free, psT4_free]

                with nc.psum_tensor(f"pg_{ei}", [128, 512], F32) as pg256:
                    for dh2 in range(2):
                        for q in range(4):
                            dt = 4 * dh2 + q
                            pbq = pB[q] if q < 3 else pg256
                            if dh2 == 0:
                                if ei == 0:
                                    wl = [hTw, b.vv(), b.ss()]
                                else:
                                    wl = [hTw, rflush[7], s_ffn]
                                if q == 3:
                                    wl = wl + gs_wl
                            else:
                                wl = [hTw, ("S", hg_S[(0, q)])]
                            for ti in range(OT):
                                b.emit("T", lambda e, ti=ti, dt=dt, q=q: e.matmul(
                                    pA[q][:, :], hTok[:, ti, dt * 128:(dt + 1) * 128],
                                    gT[:, ti, 0:512], start=(ti == 0), stop=(ti == OT - 1)),
                                    waits=(wl if ti == 0 else ()), inc=("T" if ti == OT - 1 else None))
                                b.emit("T", lambda e, ti=ti, dt=dt, pbq=pbq: e.matmul(
                                    pbq[:, 0:256], hTok[:, ti, dt * 128:(dt + 1) * 128],
                                    gT[:, ti, 512:768], start=(ti == 0), stop=(ti == OT - 1)),
                                    inc=("T" if ti == OT - 1 else None))
                            mmw = b.tt()
                            b.emit("S", lambda e, dt=dt, q=q: e.activation(
                                hgT[:, dt, 0:512], pA[q][:, :], AF.Identity),
                                waits=[mmw, b.ss()], inc="S")
                            b.emit("S", lambda e, dt=dt, pbq=pbq: e.activation(
                                hgT[:, dt, 512:768], pbq[:, 0:256], AF.Identity),
                                waits=[b.ss()], inc="S")
                            hg_S[(dh2, q)] = b.c("S")
                gather_T = b.tt()
                hg_ready = [("S", b.c("S"))]
                if ei + 1 < 8:
                    build_gT(ei + 1)

                for ci in range(6):
                    stp = ei * 6 + ci
                    f0, nf = FCH[ci]
                    buf = stp % 2
                    lw = wload[stp]
                    act_w = {}
                    tw_bank = {}
                    for fl in range(nf):
                        gb = fl % 2
                        wl = [lw] + hg_ready
                        if gu_war.get(("g", gb)) is not None:
                            wl.extend(gu_war[("g", gb)])
                        for half, (pp, w0, w1) in enumerate(
                                ((pA, 0, 512), (pB, 512, 768))):
                            for dt in range(DT):
                                b.emit("T", lambda e, gb=gb, buf=buf, fl=fl, dt=dt, pp=pp, w0=w0, w1=w1: e.matmul(
                                    pp[gb][:, 0:w1 - w0], gbuf[buf][:, dt, fl * 128:(fl + 1) * 128],
                                    hgT[:, dt, w0:w1], start=(dt == 0), stop=(dt == DT - 1)),
                                    waits=(wl if (dt == 0 and half == 0) else ()),
                                    inc=("T" if dt == DT - 1 else None))
                        gmm = b.tt()
                        wl = [lw]
                        if gu_war.get(("u", gb)) is not None:
                            wl.extend(gu_war[("u", gb)])
                        if gu_war.get(("ub",)) is not None:
                            wl.extend(gu_war[("ub",)])
                        for half, (upp, w0, w1) in enumerate(
                                ((pA[2 + gb], 0, 512), (pB[2], 512, 768))):
                            for dt in range(DT):
                                b.emit("T", lambda e, buf=buf, fl=fl, dt=dt, upp=upp, w0=w0, w1=w1: e.matmul(
                                    upp[:, 0:w1 - w0], ubuf[buf][:, dt, fl * 128:(fl + 1) * 128],
                                    hgT[:, dt, w0:w1], start=(dt == 0), stop=(dt == DT - 1)),
                                    waits=(wl if (dt == 0 and half == 0) else ()),
                                    inc=("T" if dt == DT - 1 else None))
                        umm = b.tt()
                        wl = [gmm, b.ss()]
                        if t_free is not None:
                            wl.append(t_free)
                        b.emit("S", lambda e, gb=gb: e.activation(
                            t_sb[:, 0:512], pA[gb][:, :], AF.Tanh, scale=0.5),
                            waits=wl, inc="S")
                        b.emit("S", lambda e, gb=gb: e.activation(
                            t_sb[:, 512:768], pB[gb][:, 0:256], AF.Tanh, scale=0.5),
                            waits=[b.ss()], inc="S")
                        tw = b.ss()
                        b.emit("V", lambda e, gb=gb: e.scalar_tensor_tensor(
                            gp_sb[:, 0:512], t_sb[:, 0:512], 1.0, pA[gb][:, :],
                            ALU.add, ALU.mult),
                            waits=[gmm, tw, b.vv()], inc="V")
                        b.emit("V", lambda e, gb=gb: e.scalar_tensor_tensor(
                            gp_sb[:, 512:768], t_sb[:, 512:768], 1.0, pB[gb][:, 0:256],
                            ALU.add, ALU.mult),
                            waits=[b.vv()], inc="V")
                        gu_war[("g", gb)] = [b.vv(), tw]
                        tw_bank[gb] = tw
                        t_free = b.vv()
                        b.emit("V", lambda e, gb=gb, fl=fl: e.tensor_mul(
                            act_sb[:, fl, 0:512], gp_sb[:, 0:512], pA[2 + gb][:, :]),
                            waits=[umm, b.vv()] + ([act_T] if (fl == 0 and act_T) else []),
                            inc="V")
                        b.emit("V", lambda e, fl=fl: e.tensor_mul(
                            act_sb[:, fl, 512:768], gp_sb[:, 512:768], pB[2][:, 0:256]),
                            waits=[b.vv()], inc="V")
                        gu_war[("u", gb)] = [b.vv()]
                        gu_war[("ub",)] = [b.vv()]
                        act_w[fl] = b.vv()
                    act_ready = act_w[nf - 1]
                    for sp in range(3):
                        for q2 in range(2):
                            stile = sp * 2 + q2
                            for dh in range(2):
                                bank = q2 * 2 + dh
                                wl = [act_ready, lw]
                                if dw_war.get(bank) is not None:
                                    wl.append(dw_war[bank])
                                wl = wl + [wload[("d", stp)]]
                                for fl in range(nf):
                                    b.emit("T", lambda e, bank=bank, fl=fl, stile=stile, dh=dh, nf=nf, buf=buf: e.matmul(
                                        pA[bank][:, :],
                                        act_sb[:, fl, stile * 128:(stile + 1) * 128],
                                        dbuf[buf][:, fl, dh * 512:(dh + 1) * 512],
                                        start=(fl == 0), stop=(fl == nf - 1)),
                                        waits=(wl if fl == 0 else ()),
                                        inc=("T" if fl == nf - 1 else None))
                                dmm = b.tt()
                                if ci == 0:
                                    b.emit("V", lambda e, bank=bank, stile=stile, dh=dh: e.tensor_copy(
                                        dwacc[:, stile, dh * 512:(dh + 1) * 512], pA[bank][:, :]),
                                        waits=[dmm, b.vv()], inc="V")
                                else:
                                    b.emit("V", lambda e, bank=bank, stile=stile, dh=dh: e.tensor_add(
                                        dwacc[:, stile, dh * 512:(dh + 1) * 512], pA[bank][:, :],
                                        dwacc[:, stile, dh * 512:(dh + 1) * 512]),
                                        waits=[dmm, b.vv()], inc="V")
                                dw_war[bank] = b.vv()
                    act_T = b.tt()
                    for _gb in (0, 1):
                        gu_war[("g", _gb)] = [dw_war[_gb], tw_bank[_gb]]
                        gu_war[("u", _gb)] = [dw_war[2 + _gb]]
                    gu_war[("ub",)] = [act_w[nf - 1]]
                    wfree[buf] = b.tt()
                    dfree[stp % 2] = b.tt()
                    if stp + 2 < 48:
                        load_chunk(stp + 2)
                        load_dchunk(stp + 2)
                dw_ready = b.vv()
                s_ffn = b.ss()

                with nc.psum_tensor(f"psc_{ei}", [128, 512], F32) as psc:
                    setA = [pA[0], pA[1], pA[2], pA[3]]
                    setB = [pB[0], pB[1], pB[2], psc]
                    stfree = {0: None, 1: None}
                    for rnd in range(8):
                        banks = setB if rnd % 2 == 0 else setA
                        if ei == 7:
                            for k2 in range(2):
                                ti = rnd * 2 + k2
                                wlx = [("dX2W", dX2W_all), ("dX2X", dX2X_all)]
                                if stfree[k2] is not None:
                                    wlx.append(stfree[k2])
                                xsem = "dX2R" if k2 == 0 else "dX2S"
                                b.emit("Y", lambda e, ti=ti, k2=k2: e.dma_start(
                                    x2t[k2][:, :], x2_s[ti * 128:(ti + 1) * 128, :]),
                                    waits=wlx, inc=xsem, k=16)
                                xld[ti] = (xsem, b.c(xsem))
                        if rnd < 2:
                            wl = gs_wl + [dw_ready, s_ffn]
                        else:
                            wl = [rflush[rnd - 2]]
                        for ct in range(CT):
                            for tg in range(4):
                                ti = rnd * 2 + tg // 2
                                dh = tg % 2
                                pp = banks[tg]
                                b.emit("T", lambda e, ct=ct, ti=ti, dh=dh, pp=pp: e.matmul(
                                    pp[:, :], gS[:, ct, ti * 128:(ti + 1) * 128],
                                    dwacc[:, ct, dh * 512:(dh + 1) * 512],
                                    start=(ct == 0), stop=(ct == CT - 1)),
                                    waits=(wl if (ct == 0 and tg == 0) else ()),
                                    inc=("T" if (ct == CT - 1 and tg == 3) else None))
                        smm = b.tt()
                        for tg in range(4):
                            ti = rnd * 2 + tg // 2
                            dh = tg % 2
                            pp = banks[tg]
                            if ei == 0:
                                b.emit("V", lambda e, ti=ti, dh=dh, pp=pp, ei=ei: e.tensor_single_scalar(
                                    acc[:, ti, dh * 512:(dh + 1) * 512], pp[:, :],
                                    w_sb[:, ti, ei:ei + 1], ALU.mult),
                                    waits=[smm, b.vv()], inc="V")
                            else:
                                b.emit("V", lambda e, ti=ti, dh=dh, pp=pp, ei=ei: e.scalar_tensor_tensor(
                                    acc[:, ti, dh * 512:(dh + 1) * 512], pp[:, :],
                                    w_sb[:, ti, ei:ei + 1],
                                    acc[:, ti, dh * 512:(dh + 1) * 512],
                                    ALU.mult, ALU.add),
                                    waits=[smm, b.vv()], inc="V")
                        rflush[rnd] = b.vv()
                        if ei == 7:
                            for k2 in range(2):
                                ti = rnd * 2 + k2
                                b.emit("V", lambda e, ti=ti, k2=k2: e.tensor_add(
                                    x2t[k2][:, :], acc[:, ti, :], x2t[k2][:, :]),
                                    waits=[xld[ti], b.vv()], inc="V")
                                ow = b.vv()
                                b.emit("Y", lambda e, ti=ti, k2=k2: e.dma_start(
                                    out_d[ti * 128:(ti + 1) * 128, :], x2t[k2][:, :]),
                                    waits=[ow], inc=("dOut" if k2 == 0 else "dOu2"), k=16)
                                stfree[k2] = (("dOut" if k2 == 0 else "dOu2"),
                                              b.c("dOut" if k2 == 0 else "dOu2"))
                for q in range(4):
                    hg_S[(0, q)] = None
                    hg_S[(1, q)] = None
                gu_war = {}
                dw_war = {}
                hg_ready = []
                act_T = None
                hTw = ("dHR", b.c("dHR"))
            b.wait_only("Y", [("dOut", b.c("dOut")), ("dOu2", b.c("dOu2"))])

        blk = E(nc.Block())
        opsd = b.ops
        blk.tensor(lambda e: [r(e) for r in opsd["T"]] and None)
        blk.vector(lambda e: [r(e) for r in opsd["V"]] and None)
        blk.scalar(lambda e: [r(e) for r in opsd["S"]] and None)
        blk.gpsimd(lambda e: [r(e) for r in opsd["G"]] and None)
        blk.sync(lambda e: [r(e) for r in opsd["Y"]] and None)
    return nc


_NC_CACHE = {}


def _enable_ldw_opt():
    try:
        from concourse.compiler_utils import get_compiler_flags, set_compiler_flags
        flags = get_compiler_flags()
        nf = [f.replace('--enable-ldw-opt=false', '--enable-ldw-opt=true') for f in flags]
        if nf != flags:
            set_compiler_flags(nf)
    except Exception:
        pass


def _host_consts(p):
    hd = 64
    j = np.arange(32)
    invf = (1.0 / THETA ** (2 * j / hd)).astype(np.float64)
    t = np.arange(SQ, dtype=np.float64)
    ang = t[None, :] * invf[:, None]
    cos = np.cos(ang)
    sin = np.sin(ang)
    pp = np.arange(128)
    jj = pp & 31
    lower = (pp & 63) < 32
    cosK = cos[jj, :]
    sinK = np.where(lower[:, None], -sin[jj, :], sin[jj, :])
    own_tok = ((2 * np.arange(OT) + p)[:, None] * 128 + np.arange(128)[None, :]).reshape(-1)
    cosQ = cosK[:, own_tok]
    sinQ = sinK[:, own_tok]

    masks = np.zeros((128, 8, 512), dtype=np.float32)
    kk = np.arange(128)
    tri = (kk[:, None] <= kk[None, :]).astype(np.float32)
    for r in range(8):
        for i in range(4):
            lim = 2 * i + p
            if r < lim:
                masks[:, r, i * 128:(i + 1) * 128] = 1.0
            elif r == lim:
                masks[:, r, i * 128:(i + 1) * 128] = tri

    permM = np.zeros((128, 128), dtype=np.float32)
    for m in range(128):
        src = m + 32 if (m & 63) < 32 else m - 32
        permM[src, m] = 1.0
    sel0 = np.zeros((128, 128), dtype=np.float32)
    sel0[0, :] = 1.0
    return (np.ascontiguousarray(cosK, dtype=np.float32), np.ascontiguousarray(sinK, dtype=np.float32),
            np.ascontiguousarray(cosQ, dtype=np.float32), np.ascontiguousarray(sinQ, dtype=np.float32),
            np.ascontiguousarray(masks, dtype=BFD), np.ascontiguousarray(permM, dtype=BFD),
            np.eye(128, dtype=BFD), np.ascontiguousarray(sel0, dtype=BFD))


def kernel(L0_attn_x, L0_attn_w_qkv, L0_attn_b_qkv, L0_attn_w_out, L0_attn_b_out,
           L0_ffn_router_w, L0_ffn_gate_proj, L0_ffn_up_proj, L0_ffn_down_proj):
    x = np.ascontiguousarray(np.asarray(L0_attn_x), dtype=np.float32)
    wqkv = np.asarray(L0_attn_w_qkv, dtype=np.float32)
    bqkv = np.asarray(L0_attn_b_qkv, dtype=np.float32)
    wout = np.asarray(L0_attn_w_out, dtype=np.float32)
    bout = np.asarray(L0_attn_b_out, dtype=np.float32)
    rw = np.ascontiguousarray(np.asarray(L0_ffn_router_w), dtype=np.float32)
    gate = np.ascontiguousarray(np.asarray(L0_ffn_gate_proj), dtype=np.float32)
    up = np.ascontiguousarray(np.asarray(L0_ffn_up_proj), dtype=np.float32)
    down = np.ascontiguousarray(np.asarray(L0_ffn_down_proj), dtype=np.float32)

    perm = np.concatenate([64 * h + np.r_[0:64:2, 1:64:2] for h in range(NHEAD)])

    def wprep(w):
        return np.ascontiguousarray(w.T.reshape(DT, 128, D).transpose(1, 0, 2))

    wqT = wprep(wqkv[0:D][perm])
    wkT = wprep(wqkv[D:2 * D][perm])
    wvT = wprep(wqkv[2 * D:3 * D])
    woT = wprep(wout)

    def fprep6(w):
        out = np.empty((8, 5, 128, DT, 512), np.float32)
        for ci in range(5):
            blk = w[:, :, ci * 512:(ci + 1) * 512]
            out[:, ci] = blk.reshape(8, DT, 128, 512).transpose(0, 2, 1, 3)
        return np.ascontiguousarray(out)

    def fprep4(w):
        blk = w[:, :, 2560:2816]
        return np.ascontiguousarray(blk.reshape(8, DT, 128, 256).transpose(0, 2, 1, 3))

    def dprep6(w):
        out = np.empty((8, 5, 128, 4, D), np.float32)
        for ci in range(5):
            blk = w[:, ci * 512:(ci + 1) * 512, :]
            out[:, ci] = blk.reshape(8, 4, 128, D).transpose(0, 2, 1, 3)
        return np.ascontiguousarray(out)

    def dprep4(w):
        blk = w[:, 2560:2816, :]
        return np.ascontiguousarray(blk.reshape(8, 2, 128, D).transpose(0, 2, 1, 3))

    iota768 = np.ascontiguousarray(np.tile(np.arange(768, dtype=np.float32), (128, 1)))
    _k = np.arange(128)
    trils = (_k[:, None] < _k[None, :]).astype(BFD)
    onesf = np.ones((128, 128), BFD)
    gate6, gate4 = fprep6(gate), fprep4(gate)
    up6, up4 = fprep6(up), fprep4(up)
    down6, down4 = dprep6(down), dprep4(down)
    rwR = np.ascontiguousarray(rw.reshape(DT, 128, 8).transpose(1, 0, 2))
    bq128 = np.ascontiguousarray(bqkv[0:D][perm].reshape(8, 128).T)
    bk128 = np.ascontiguousarray(bqkv[D:2 * D][perm].reshape(8, 128).T)
    bvbc = np.ascontiguousarray(np.broadcast_to(bqkv[2 * D:3 * D], (128, D)))
    bobc = np.ascontiguousarray(np.broadcast_to(bout, (128, D)))

    _enable_ldw_opt()
    if "nc" not in _NC_CACHE:
        _NC_CACHE["nc"] = build_program()
    nc = _NC_CACHE["nc"]

    in_maps = []
    for core in range(8):
        bb, p = core // 2, core % 2
        cosK, sinK, cosQ, sinQ, masks, permM, id128, sel0 = _host_consts(p)
        xb = np.ascontiguousarray(x[bb])
        own = 2 * np.arange(OT) + p
        xown = np.ascontiguousarray(xb.reshape(NT, 128, D)[own].reshape(2048, D))
        in_maps.append(dict(
            x=xb, xown=xown, wqT=wqT, wkT=wkT, wvT=wvT, woT=woT,
            bq128=bq128, bk128=bk128, bvbc=bvbc, bobc=bobc, rw=rwR,
            gate6=gate6, gate4=gate4, up6=up6, up4=up4, down6=down6, down4=down4,
            cosK=cosK, sinK=sinK, cosQ=cosQ, sinQ=sinQ, masks=masks,
            perm128=permM, id128=id128, sel0=sel0,
            iota768=iota768, trils=trils, onesf=onesf))

    res = run_bass_kernel_spmd(nc, in_maps, core_ids=list(range(8)))
    full = np.empty((4, SQ, D), dtype=np.float32)
    for core in range(8):
        bb, p = core // 2, core % 2
        o = np.asarray(res.results[core]["out"]).reshape(OT, 128, D)
        full[bb].reshape(NT, 128, D)[2 * np.arange(OT) + p] = o
    return full


# version 22
# speedup vs baseline: 1.0157x; 1.0157x over previous
import numpy as np
import ml_dtypes
from contextlib import ExitStack

import concourse.bass as bass
import concourse.mybir as mybir
from concourse.bass_utils import run_bass_kernel_spmd

F32 = mybir.dt.float32
BF16 = mybir.dt.bfloat16
AF = mybir.ActivationFunctionType
ALU = mybir.AluOpType
AX = mybir.AxisListType
BFD = ml_dtypes.bfloat16

D = 1024
SQ = 4096
NHEAD = 16
DT = 8
NT = 32
OT = 16
NCH = 8
NJ = 4
FCH = [(0, 4), (4, 4), (8, 4), (12, 4), (16, 4), (20, 2)]
CAP = 768
CT = 6
THETA = 10000.0

SEMS = ("T", "V", "S", "G", "dC", "dT0", "dT1", "dXa", "dXb", "dXc", "dXd", "dXQa", "dXQb", "dQW", "dQR",
        "dXR", "dX2W", "dX2X", "dHW", "dWgt", "dMA", "dMB", "dMD", "dME", "dHR", "dX2R", "dX2S", "dOut", "dOu2")


class Builder:
    def __init__(self, nc, st):
        self.nc = nc
        self.ops = {k: [] for k in "TVSGY"}
        self.cnt = {}
        self.sem = {}
        self.lastwait = {k: {} for k in "TVSGY"}
        for n in SEMS:
            self.sem[n] = st.enter_context(nc.semaphore(n))
            self.cnt[n] = 0

    def emit(self, eng, fn, waits=(), inc=None, k=1):
        lw = self.lastwait[eng]
        ws = []
        dedup = eng not in "YG"
        for s, v in waits:
            if v > 0 and (not dedup or lw.get(s, 0) < v):
                ws.append((s, v))
                if lw.get(s, 0) < v:
                    lw[s] = v
        if inc is not None:
            self.cnt[inc] += k
        sems = self.sem

        def run(e, ws=tuple(ws), fn=fn, inc=inc, k=k):
            for s, v in ws:
                e.wait_ge(sems[s], v)
            ins = fn(e)
            if inc is not None and ins is not None:
                ins.then_inc(sems[inc], k)
        self.ops[eng].append(run)

    def wait_only(self, eng, waits):
        self.emit(eng, lambda e: None, waits=waits)

    def barrier(self):
        snap = [(n, self.cnt[n]) for n in SEMS if self.cnt[n] > 0]
        for eng in "TVSGY":
            self.wait_only(eng, [(n, v) for n, v in snap if n != eng])

    def c(self, n):
        return self.cnt[n]

    def vv(self):
        return ("V", self.cnt["V"])

    def ss(self):
        return ("S", self.cnt["S"])

    def tt(self):
        return ("T", self.cnt["T"])


def build_program():
    nc = bass.Bass("TRN2", target_bir_lowering=False)

    x_d = nc.dram_tensor("x", [SQ, D], F32, kind="ExternalInput")
    xown_d = nc.dram_tensor("xown", [2048, D], F32, kind="ExternalInput")
    wqT_d = nc.dram_tensor("wqT", [128, DT, D], F32, kind="ExternalInput")
    wkT_d = nc.dram_tensor("wkT", [128, DT, D], F32, kind="ExternalInput")
    wvT_d = nc.dram_tensor("wvT", [128, DT, D], F32, kind="ExternalInput")
    woT_d = nc.dram_tensor("woT", [128, DT, D], F32, kind="ExternalInput")
    bq_d = nc.dram_tensor("bq128", [128, 8], F32, kind="ExternalInput")
    bk_d = nc.dram_tensor("bk128", [128, 8], F32, kind="ExternalInput")
    bv_d = nc.dram_tensor("bvbc", [128, D], F32, kind="ExternalInput")
    bo_d = nc.dram_tensor("bobc", [128, D], F32, kind="ExternalInput")
    rw_d = nc.dram_tensor("rw", [128, DT, 8], F32, kind="ExternalInput")
    gate6_d = nc.dram_tensor("gate6", [8, 5, 128, DT, 512], F32, kind="ExternalInput")
    gate4_d = nc.dram_tensor("gate4", [8, 128, DT, 256], F32, kind="ExternalInput")
    up6_d = nc.dram_tensor("up6", [8, 5, 128, DT, 512], F32, kind="ExternalInput")
    up4_d = nc.dram_tensor("up4", [8, 128, DT, 256], F32, kind="ExternalInput")
    down6_d = nc.dram_tensor("down6", [8, 5, 128, 4, D], F32, kind="ExternalInput")
    down4_d = nc.dram_tensor("down4", [8, 128, 2, D], F32, kind="ExternalInput")
    iota_d = nc.dram_tensor("iota768", [128, 768], F32, kind="ExternalInput")
    trils_d = nc.dram_tensor("trils", [128, 128], BF16, kind="ExternalInput")
    onesf_d = nc.dram_tensor("onesf", [128, 128], BF16, kind="ExternalInput")
    cosK_d = nc.dram_tensor("cosK", [128, SQ], F32, kind="ExternalInput")
    sinK_d = nc.dram_tensor("sinK", [128, SQ], F32, kind="ExternalInput")
    cosQ_d = nc.dram_tensor("cosQ", [128, 2048], F32, kind="ExternalInput")
    sinQ_d = nc.dram_tensor("sinQ", [128, 2048], F32, kind="ExternalInput")
    masks_d = nc.dram_tensor("masks", [128, 8, 512], BF16, kind="ExternalInput")
    perm_d = nc.dram_tensor("perm128", [128, 128], BF16, kind="ExternalInput")
    idb_d = nc.dram_tensor("id128", [128, 128], BF16, kind="ExternalInput")
    sel0_d = nc.dram_tensor("sel0", [128, 128], BF16, kind="ExternalInput")
    out_d = nc.dram_tensor("out", [2048, D], F32, kind="ExternalOutput")

    qT_s = nc.dram_tensor("qTs", [DT, 128, 2048], BF16, kind="Internal")
    hTok_s = nc.dram_tensor("hToks", [2048, D], BF16, kind="Internal")
    x2_s = nc.dram_tensor("x2s", [2048, D], F32, kind="Internal")

    st = ExitStack()
    with st:
        b = Builder(nc, st)
        E = st.enter_context

        perm_sb = E(nc.sbuf_tensor("perm_sb", [128, 128], BF16))
        idb_sb = E(nc.sbuf_tensor("idb_sb", [128, 128], BF16))
        bq_sb = E(nc.sbuf_tensor("bq_sb", [128, 8], F32))
        bk_sb = E(nc.sbuf_tensor("bk_sb", [128, 8], F32))
        w_sb = E(nc.sbuf_tensor("w_sb", [128, OT, 8], F32))
        msel = E(nc.sbuf_tensor("msel", [128, OT, 8], BF16))
        st1 = E(nc.sbuf_tensor("st1", [128, 8], F32))
        st2 = E(nc.sbuf_tensor("st2", [128, 8], F32))
        nwv = E(nc.sbuf_tensor("nwv", [128, 8], F32))
        nwy = E(nc.sbuf_tensor("nwy", [128, 8], F32))
        nwu = E(nc.sbuf_tensor("nwu", [128, 8], F32))
        nmu = E(nc.sbuf_tensor("nmu", [128, 8], F32))

        sKV = ExitStack()
        kT_sb = sKV.enter_context(nc.sbuf_tensor("kT_sb", [128, DT, SQ], BF16))
        v_sb = sKV.enter_context(nc.sbuf_tensor("v_sb", [128, NT, NHEAD * 65], BF16))

        b.emit("Y", lambda e: e.dma_start(perm_sb[:], perm_d[:, :]), inc="dC", k=16)
        b.emit("Y", lambda e: e.dma_start(idb_sb[:], idb_d[:, :]), inc="dC", k=16)
        b.emit("Y", lambda e: e.dma_start(bq_sb[:], bq_d[:, :]), inc="dC", k=16)
        b.emit("Y", lambda e: e.dma_start(bk_sb[:], bk_d[:, :]), inc="dC", k=16)
        b.emit("G", lambda e: e.memset(v_sb[:, :, 64:NHEAD * 65:65], 1.0), inc="G")
        b.emit("S", lambda e: e.preload_act_table(AF.Exp), inc="S")
        dC0 = b.c("dC")

        def ln_stats_newton(ncols, co=0, extra=()):
            sV0 = b.c("V")
            sS0 = b.c("S")
            def SL(t):
                return t[:, co:co + ncols]
            b.emit("V", lambda e: e.tensor_single_scalar(
                SL(nmu), SL(st1), 1.0 / D, ALU.mult),
                waits=[("V", sV0)] + list(extra), inc="V")
            b.emit("V", lambda e: e.tensor_mul(SL(nwu), SL(nmu), SL(nmu)),
                   waits=[b.vv()], inc="V")
            b.emit("V", lambda e: e.scalar_tensor_tensor(
                SL(nwv), SL(st2), 1.0 / D, SL(nwu),
                ALU.mult, ALU.subtract),
                waits=[("S", sS0), b.vv()], inc="V")
            b.emit("V", lambda e: e.tensor_single_scalar(
                SL(nwv), SL(nwv), 1e-5, ALU.add),
                waits=[b.vv()], inc="V")
            b.emit("V", lambda e: e.tensor_single_scalar(
                SL(nwu), SL(nwv), 1.0, ALU.add),
                waits=[b.vv()], inc="V")
            b.emit("V", lambda e: e.reciprocal(SL(nwu), SL(nwu)),
                   waits=[b.vv()], inc="V")
            b.emit("V", lambda e: e.tensor_single_scalar(
                SL(nwy), SL(nwu), 2.0, ALU.mult),
                waits=[b.vv()], inc="V")
            for _ in range(4):
                b.emit("V", lambda e: e.tensor_mul(SL(nwu), SL(nwy), SL(nwy)),
                       waits=[b.vv()], inc="V")
                b.emit("V", lambda e: e.tensor_mul(SL(nwu), SL(nwu), SL(nwv)),
                       waits=[b.vv()], inc="V")
                b.emit("V", lambda e: e.tensor_scalar(
                    SL(nwu), SL(nwu), -0.5, 1.5, ALU.mult, ALU.add),
                    waits=[b.vv()], inc="V")
                b.emit("V", lambda e: e.tensor_mul(SL(nwy), SL(nwy), SL(nwu)),
                       waits=[b.vv()], inc="V")
            return b.c("V")

        with ExitStack() as sA:
            EA = sA.enter_context
            wk_sb = EA(nc.sbuf_tensor("wk_sb", [128, DT, D], BF16))
            wv_sb = EA(nc.sbuf_tensor("wv_sb", [128, DT, D], BF16))
            cosS = EA(nc.sbuf_tensor("cosS", [128, 1, 512], F32))
            sinS = EA(nc.sbuf_tensor("sinS", [128, 1, 512], F32))
            ln1T = EA(nc.sbuf_tensor("ln1T", [128, DT, 512], BF16))
            xbuf = EA(nc.sbuf_tensor("xbuf", [128, 4, D], F32))
            lnb = EA(nc.sbuf_tensor("lnb", [128, D], BF16))
            tmp1 = EA(nc.sbuf_tensor("tmp1", [128, 512], BF16))
            tmp2 = EA(nc.sbuf_tensor("tmp2", [128, 512], BF16))
            bv_sb = EA(nc.sbuf_tensor("bv_sb", [128, D], BF16))
            psT = [EA(nc.psum_tensor(f"psT{i}", [128, 512], BF16)) for i in range(2)]
            kqv = [EA(nc.psum_tensor(f"kqv{i}", [128, 512], F32)) for i in range(2)]
            shf = EA(nc.psum_tensor("shf", [128, 512], F32))
            dump = EA(nc.psum_tensor("dump", [128, 1024], F32))

            b.emit("G", lambda e: e.dma_start(
                wk_sb[:], wkT_d[:, :, :]), inc="dWgt", k=16)
            b.emit("G", lambda e: e.dma_start(
                wv_sb[:], wvT_d[:, :, :]), inc="dWgt", k=16)
            b.emit("G", lambda e: e.dma_start(bv_sb[:], bv_d[:, :]), inc="dWgt", k=16)
            dC1 = b.c("dC")
            dW1 = b.c("dWgt")

            xfree = [None] * 4
            cfree = [None, None]
            stats_pipe = {}

            def ph1_load_stats(c):
                nonlocal dump_free
                co = 4 * (c % 2)
                for t in range(4):
                    tt = 4 * c + t
                    s8 = t
                    xsem = ("dXa", "dXb", "dXc", "dXd")[t]
                    lw = [x for x in [xfree[s8]] if x]
                    b.emit("Y", lambda e, tt=tt, s8=s8: e.dma_start(
                        xbuf[:, s8, :], x_d[tt * 128:(tt + 1) * 128, :]),
                        waits=lw, inc=xsem, k=16)
                    xw = (xsem, b.c(xsem))
                    b.emit("V", lambda e, s8=s8, co=co, t=t: e.reduce_sum(
                        st1[:, co + t:co + t + 1], xbuf[:, s8, :], axis=AX.X, negate=True),
                        waits=[xw], inc="V")
                    dw = [x for x in [dump_free] if x]
                    b.emit("S", lambda e, s8=s8, co=co, t=t: e.activation(
                        dump[:, :], xbuf[:, s8, :], AF.Square,
                        accum_out=st2[:, co + t:co + t + 1]),
                        waits=[xw] + dw, inc="S")
                    dump_free = b.ss()
                stats_pipe[c] = ln_stats_newton(4, co=co)
            lnb_free = None
            psT_free = [None, None]
            kqv_free = [None, None]
            shf_free = None
            ln1T_free = None
            dump_free = None

            ph1_load_stats(0)
            for c in range(NCH):
                cslot = 0
                tsem = "dT" + str(c % 2)
                cw = [x for x in [cfree[cslot]] if x]
                b.emit("Y", lambda e, c=c, cslot=cslot: e.dma_start(
                    cosS[:, cslot, :], cosK_d[:, c * 512:(c + 1) * 512]),
                    waits=cw, inc=tsem, k=16)
                b.emit("Y", lambda e, c=c, cslot=cslot: e.dma_start(
                    sinS[:, cslot, :], sinK_d[:, c * 512:(c + 1) * 512]),
                    inc=tsem, k=16)
                tab_w = (tsem, b.c(tsem))

                co = 4 * (c % 2)
                stats_done = stats_pipe[c]

                for t in range(4):
                    s8 = t
                    lw = [("V", stats_done)] + ([lnb_free] if lnb_free else [])
                    b.emit("V", lambda e, s8=s8, co=co, t=t: e.tensor_scalar(
                        lnb[:, :], xbuf[:, s8, :], nmu[:, co + t:co + t + 1],
                        nwy[:, co + t:co + t + 1], ALU.add, ALU.mult),
                        waits=lw, inc="V")
                    lnw = b.vv()
                    xfree[s8] = lnw
                    for half in range(2):
                        for q in range(4):
                            dt = 4 * half + q
                            wl = [lnw, ("dC", dC1)]
                            if q == 0 and psT_free[half]:
                                wl.append(psT_free[half])
                            b.emit("T", lambda e, half=half, q=q, dt=dt: e.transpose(
                                psT[half][:, q * 128:(q + 1) * 128],
                                lnb[:, dt * 128:(dt + 1) * 128], idb_sb[:]),
                                waits=wl, inc="T" if q == 3 else None)
                        trw = b.tt()
                        wl = [trw]
                        if t == 0 and half == 0 and ln1T_free:
                            wl.append(ln1T_free)
                        b.emit("V", lambda e, half=half, t=t: e.tensor_copy(
                            ln1T[:, 4 * half:4 * half + 4, t * 128:(t + 1) * 128],
                            psT[half][:, :].rearrange("p (a q) -> p a q", a=4)),
                            waits=wl, inc="V")
                        psT_free[half] = b.vv()
                    lnb_free = b.vv()
                lnT_ready = b.vv()
                if c + 1 < NCH:
                    ph1_load_stats(c + 1)

                for ft in range(8):
                    bank = ft % 2
                    wl = [lnT_ready, ("dWgt", dW1)]
                    if kqv_free[bank]:
                        wl.append(kqv_free[bank])
                    for dt in range(DT):
                        b.emit("T", lambda e, bank=bank, ft=ft, dt=dt: e.matmul(
                            kqv[bank][:, :], wk_sb[:, dt, ft * 128:(ft + 1) * 128],
                            ln1T[:, dt, :], start=(dt == 0), stop=(dt == DT - 1)),
                            waits=(wl if dt == 0 else ()), inc=("T" if dt == DT - 1 else None))
                    mmw = b.tt()
                    b.emit("S", lambda e, bank=bank, ft=ft, c=c: e.activation(
                        kT_sb[:, ft, c * 512:(c + 1) * 512], kqv[bank][:, :], AF.Identity,
                        bias=bk_sb[:, ft:ft + 1]),
                        waits=[mmw, ("dC", dC1)], inc="S")
                    kqv_free[bank] = b.ss()
                kevac = b.c("S")

                for t in range(4):
                    tt = 4 * c + t
                    for nh in range(2):
                        bank = (t * 2 + nh) % 2
                        wl = [lnT_ready]
                        if kqv_free[bank]:
                            wl.append(kqv_free[bank])
                        for dt in range(DT):
                            b.emit("T", lambda e, bank=bank, t=t, nh=nh, dt=dt: e.matmul(
                                kqv[bank][:, :], ln1T[:, dt, t * 128:(t + 1) * 128],
                                wv_sb[:, dt, nh * 512:(nh + 1) * 512],
                                start=(dt == 0), stop=(dt == DT - 1)),
                                waits=(wl if dt == 0 else ()), inc=("T" if dt == DT - 1 else None))
                        mmw = b.tt()
                        b.emit("V", lambda e, bank=bank, tt=tt, nh=nh: e.tensor_add(
                            v_sb[:, tt, nh * 520:(nh + 1) * 520]
                            .rearrange("p (h q) -> p h q", h=8)[:, :, 0:64],
                            kqv[bank][:, :].rearrange("p (h q) -> p h q", h=8),
                            bv_sb[:, nh * 512:(nh + 1) * 512]
                            .rearrange("p (h q) -> p h q", h=8)),
                            waits=[mmw, ("dC", dC1), ("G", 1)], inc="V")
                        kqv_free[bank] = b.vv()

                for ft in range(8):
                    wl = [("S", kevac), ("dC", dC0)]
                    if shf_free:
                        wl.append(shf_free)
                    b.emit("T", lambda e, ft=ft, c=c: e.matmul(
                        shf[:, :], perm_sb[:, :], kT_sb[:, ft, c * 512:(c + 1) * 512],
                        start=True, stop=True),
                        waits=wl, inc="T")
                    shw = b.tt()
                    b.emit("V", lambda e, ft=ft, c=c, cslot=cslot: e.tensor_mul(
                        tmp1[:, :], kT_sb[:, ft, c * 512:(c + 1) * 512], cosS[:, cslot, :]),
                        waits=[("S", kevac), tab_w, b.vv()], inc="V")
                    b.emit("V", lambda e, cslot=cslot: e.tensor_mul(
                        tmp2[:, :], shf[:, :], sinS[:, cslot, :]),
                        waits=[shw, b.vv()], inc="V")
                    shf_free = b.vv()
                    b.emit("V", lambda e, ft=ft, c=c: e.tensor_add(
                        kT_sb[:, ft, c * 512:(c + 1) * 512], tmp1[:, :], tmp2[:, :]),
                        waits=[b.vv()], inc="V")
                cfree[cslot] = b.vv()
                ln1T_free = b.tt()

        b.barrier()

        with ExitStack() as sB:
            EB = sB.enter_context
            wq_sb = EB(nc.sbuf_tensor("wq_sb", [128, DT, D], BF16))
            cosQs = EB(nc.sbuf_tensor("cosQs", [128, 2048], F32))
            sinQs = EB(nc.sbuf_tensor("sinQs", [128, 2048], F32))
            lnq = EB(nc.sbuf_tensor("lnq", [128, DT, 256], BF16))
            xq = EB(nc.sbuf_tensor("xq", [128, 2, D], F32))
            lnb2 = EB(nc.sbuf_tensor("lnb2", [128, D], BF16))
            qTq = EB(nc.sbuf_tensor("qTq", [128, DT, 256], BF16))
            tq1 = EB(nc.sbuf_tensor("tq1", [128, 256], F32))
            tq2 = EB(nc.sbuf_tensor("tq2", [128, 256], F32))
            psT2 = [EB(nc.psum_tensor(f"psTa{i}", [128, 512], BF16)) for i in range(2)]
            qps = [EB(nc.psum_tensor(f"qps{i}", [128, 256], F32)) for i in range(2)]
            shq = EB(nc.psum_tensor("shq", [128, 256], F32))
            dump2 = EB(nc.psum_tensor("dump2", [128, 1024], F32))

            b.emit("Y", lambda e: e.dma_start(cosQs[:], cosQ_d[:, :]), inc="dC", k=16)
            b.emit("Y", lambda e: e.dma_start(sinQs[:], sinQ_d[:, :]), inc="dC", k=16)
            b.emit("G", lambda e: e.dma_start(
                wq_sb[:], wqT_d[:, :, :]), inc="dWgt", k=16)
            dC2 = b.c("dC")
            dW2 = b.c("dWgt")

            xqfree = [None, None]
            lnb2_free = None
            psT2_free = [None, None]
            qps_free = [None, None]
            shq_free = None
            lnq_free = None
            qTq_free = None
            dump2_free = None

            def ph2_front(pr):
                nonlocal dump2_free
                for t in range(2):
                    slot = 2 * pr + t
                    xsem = ("dXQa", "dXQb")[t]
                    lw = [x for x in [xqfree[t]] if x]
                    b.emit("Y", lambda e, slot=slot, t=t: e.dma_start(
                        xq[:, t, :], xown_d[slot * 128:(slot + 1) * 128, :]),
                        waits=lw, inc=xsem, k=16)
                    xw = (xsem, b.c(xsem))
                    b.emit("V", lambda e, t=t: e.reduce_sum(
                        st1[:, t:t + 1], xq[:, t, :], axis=AX.X, negate=True),
                        waits=[xw], inc="V")
                    dw = [x for x in [dump2_free] if x]
                    b.emit("S", lambda e, t=t: e.activation(
                        dump2[:, :], xq[:, t, :], AF.Square, accum_out=st2[:, t:t + 1]),
                        waits=[xw] + dw, inc="S")
                    dump2_free = b.ss()
                return ln_stats_newton(2)

            stats_next = ph2_front(0)
            for pr in range(8):
                stats_done = stats_next

                for t in range(2):
                    lw = [("V", stats_done)] + ([lnb2_free] if lnb2_free else [])
                    b.emit("V", lambda e, t=t: e.tensor_scalar(
                        lnb2[:, :], xq[:, t, :], nmu[:, t:t + 1], nwy[:, t:t + 1],
                        ALU.add, ALU.mult),
                        waits=lw, inc="V")
                    lnw = b.vv()
                    xqfree[t] = lnw
                    for half in range(2):
                        for q in range(4):
                            dt = 4 * half + q
                            wl = [lnw]
                            if q == 0 and psT2_free[half]:
                                wl.append(psT2_free[half])
                            wl = wl + [("dC", dC2)]
                            b.emit("T", lambda e, half=half, q=q, dt=dt: e.transpose(
                                psT2[half][:, q * 128:(q + 1) * 128],
                                lnb2[:, dt * 128:(dt + 1) * 128], idb_sb[:]),
                                waits=wl, inc="T" if q == 3 else None)
                        trw = b.tt()
                        wl = [trw]
                        if t == 0 and half == 0 and lnq_free:
                            wl.append(lnq_free)
                        b.emit("V", lambda e, half=half, t=t: e.tensor_copy(
                            lnq[:, 4 * half:4 * half + 4, t * 128:(t + 1) * 128],
                            psT2[half][:, :].rearrange("p (a q) -> p a q", a=4)),
                            waits=wl, inc="V")
                        psT2_free[half] = b.vv()
                    lnb2_free = b.vv()
                lnq_ready = b.vv()
                if pr + 1 < 8:
                    stats_next = ph2_front(pr + 1)

                for ft in range(8):
                    bank = ft % 2
                    wl = [lnq_ready, ("dWgt", dW2)]
                    if qps_free[bank]:
                        wl.append(qps_free[bank])
                    if ft < 2 and qTq_free:
                        wl.append(qTq_free)
                    for dt in range(DT):
                        b.emit("T", lambda e, bank=bank, ft=ft, dt=dt: e.matmul(
                            qps[bank][:, :], wq_sb[:, dt, ft * 128:(ft + 1) * 128],
                            lnq[:, dt, :], start=(dt == 0), stop=(dt == DT - 1)),
                            waits=(wl if dt == 0 else ()), inc=("T" if dt == DT - 1 else None))
                    mmw = b.tt()
                    b.emit("S", lambda e, bank=bank, ft=ft: e.activation(
                        qTq[:, ft, :], qps[bank][:, :], AF.Identity, bias=bq_sb[:, ft:ft + 1]),
                        waits=[mmw, ("dC", dC2)] + ([qTq_free] if qTq_free else []), inc="S")
                    qps_free[bank] = b.ss()
                qevac = b.c("S")
                lnq_free = b.tt()

                for ft in range(8):
                    wl = [("S", qevac)] + ([shq_free] if shq_free else [])
                    b.emit("T", lambda e, ft=ft: e.matmul(
                        shq[:, :], perm_sb[:, :], qTq[:, ft, :], start=True, stop=True),
                        waits=wl + [("dC", dC2)], inc="T")
                    shw = b.tt()
                    b.emit("V", lambda e, ft=ft, pr=pr: e.tensor_mul(
                        tq1[:, :], qTq[:, ft, :], cosQs[:, pr * 256:(pr + 1) * 256]),
                        waits=[("S", qevac), ("dC", dC2), b.vv()], inc="V")
                    b.emit("V", lambda e, pr=pr: e.tensor_mul(
                        tq2[:, :], shq[:, :], sinQs[:, pr * 256:(pr + 1) * 256]),
                        waits=[shw, b.vv()], inc="V")
                    shq_free = b.vv()
                    b.emit("V", lambda e, ft=ft: e.tensor_add(
                        qTq[:, ft, :], tq1[:, :], tq2[:, :]),
                        waits=[b.vv()], inc="V")
                ropeq = b.vv()
                b.emit("Y", lambda e, pr=pr: e.dma_start(
                    qT_s[:, :, pr * 256:(pr + 1) * 256].rearrange("dt p q -> p dt q"),
                    qTq[:, :, :]),
                    waits=[ropeq], inc="dQW", k=16)
                qTq_free = ("dQW", b.c("dQW"))

        b.barrier()
        dQW_all = b.c("dQW")

        with ExitStack() as sC:
            EC = sC.enter_context
            wo_sb = EC(nc.sbuf_tensor("wo_sb", [128, DT, D], BF16))
            qTc = EC(nc.sbuf_tensor("qTc", [128, DT, 512], BF16))
            masks_sb = EC(nc.sbuf_tensor("masks_sb", [128, 8, 512], BF16))
            p_sb = EC(nc.sbuf_tensor("p_sb", [128, 8, 512], BF16))
            aoT = EC(nc.sbuf_tensor("aoT", [128, DT, 512], BF16))
            rcs = EC(nc.sbuf_tensor("rcs", [128, 512], BF16))
            bc_sb = EC(nc.sbuf_tensor("bc_sb", [128, 512], BF16))
            sel_sb = EC(nc.sbuf_tensor("sel_sb", [128, 128], BF16))
            xres = EC(nc.sbuf_tensor("xres", [128, D], F32))
            x2_sb = EC(nc.sbuf_tensor("x2_sb", [128, 2, D], F32))
            ln2b4 = EC(nc.sbuf_tensor("ln2b4", [128, 4, D], BF16))
            hTt = EC(nc.sbuf_tensor("hTt", [128, DT, 128], BF16))
            bo_sb = EC(nc.sbuf_tensor("bo_sb", [128, D], F32))
            rw_sb = EC(nc.sbuf_tensor("rw_sb", [128, DT, 8], BF16))
            pr_sb = EC(nc.sbuf_tensor("pr_sb", [128, 8], F32))
            p2_sb = EC(nc.sbuf_tensor("p2_sb", [128, 8], F32))
            mk1 = EC(nc.sbuf_tensor("mk1", [128, 8], F32))
            mk2 = EC(nc.sbuf_tensor("mk2", [128, 8], F32))
            rr1 = EC(nc.sbuf_tensor("rr1", [128, 1], F32))
            rr2 = EC(nc.sbuf_tensor("rr2", [128, 1], F32))
            rr3 = EC(nc.sbuf_tensor("rr3", [128, 1], F32))
            sps = [EC(nc.psum_tensor(f"sps{i}", [128, 1024], F32)) for i in range(2)]
            aops = [EC(nc.psum_tensor(f"aops{i}", [128, 512], F32)) for i in range(2)]
            bcps = EC(nc.psum_tensor("bcps", [128, 512], F32))
            psT3 = EC(nc.psum_tensor("psT3", [128, 512], BF16))
            ops_ = [sps[0], sps[1]]

            b.emit("Y", lambda e: e.dma_start(masks_sb[:], masks_d[:, :, :]), inc="dC", k=16)
            b.emit("Y", lambda e: e.dma_start(sel_sb[:], sel0_d[:, :]), inc="dC", k=16)
            b.emit("Y", lambda e: e.dma_start(bo_sb[:], bo_d[:, :]), inc="dC", k=16)
            b.emit("G", lambda e: e.dma_start(
                wo_sb[:], woT_d[:, :, :]), inc="dWgt", k=16)
            b.emit("G", lambda e: e.dma_start(
                rw_sb[:], rw_d[:, :, :]), inc="dWgt", k=16)
            b.emit("G", lambda e: e.memset(rcs[:, :], 0.0), inc="G")
            dC3 = b.c("dC")
            dW3 = b.c("dWgt")
            sG3 = b.c("G")

            exp_war = {}
            av_war = {}
            sps_war = {}
            lnfreeY = None
            lnwarT = [None] * 4
            ao_free = [None, None]
            aoT_reuse = None
            rcs_free = None
            ops_free = [None, None]
            x2free = {0: None, 1: None}
            hTt_free = None
            psT3_free = None
            qtc_war = None
            sq_s = None

            for j in range(NJ):
                nk = 8 * j + 8
                bc_war_v = b.vv()
                bc_war_s = b.ss()
                b.emit("Y", lambda e, j=j: e.dma_start(
                    qTc[:, :, :],
                    qT_s[:, :, j * 512:(j + 1) * 512].rearrange("dt p q -> p dt q")),
                    waits=[("dQW", dQW_all), qtc_war if qtc_war else b.tt()],
                    inc="dQR", k=16)
                qw = ("dQR", b.c("dQR"))

                epi_pending = None
                for h in range(NHEAD):
                    po = (h % 2) * 64
                    fth = h // 2
                    ab = h % 2
                    LEAD = 4 if j == 0 else (3 if j == 1 else 2)
                    npr = nk // 2
                    pready = {}
                    for pi in range(npr + LEAD):
                        if pi == 2 and epi_pending is not None:
                            epi_pending()
                            epi_pending = None
                        if pi < npr:
                            kp = pi
                            pb = kp % 2
                            ps4 = kp % 4
                            wl = [qw]
                            if exp_war.get(pb) is not None:
                                wl.append(("S", exp_war[pb]))
                            if sps_war.get(pb) is not None:
                                wl.append(sps_war[pb])
                                sps_war[pb] = None
                            for kk2 in range(2):
                                k = 2 * kp + kk2
                                b.emit("T", lambda e, k=k, pb=pb, kk2=kk2, po=po, fth=fth: e.matmul(
                                    sps[pb][:, kk2 * 512:(kk2 + 1) * 512],
                                    kT_sb[po:po + 64, fth, k * 128:(k + 1) * 128],
                                    qTc[po:po + 64, fth, :], start=True, stop=True),
                                    waits=(wl if kk2 == 0 else ()), inc=("T" if kk2 == 1 else None))
                            scw = b.tt()
                            wl = [scw]
                            if av_war.get(ps4) is not None:
                                wl.append(("T", av_war[ps4]))
                            b.emit("S", lambda e, pb=pb, ps4=ps4: e.activation(
                                p_sb[:, 2 * ps4:2 * ps4 + 2, :], sps[pb][:, :],
                                AF.Exp, scale=0.125),
                                waits=wl, inc="S")
                            exp_war[pb] = b.c("S")
                            pw = b.ss()
                            if kp >= 4 * j:
                                r0 = 2 * (kp - 4 * j)
                                b.emit("V", lambda e, ps4=ps4, r0=r0: e.tensor_mul(
                                    p_sb[:, 2 * ps4:2 * ps4 + 2, :],
                                    p_sb[:, 2 * ps4:2 * ps4 + 2, :],
                                    masks_sb[:, r0:r0 + 2, :]),
                                    waits=[pw, ("dC", dC3)], inc="V")
                                pw = b.vv()
                            pready[kp] = pw
                        if pi >= LEAD:
                            kp2 = pi - LEAD
                            ps4 = kp2 % 4
                            wl = [pready.pop(kp2), ("G", 1)]
                            if kp2 == 0 and ao_free[ab] is not None:
                                wl.append(ao_free[ab])
                            for kk2 in range(2):
                                k = 2 * kp2 + kk2
                                b.emit("T", lambda e, k=k, h=h, ps4=ps4, kk2=kk2, ab=ab, nk=nk: e.matmul(
                                    aops[ab][0:65, :], v_sb[:, k, h * 65:(h + 1) * 65],
                                    p_sb[:, 2 * ps4 + kk2, :], start=(k == 0), stop=(k == nk - 1)),
                                    waits=(wl if kk2 == 0 else ()), inc=("T" if kk2 == 1 else None))
                            av_war[ps4] = b.c("T")
                    av_last = b.tt()

                    def _epi(av_last=av_last, ab=ab, po=po, fth=fth):
                        nonlocal rcs_free
                        wl = [av_last, ("G", sG3)] + ([rcs_free] if rcs_free else [])
                        def _recip(e, ab=ab):
                            with nc.allow_low_precision(reason="softmax denominator reciprocal cast to bf16 for the selector broadcast matmul"):
                                return e.reciprocal(rcs[0:1, :], aops[ab][64:65, :])
                        b.emit("V", _recip, waits=wl, inc="V")
                        rw_ = b.vv()
                        b.emit("T", lambda e: e.matmul(
                            bcps[:, :], sel_sb[:, :], rcs[:, :], start=True, stop=True),
                            waits=[rw_, ("dC", dC3), bc_war_v, bc_war_s, b.ss()], inc="T")
                        bcw = b.tt()
                        rcs_free = bcw
                        b.emit("S", lambda e: e.activation(
                            bc_sb[0:64, :], bcps[0:64, :], AF.Identity),
                            waits=[bcw, b.ss()], inc="S")
                        bcev = b.ss()
                        wl = [bcev] + ([aoT_reuse] if aoT_reuse else [])
                        b.emit("V", lambda e, ab=ab, po=po, fth=fth: e.tensor_mul(
                            aoT[po:po + 64, fth, :], aops[ab][0:64, :], bc_sb[0:64, :]),
                            waits=wl, inc="V")
                        ao_free[ab] = b.vv()

                    if h == NHEAD - 1:
                        _epi()
                    else:
                        epi_pending = _epi
                aoT_ready = b.vv()
                qtc_war = b.tt()

                lnws = []
                for i in range(4):
                    slot = 4 * j + i
                    xp = i % 2
                    b.emit("Y", lambda e, slot=slot: e.dma_start(
                        xres[:, :], xown_d[slot * 128:(slot + 1) * 128, :]),
                        waits=[b.vv()] + ([sq_s] if sq_s else []), inc="dXR", k=16)
                    xw = ("dXR", b.c("dXR"))
                    wl = [xw, ("dC", dC3), b.vv()]
                    if x2free[xp] is not None:
                        wl.append(x2free[xp])
                    b.emit("V", lambda e, xp=xp: e.tensor_add(
                        x2_sb[:, xp, :], xres[:, :], bo_sb[:, :]),
                        waits=wl, inc="V")
                    xbw = b.vv()
                    for oc in range(2):
                        wl = [aoT_ready, ("dWgt", dW3)]
                        if ops_free[oc]:
                            wl.append(ops_free[oc])
                        if exp_war.get(oc) is not None:
                            wl.append(("S", exp_war[oc]))
                        for dt in range(DT):
                            b.emit("T", lambda e, oc=oc, i=i, dt=dt: e.matmul(
                                ops_[oc][:, 0:512], aoT[:, dt, i * 128:(i + 1) * 128],
                                wo_sb[:, dt, oc * 512:(oc + 1) * 512],
                                start=(dt == 0), stop=(dt == DT - 1)),
                                waits=(wl if dt == 0 else ()), inc=("T" if dt == DT - 1 else None))
                        mmw = b.tt()
                        b.emit("V", lambda e, oc=oc, xp=xp: e.tensor_add(
                            x2_sb[:, xp, oc * 512:(oc + 1) * 512], ops_[oc][:, 0:512],
                            x2_sb[:, xp, oc * 512:(oc + 1) * 512]),
                            waits=[mmw, xbw], inc="V")
                        ops_free[oc] = b.vv()
                    x2w = b.vv()
                    b.emit("V", lambda e, i=i, xp=xp: e.reduce_sum(
                        st1[:, i:i + 1], x2_sb[:, xp, :], axis=AX.X, negate=True),
                        waits=[x2w], inc="V")
                    b.emit("S", lambda e, i=i, xp=xp: e.activation(
                        xres[:, :], x2_sb[:, xp, :], AF.Square, accum_out=st2[:, i:i + 1]),
                        waits=[x2w, b.ss()], inc="S")
                    sq_s = b.ss()
                    stats_done = ln_stats_newton(1, co=i)
                    wl = [("V", stats_done)]
                    if lnwarT[i] is not None:
                        wl.append(lnwarT[i])
                    if lnfreeY is not None:
                        wl.append(lnfreeY)
                    b.emit("V", lambda e, i=i, xp=xp: e.tensor_scalar(
                        ln2b4[:, i, :], x2_sb[:, xp, :], nmu[:, i:i + 1], nwy[:, i:i + 1],
                        ALU.add, ALU.mult),
                        waits=wl, inc="V")
                    lnw = b.vv()
                    lnws.append(lnw)
                    b.emit("Y", lambda e, slot=slot, i=i: e.dma_start(
                        hTok_s[slot * 128:(slot + 1) * 128, :], ln2b4[:, i, :]),
                        waits=[lnw], inc="dHW", k=16)
                    xsm = "dX2W" if xp == 0 else "dX2X"
                    b.emit("Y", lambda e, slot=slot, xp=xp: e.dma_start(
                        x2_s[slot * 128:(slot + 1) * 128, :], x2_sb[:, xp, :]),
                        waits=[lnw], inc=xsm, k=16)
                    x2free[xp] = (xsm, b.c(xsm))
                lnfreeY = ("dHW", b.c("dHW"))
                aoT_reuse = b.tt()
                sps_war[0] = ops_free[0]
                sps_war[1] = ops_free[1]

                for i in range(4):
                    slot = 4 * j + i
                    for half in range(2):
                        for q in range(4):
                            dt = 4 * half + q
                            wl = [lnws[i]]
                            if q == 0 and psT3_free:
                                wl.append(psT3_free)
                            wl = wl + [("dC", dC3)]
                            b.emit("T", lambda e, q=q, dt=dt, i=i: e.transpose(
                                psT3[:, q * 128:(q + 1) * 128],
                                ln2b4[:, i, dt * 128:(dt + 1) * 128], idb_sb[:]),
                                waits=wl, inc="T" if q == 3 else None)
                        trw = b.tt()
                        wl = [trw] + (list(hTt_free) if (half == 0 and hTt_free) else [])
                        b.emit("V", lambda e, half=half: e.tensor_copy(
                            hTt[:, 4 * half:4 * half + 4, :],
                            psT3[:, :].rearrange("p (a q) -> p a q", a=4)),
                            waits=wl, inc="V")
                        psT3_free = b.vv()
                    lnwarT[i] = b.tt()
                    httw = b.vv()
                    for dt in range(DT):
                        b.emit("T", lambda e, dt=dt: e.matmul(
                            bcps[:, 0:8], hTt[:, dt, :], rw_sb[:, dt, :],
                            start=(dt == 0), stop=(dt == DT - 1)),
                            waits=([httw, ("dWgt", dW3), b.vv(), b.ss()] if dt == 0 else ()),
                            inc=("T" if dt == DT - 1 else None))
                    rmm = b.tt()
                    hTt_free = [rmm]
                    b.emit("V", lambda e: e.reduce_max(
                        rr1[:, 0:1], bcps[:, 0:8], axis=AX.X, negate=True),
                        waits=[rmm], inc="V")
                    b.emit("S", lambda e: e.activation(
                        pr_sb[:, :], bcps[:, 0:8], AF.Exp, bias=rr1[:, 0:1]),
                        waits=[b.vv(), b.ss()], inc="S")
                    spr = b.ss()
                    b.emit("V", lambda e: e.reduce_max(rr1[:, 0:1], pr_sb[:, :], axis=AX.X),
                           waits=[spr, b.vv()], inc="V")
                    b.emit("V", lambda e: e.tensor_single_scalar(
                        mk1[:, :], pr_sb[:, :], rr1[:, 0:1], ALU.is_ge),
                        waits=[b.vv()], inc="V")
                    b.emit("V", lambda e: e.scalar_tensor_tensor(
                        p2_sb[:, :], mk1[:, :], -4.0, pr_sb[:, :], ALU.mult, ALU.add),
                        waits=[b.vv()], inc="V")
                    b.emit("V", lambda e: e.reduce_max(rr2[:, 0:1], p2_sb[:, :], axis=AX.X),
                           waits=[b.vv()], inc="V")
                    b.emit("V", lambda e: e.tensor_single_scalar(
                        mk2[:, :], p2_sb[:, :], rr2[:, 0:1], ALU.is_ge),
                        waits=[b.vv()], inc="V")
                    b.emit("V", lambda e: e.tensor_add(rr3[:, 0:1], rr1[:, 0:1], rr2[:, 0:1]),
                           waits=[b.vv()], inc="V")
                    b.emit("V", lambda e: e.tensor_single_scalar(
                        rr3[:, 0:1], rr3[:, 0:1], 2.0, ALU.mult),
                        waits=[b.vv()], inc="V")
                    b.emit("V", lambda e: e.reciprocal(rr3[:, 0:1], rr3[:, 0:1]),
                           waits=[b.vv()], inc="V")
                    b.emit("V", lambda e, slot=slot: e.tensor_add(
                        msel[:, slot, :], mk1[:, :], mk2[:, :]),
                        waits=[b.vv()], inc="V")
                    b.emit("V", lambda e, slot=slot: e.tensor_mul(
                        mk1[:, :], msel[:, slot, :], pr_sb[:, :]),
                        waits=[b.vv()], inc="V")
                    b.emit("V", lambda e, slot=slot: e.tensor_single_scalar(
                        w_sb[:, slot, :], mk1[:, :], rr3[:, 0:1], ALU.mult),
                        waits=[b.vv()], inc="V")
            dX2W_all = b.c("dX2W")
            dX2X_all = b.c("dX2X")
            dHW_all = b.c("dHW")
            router_done = b.c("V")

        sKV.close()
        b.barrier()

        with ExitStack() as sD:
            ED = sD.enter_context
            hTok = ED(nc.sbuf_tensor("hTok", [128, OT, D], BF16))
            acc = ED(nc.sbuf_tensor("acc", [128, OT, D], BF16))
            gbuf = [ED(nc.sbuf_tensor(f"gbuf{i}", [128, DT, 512], BF16)) for i in range(2)]
            ubuf = [ED(nc.sbuf_tensor(f"ubuf{i}", [128, DT, 512], BF16)) for i in range(2)]
            dbuf = [ED(nc.sbuf_tensor(f"dbuf{i}", [128, 4, D], BF16)) for i in range(2)]
            gT = ED(nc.sbuf_tensor("gT", [128, OT, CAP], BF16))
            gS = ED(nc.sbuf_tensor("gS", [128, CT, 2048], BF16))
            hgT = ED(nc.sbuf_tensor("hgT", [128, DT, CAP], BF16))
            act_sb = ED(nc.sbuf_tensor("act_sb", [128, 4, CAP], BF16))
            dwacc = ED(nc.sbuf_tensor("dwacc", [128, CT, D], BF16))
            t_sb = ED(nc.sbuf_tensor("t_sb", [128, CAP], BF16))
            gp_sb = ED(nc.sbuf_tensor("gp_sb", [128, CAP], BF16))
            x2t = [ED(nc.sbuf_tensor(f"x2t{i}", [128, D], F32)) for i in range(2)]
            iota_sb = ED(nc.sbuf_tensor("iota_sb", [128, CAP], F32))
            trils_sb = ED(nc.sbuf_tensor("trils_sb", [128, 128], BF16))
            onesf_sb = ED(nc.sbuf_tensor("onesf_sb", [128, 128], BF16))
            rank_sb = ED(nc.sbuf_tensor("rank_sb", [128, OT, 8], F32))
            pA = [ED(nc.psum_tensor(f"pA{i}", [128, 512], F32)) for i in range(4)]
            pB = [ED(nc.psum_tensor(f"pB{i}", [128, 512], F32)) for i in range(3)]

            for nm, dd in ((iota_sb, iota_d),
                           (trils_sb, trils_d), (onesf_sb, onesf_d)):
                b.emit("Y", lambda e, nm=nm, dd=dd: e.dma_start(nm[:], dd[:]), inc="dC", k=16)
            b.emit("Y", lambda e: e.dma_start(
                hTok[:, :, :], hTok_s[:, :].rearrange("(t p) d -> p t d", p=128)),
                waits=[("dHW", dHW_all)], inc="dHR", k=16)
            dC4 = b.c("dC")
            hTw = ("dHR", b.c("dHR"))
            sG4 = b.c("G")

            rk_done = []
            for i in range(OT):
                bank = pA[i % 2]
                wl = [("V", router_done), ("dC", dC4)]
                if i >= 2:
                    wl.append(rk_done[i - 2])
                for jj in range(i + 1):
                    b.emit("T", lambda e, i=i, jj=jj, bank=bank: e.matmul(
                        bank[:, 0:8], trils_sb[:, :] if jj == i else onesf_sb[:, :],
                        msel[:, jj, :], start=(jj == 0), stop=(jj == i)),
                        waits=(wl if jj == 0 else ()), inc=("T" if jj == i else None))
                mmw = b.tt()
                b.emit("V", lambda e, i=i, bank=bank: e.scalar_tensor_tensor(
                    rank_sb[:, i, :], msel[:, i, :], 4096.0, bank[:, 0:8],
                    ALU.mult, ALU.add),
                    waits=[mmw, b.vv()], inc="V")
                b.emit("V", lambda e, i=i: e.tensor_single_scalar(
                    rank_sb[:, i, :], rank_sb[:, i, :], -4096.0, ALU.add),
                    waits=[b.vv()], inc="V")
                rk_done.append(b.vv())
            rank_ready = b.vv()


            wsem = ["dMA", "dMB"]
            wload = {}
            wfree = {0: None, 1: None}

            def load_chunk(stp):
                ei, ci = stp // 6, stp % 6
                f0, nf = FCH[ci]
                buf = stp % 2
                sname = wsem[buf]
                wl = [x for x in [wfree[buf]] if x]
                if ci < 5:
                    gsrc = lambda e, ei=ei, ci=ci: gate6_d[ei, ci, :, :, :]
                    usrc = lambda e, ei=ei, ci=ci: up6_d[ei, ci, :, :, :]
                    dsrc = lambda e, ei=ei, ci=ci: down6_d[ei, ci, :, :, :]
                else:
                    gsrc = lambda e, ei=ei: gate4_d[ei, :, :, :]
                    usrc = lambda e, ei=ei: up4_d[ei, :, :, :]
                    dsrc = lambda e, ei=ei: down4_d[ei, :, :, :]
                b.emit("G", lambda e, nf=nf, buf=buf, gsrc=gsrc: e.dma_start(
                    gbuf[buf][:, :, 0:nf * 128], gsrc(e)), waits=wl, inc=sname, k=16)
                b.emit("G", lambda e, nf=nf, buf=buf, usrc=usrc: e.dma_start(
                    ubuf[buf][:, :, 0:nf * 128], usrc(e)), inc=sname, k=16)
                wload[stp] = (sname, b.c(sname))

            dfree = {0: None, 1: None}

            def load_dchunk(stp):
                ei, ci = stp // 6, stp % 6
                f0, nf = FCH[ci]
                dbf = stp % 2
                if ci < 5:
                    dsrc = lambda e, ei=ei, ci=ci: down6_d[ei, ci, :, :, :]
                else:
                    dsrc = lambda e, ei=ei: down4_d[ei, :, :, :]
                dsem = "dMD" if dbf == 0 else "dME"
                wl = [x for x in [dfree[dbf]] if x]
                b.emit("G", lambda e, nf=nf, dbf=dbf, dsrc=dsrc: e.dma_start(
                    dbuf[dbf][:, 0:nf, :], dsrc(e)), waits=wl, inc=dsem, k=16)
                wload[("d", stp)] = (dsem, b.c(dsem))

            load_chunk(0)
            load_chunk(1)
            load_dchunk(0)
            load_dchunk(1)

            gather_T = None
            hg_S = {}
            gu_war = {}
            dw_war = {}
            act_T = None
            t_free = None
            rflush = {}
            s_ffn = None
            xld = {}

            gT_pipe = {}

            def build_gT(ei):
                wl = [rank_ready, ("dC", dC4)]
                if gather_T is not None:
                    wl.append(gather_T)
                for ti in range(OT):
                    b.emit("V", lambda e, ti=ti, ei=ei: e.tensor_single_scalar(
                        gT[:, ti, :], iota_sb[:, :], rank_sb[:, ti, ei:ei + 1],
                        ALU.is_equal),
                        waits=wl + [b.vv()], inc="V")
                gT_pipe[ei] = b.vv()

            build_gT(0)
            for ei in range(8):
                gT_ready = gT_pipe[ei]

                with nc.psum_tensor(f"psT4_{ei}", [128, 1024], BF16) as psT4:
                    psT4_free = None
                    gidx = 0
                    for ct in range(CT):
                        for tp in range(2):
                            if gidx == 0:
                                if ei == 0:
                                    wl2 = [gT_ready, b.vv(), b.ss(), ("dC", dC4)]
                                else:
                                    wl2 = [gT_ready, rflush[6], s_ffn, ("dC", dC4)]
                            else:
                                wl2 = [gT_ready, psT4_free]
                            for q8 in range(8):
                                ti = tp * 8 + q8
                                b.emit("T", lambda e, ti=ti, ct=ct, q8=q8, psT4=psT4: e.transpose(
                                    psT4[:, q8 * 128:(q8 + 1) * 128],
                                    gT[:, ti, ct * 128:(ct + 1) * 128], idb_sb[:]),
                                    waits=(wl2 if q8 == 0 else ()),
                                    inc=("T" if q8 == 7 else None))
                            trw = b.tt()
                            b.emit("V", lambda e, ct=ct, tp=tp, psT4=psT4: e.tensor_copy(
                                gS[:, ct, tp * 1024:(tp + 1) * 1024], psT4[:, :]),
                                waits=[trw, b.vv()], inc="V")
                            psT4_free = b.vv()
                            gidx += 1
                gs_wl = [psT4_free, psT4_free]

                with nc.psum_tensor(f"pg_{ei}", [128, 512], F32) as pg256:
                    for dh2 in range(2):
                        for q in range(4):
                            dt = 4 * dh2 + q
                            pbq = pB[q] if q < 3 else pg256
                            if dh2 == 0:
                                if ei == 0:
                                    wl = [hTw, b.vv(), b.ss()]
                                else:
                                    wl = [hTw, rflush[7], s_ffn]
                                if q == 3:
                                    wl = wl + gs_wl
                            else:
                                wl = [hTw, ("S", hg_S[(0, q)])]
                            for ti in range(OT):
                                b.emit("T", lambda e, ti=ti, dt=dt, q=q: e.matmul(
                                    pA[q][:, :], hTok[:, ti, dt * 128:(dt + 1) * 128],
                                    gT[:, ti, 0:512], start=(ti == 0), stop=(ti == OT - 1)),
                                    waits=(wl if ti == 0 else ()), inc=("T" if ti == OT - 1 else None))
                                b.emit("T", lambda e, ti=ti, dt=dt, pbq=pbq: e.matmul(
                                    pbq[:, 0:256], hTok[:, ti, dt * 128:(dt + 1) * 128],
                                    gT[:, ti, 512:768], start=(ti == 0), stop=(ti == OT - 1)),
                                    inc=("T" if ti == OT - 1 else None))
                            mmw = b.tt()
                            b.emit("S", lambda e, dt=dt, q=q: e.activation(
                                hgT[:, dt, 0:512], pA[q][:, :], AF.Identity),
                                waits=[mmw, b.ss()], inc="S")
                            b.emit("S", lambda e, dt=dt, pbq=pbq: e.activation(
                                hgT[:, dt, 512:768], pbq[:, 0:256], AF.Identity),
                                waits=[b.ss()], inc="S")
                            hg_S[(dh2, q)] = b.c("S")
                gather_T = b.tt()
                hg_ready = [("S", b.c("S"))]
                if ei + 1 < 8:
                    build_gT(ei + 1)

                for ci in range(6):
                    stp = ei * 6 + ci
                    f0, nf = FCH[ci]
                    buf = stp % 2
                    lw = wload[stp]
                    act_w = {}
                    tw_bank = {}
                    for fl in range(nf):
                        gb = fl % 2
                        wl = [lw] + hg_ready
                        if gu_war.get(("g", gb)) is not None:
                            wl.extend(gu_war[("g", gb)])
                        for half, (pp, w0, w1) in enumerate(
                                ((pA, 0, 512), (pB, 512, 768))):
                            for dt in range(DT):
                                b.emit("T", lambda e, gb=gb, buf=buf, fl=fl, dt=dt, pp=pp, w0=w0, w1=w1: e.matmul(
                                    pp[gb][:, 0:w1 - w0], gbuf[buf][:, dt, fl * 128:(fl + 1) * 128],
                                    hgT[:, dt, w0:w1], start=(dt == 0), stop=(dt == DT - 1)),
                                    waits=(wl if (dt == 0 and half == 0) else ()),
                                    inc=("T" if dt == DT - 1 else None))
                        gmm = b.tt()
                        wl = [lw]
                        if gu_war.get(("u", gb)) is not None:
                            wl.extend(gu_war[("u", gb)])
                        if gu_war.get(("ub",)) is not None:
                            wl.extend(gu_war[("ub",)])
                        for half, (upp, w0, w1) in enumerate(
                                ((pA[2 + gb], 0, 512), (pB[2], 512, 768))):
                            for dt in range(DT):
                                b.emit("T", lambda e, buf=buf, fl=fl, dt=dt, upp=upp, w0=w0, w1=w1: e.matmul(
                                    upp[:, 0:w1 - w0], ubuf[buf][:, dt, fl * 128:(fl + 1) * 128],
                                    hgT[:, dt, w0:w1], start=(dt == 0), stop=(dt == DT - 1)),
                                    waits=(wl if (dt == 0 and half == 0) else ()),
                                    inc=("T" if dt == DT - 1 else None))
                        umm = b.tt()
                        wl = [gmm, b.ss()]
                        if t_free is not None:
                            wl.append(t_free)
                        b.emit("S", lambda e, gb=gb: e.activation(
                            t_sb[:, 0:512], pA[gb][:, :], AF.Tanh, scale=0.5),
                            waits=wl, inc="S")
                        b.emit("S", lambda e, gb=gb: e.activation(
                            t_sb[:, 512:768], pB[gb][:, 0:256], AF.Tanh, scale=0.5),
                            waits=[b.ss()], inc="S")
                        tw = b.ss()
                        b.emit("V", lambda e, gb=gb: e.scalar_tensor_tensor(
                            gp_sb[:, 0:512], t_sb[:, 0:512], 1.0, pA[gb][:, :],
                            ALU.add, ALU.mult),
                            waits=[gmm, tw, b.vv()], inc="V")
                        b.emit("V", lambda e, gb=gb: e.scalar_tensor_tensor(
                            gp_sb[:, 512:768], t_sb[:, 512:768], 1.0, pB[gb][:, 0:256],
                            ALU.add, ALU.mult),
                            waits=[b.vv()], inc="V")
                        gu_war[("g", gb)] = [b.vv(), tw]
                        tw_bank[gb] = tw
                        t_free = b.vv()
                        b.emit("V", lambda e, gb=gb, fl=fl: e.tensor_mul(
                            act_sb[:, fl, 0:512], gp_sb[:, 0:512], pA[2 + gb][:, :]),
                            waits=[umm, b.vv()] + ([act_T] if (fl == 0 and act_T) else []),
                            inc="V")
                        b.emit("V", lambda e, fl=fl: e.tensor_mul(
                            act_sb[:, fl, 512:768], gp_sb[:, 512:768], pB[2][:, 0:256]),
                            waits=[b.vv()], inc="V")
                        gu_war[("u", gb)] = [b.vv()]
                        gu_war[("ub",)] = [b.vv()]
                        act_w[fl] = b.vv()
                    act_ready = act_w[nf - 1]
                    for sp in range(3):
                        for q2 in range(2):
                            stile = sp * 2 + q2
                            for dh in range(2):
                                bank = q2 * 2 + dh
                                wl = [act_ready, lw]
                                if dw_war.get(bank) is not None:
                                    wl.append(dw_war[bank])
                                wl = wl + [wload[("d", stp)]]
                                for fl in range(nf):
                                    b.emit("T", lambda e, bank=bank, fl=fl, stile=stile, dh=dh, nf=nf, buf=buf: e.matmul(
                                        pA[bank][:, :],
                                        act_sb[:, fl, stile * 128:(stile + 1) * 128],
                                        dbuf[buf][:, fl, dh * 512:(dh + 1) * 512],
                                        start=(fl == 0), stop=(fl == nf - 1)),
                                        waits=(wl if fl == 0 else ()),
                                        inc=("T" if fl == nf - 1 else None))
                                dmm = b.tt()
                                if ci == 0:
                                    b.emit("V", lambda e, bank=bank, stile=stile, dh=dh: e.tensor_copy(
                                        dwacc[:, stile, dh * 512:(dh + 1) * 512], pA[bank][:, :]),
                                        waits=[dmm, b.vv()], inc="V")
                                else:
                                    b.emit("V", lambda e, bank=bank, stile=stile, dh=dh: e.tensor_add(
                                        dwacc[:, stile, dh * 512:(dh + 1) * 512], pA[bank][:, :],
                                        dwacc[:, stile, dh * 512:(dh + 1) * 512]),
                                        waits=[dmm, b.vv()], inc="V")
                                dw_war[bank] = b.vv()
                    act_T = b.tt()
                    for _gb in (0, 1):
                        gu_war[("g", _gb)] = [dw_war[_gb], tw_bank[_gb]]
                        gu_war[("u", _gb)] = [dw_war[2 + _gb]]
                    gu_war[("ub",)] = [act_w[nf - 1]]
                    wfree[buf] = b.tt()
                    dfree[stp % 2] = b.tt()
                    if stp + 2 < 48:
                        load_chunk(stp + 2)
                        load_dchunk(stp + 2)
                dw_ready = b.vv()
                s_ffn = b.ss()

                with nc.psum_tensor(f"psc_{ei}", [128, 512], F32) as psc:
                    setA = [pA[0], pA[1], pA[2], pA[3]]
                    setB = [pB[0], pB[1], pB[2], psc]
                    stfree = {0: None, 1: None}
                    for rnd in range(8):
                        banks = setB if rnd % 2 == 0 else setA
                        if ei == 7:
                            for k2 in range(2):
                                ti = rnd * 2 + k2
                                wlx = [("dX2W", dX2W_all), ("dX2X", dX2X_all)]
                                if stfree[k2] is not None:
                                    wlx.append(stfree[k2])
                                xsem = "dX2R" if k2 == 0 else "dX2S"
                                b.emit("Y", lambda e, ti=ti, k2=k2: e.dma_start(
                                    x2t[k2][:, :], x2_s[ti * 128:(ti + 1) * 128, :]),
                                    waits=wlx, inc=xsem, k=16)
                                xld[ti] = (xsem, b.c(xsem))
                        if rnd < 2:
                            wl = gs_wl + [dw_ready, s_ffn]
                        else:
                            wl = [rflush[rnd - 2]]
                        for ct in range(CT):
                            for tg in range(4):
                                ti = rnd * 2 + tg // 2
                                dh = tg % 2
                                pp = banks[tg]
                                b.emit("T", lambda e, ct=ct, ti=ti, dh=dh, pp=pp: e.matmul(
                                    pp[:, :], gS[:, ct, ti * 128:(ti + 1) * 128],
                                    dwacc[:, ct, dh * 512:(dh + 1) * 512],
                                    start=(ct == 0), stop=(ct == CT - 1)),
                                    waits=(wl if (ct == 0 and tg == 0) else ()),
                                    inc=("T" if (ct == CT - 1 and tg == 3) else None))
                        smm = b.tt()
                        for tg in range(4):
                            ti = rnd * 2 + tg // 2
                            dh = tg % 2
                            pp = banks[tg]
                            if ei == 0:
                                b.emit("V", lambda e, ti=ti, dh=dh, pp=pp, ei=ei: e.tensor_single_scalar(
                                    acc[:, ti, dh * 512:(dh + 1) * 512], pp[:, :],
                                    w_sb[:, ti, ei:ei + 1], ALU.mult),
                                    waits=[smm, b.vv()], inc="V")
                            else:
                                b.emit("V", lambda e, ti=ti, dh=dh, pp=pp, ei=ei: e.scalar_tensor_tensor(
                                    acc[:, ti, dh * 512:(dh + 1) * 512], pp[:, :],
                                    w_sb[:, ti, ei:ei + 1],
                                    acc[:, ti, dh * 512:(dh + 1) * 512],
                                    ALU.mult, ALU.add),
                                    waits=[smm, b.vv()], inc="V")
                        rflush[rnd] = b.vv()
                        if ei == 7:
                            for k2 in range(2):
                                ti = rnd * 2 + k2
                                b.emit("V", lambda e, ti=ti, k2=k2: e.tensor_add(
                                    x2t[k2][:, :], acc[:, ti, :], x2t[k2][:, :]),
                                    waits=[xld[ti], b.vv()], inc="V")
                                ow = b.vv()
                                b.emit("Y", lambda e, ti=ti, k2=k2: e.dma_start(
                                    out_d[ti * 128:(ti + 1) * 128, :], x2t[k2][:, :]),
                                    waits=[ow], inc=("dOut" if k2 == 0 else "dOu2"), k=16)
                                stfree[k2] = (("dOut" if k2 == 0 else "dOu2"),
                                              b.c("dOut" if k2 == 0 else "dOu2"))
                for q in range(4):
                    hg_S[(0, q)] = None
                    hg_S[(1, q)] = None
                gu_war = {}
                dw_war = {}
                hg_ready = []
                act_T = None
                hTw = ("dHR", b.c("dHR"))
            b.wait_only("Y", [("dOut", b.c("dOut")), ("dOu2", b.c("dOu2"))])

        blk = E(nc.Block())
        opsd = b.ops
        blk.tensor(lambda e: [r(e) for r in opsd["T"]] and None)
        blk.vector(lambda e: [r(e) for r in opsd["V"]] and None)
        blk.scalar(lambda e: [r(e) for r in opsd["S"]] and None)
        blk.gpsimd(lambda e: [r(e) for r in opsd["G"]] and None)
        blk.sync(lambda e: [r(e) for r in opsd["Y"]] and None)
    return nc


_NC_CACHE = {}


def _enable_ldw_opt():
    try:
        from concourse.compiler_utils import get_compiler_flags, set_compiler_flags
        flags = get_compiler_flags()
        nf = [f.replace('--enable-ldw-opt=false', '--enable-ldw-opt=true') for f in flags]
        if nf != flags:
            set_compiler_flags(nf)
    except Exception:
        pass


def _host_consts(p):
    hd = 64
    j = np.arange(32)
    invf = (1.0 / THETA ** (2 * j / hd)).astype(np.float64)
    t = np.arange(SQ, dtype=np.float64)
    ang = t[None, :] * invf[:, None]
    cos = np.cos(ang)
    sin = np.sin(ang)
    pp = np.arange(128)
    jj = pp & 31
    lower = (pp & 63) < 32
    cosK = cos[jj, :]
    sinK = np.where(lower[:, None], -sin[jj, :], sin[jj, :])
    own_tok = ((2 * np.arange(OT) + p)[:, None] * 128 + np.arange(128)[None, :]).reshape(-1)
    cosQ = cosK[:, own_tok]
    sinQ = sinK[:, own_tok]

    masks = np.zeros((128, 8, 512), dtype=np.float32)
    kk = np.arange(128)
    tri = (kk[:, None] <= kk[None, :]).astype(np.float32)
    for r in range(8):
        for i in range(4):
            lim = 2 * i + p
            if r < lim:
                masks[:, r, i * 128:(i + 1) * 128] = 1.0
            elif r == lim:
                masks[:, r, i * 128:(i + 1) * 128] = tri

    permM = np.zeros((128, 128), dtype=np.float32)
    for m in range(128):
        src = m + 32 if (m & 63) < 32 else m - 32
        permM[src, m] = 1.0
    sel0 = np.zeros((128, 128), dtype=np.float32)
    sel0[0, :] = 1.0
    return (np.ascontiguousarray(cosK, dtype=np.float32), np.ascontiguousarray(sinK, dtype=np.float32),
            np.ascontiguousarray(cosQ, dtype=np.float32), np.ascontiguousarray(sinQ, dtype=np.float32),
            np.ascontiguousarray(masks, dtype=BFD), np.ascontiguousarray(permM, dtype=BFD),
            np.eye(128, dtype=BFD), np.ascontiguousarray(sel0, dtype=BFD))


def kernel(L0_attn_x, L0_attn_w_qkv, L0_attn_b_qkv, L0_attn_w_out, L0_attn_b_out,
           L0_ffn_router_w, L0_ffn_gate_proj, L0_ffn_up_proj, L0_ffn_down_proj):
    x = np.ascontiguousarray(np.asarray(L0_attn_x), dtype=np.float32)
    wqkv = np.asarray(L0_attn_w_qkv, dtype=np.float32)
    bqkv = np.asarray(L0_attn_b_qkv, dtype=np.float32)
    wout = np.asarray(L0_attn_w_out, dtype=np.float32)
    bout = np.asarray(L0_attn_b_out, dtype=np.float32)
    rw = np.ascontiguousarray(np.asarray(L0_ffn_router_w), dtype=np.float32)
    gate = np.ascontiguousarray(np.asarray(L0_ffn_gate_proj), dtype=np.float32)
    up = np.ascontiguousarray(np.asarray(L0_ffn_up_proj), dtype=np.float32)
    down = np.ascontiguousarray(np.asarray(L0_ffn_down_proj), dtype=np.float32)

    perm = np.concatenate([64 * h + np.r_[0:64:2, 1:64:2] for h in range(NHEAD)])

    def wprep(w):
        return np.ascontiguousarray(w.T.reshape(DT, 128, D).transpose(1, 0, 2))

    wqT = wprep(wqkv[0:D][perm])
    wkT = wprep(wqkv[D:2 * D][perm])
    wvT = wprep(wqkv[2 * D:3 * D])
    woT = wprep(wout)

    def fprep6(w):
        out = np.empty((8, 5, 128, DT, 512), np.float32)
        for ci in range(5):
            blk = w[:, :, ci * 512:(ci + 1) * 512]
            out[:, ci] = blk.reshape(8, DT, 128, 512).transpose(0, 2, 1, 3)
        return np.ascontiguousarray(out)

    def fprep4(w):
        blk = w[:, :, 2560:2816]
        return np.ascontiguousarray(blk.reshape(8, DT, 128, 256).transpose(0, 2, 1, 3))

    def dprep6(w):
        out = np.empty((8, 5, 128, 4, D), np.float32)
        for ci in range(5):
            blk = w[:, ci * 512:(ci + 1) * 512, :]
            out[:, ci] = blk.reshape(8, 4, 128, D).transpose(0, 2, 1, 3)
        return np.ascontiguousarray(out)

    def dprep4(w):
        blk = w[:, 2560:2816, :]
        return np.ascontiguousarray(blk.reshape(8, 2, 128, D).transpose(0, 2, 1, 3))

    iota768 = np.ascontiguousarray(np.tile(np.arange(768, dtype=np.float32), (128, 1)))
    _k = np.arange(128)
    trils = (_k[:, None] < _k[None, :]).astype(BFD)
    onesf = np.ones((128, 128), BFD)
    gate6, gate4 = fprep6(gate), fprep4(gate)
    up6, up4 = fprep6(up), fprep4(up)
    down6, down4 = dprep6(down), dprep4(down)
    rwR = np.ascontiguousarray(rw.reshape(DT, 128, 8).transpose(1, 0, 2))
    bq128 = np.ascontiguousarray(bqkv[0:D][perm].reshape(8, 128).T)
    bk128 = np.ascontiguousarray(bqkv[D:2 * D][perm].reshape(8, 128).T)
    bvbc = np.ascontiguousarray(np.broadcast_to(bqkv[2 * D:3 * D], (128, D)))
    bobc = np.ascontiguousarray(np.broadcast_to(bout, (128, D)))

    _enable_ldw_opt()
    if "nc" not in _NC_CACHE:
        _NC_CACHE["nc"] = build_program()
    nc = _NC_CACHE["nc"]

    in_maps = []
    for core in range(8):
        bb, p = core // 2, core % 2
        cosK, sinK, cosQ, sinQ, masks, permM, id128, sel0 = _host_consts(p)
        xb = np.ascontiguousarray(x[bb])
        own = 2 * np.arange(OT) + p
        xown = np.ascontiguousarray(xb.reshape(NT, 128, D)[own].reshape(2048, D))
        in_maps.append(dict(
            x=xb, xown=xown, wqT=wqT, wkT=wkT, wvT=wvT, woT=woT,
            bq128=bq128, bk128=bk128, bvbc=bvbc, bobc=bobc, rw=rwR,
            gate6=gate6, gate4=gate4, up6=up6, up4=up4, down6=down6, down4=down4,
            cosK=cosK, sinK=sinK, cosQ=cosQ, sinQ=sinQ, masks=masks,
            perm128=permM, id128=id128, sel0=sel0,
            iota768=iota768, trils=trils, onesf=onesf))

    res = run_bass_kernel_spmd(nc, in_maps, core_ids=list(range(8)))
    full = np.empty((4, SQ, D), dtype=np.float32)
    for core in range(8):
        bb, p = core // 2, core % 2
        o = np.asarray(res.results[core]["out"]).reshape(OT, 128, D)
        full[bb].reshape(NT, 128, D)[2 * np.arange(OT) + p] = o
    return full


# version 23 (final)
# speedup vs baseline: 1.0244x; 1.0086x over previous
import numpy as np
import ml_dtypes
from contextlib import ExitStack

import concourse.bass as bass
import concourse.mybir as mybir
from concourse.bass_utils import run_bass_kernel_spmd

F32 = mybir.dt.float32
BF16 = mybir.dt.bfloat16
AF = mybir.ActivationFunctionType
ALU = mybir.AluOpType
AX = mybir.AxisListType
BFD = ml_dtypes.bfloat16

D = 1024
SQ = 4096
NHEAD = 16
DT = 8
NT = 32
OT = 16
NCH = 8
NJ = 4
FCH = [(0, 4), (4, 4), (8, 4), (12, 4), (16, 4), (20, 2)]
CAP = 768
CT = 6
THETA = 10000.0

SEMS = ("T", "V", "S", "G", "dC", "dT0", "dT1", "dXa", "dXb", "dXc", "dXd", "dXQa", "dXQb", "dQW", "dQR",
        "dXR", "dX2W", "dX2X", "dHW", "dWgt", "dMA", "dMB", "dMD", "dME", "dHR", "dX2R", "dX2S", "dOut", "dOu2")


class Builder:
    def __init__(self, nc, st):
        self.nc = nc
        self.ops = {k: [] for k in "TVSGY"}
        self.cnt = {}
        self.sem = {}
        self.lastwait = {k: {} for k in "TVSGY"}
        for n in SEMS:
            self.sem[n] = st.enter_context(nc.semaphore(n))
            self.cnt[n] = 0

    def emit(self, eng, fn, waits=(), inc=None, k=1):
        lw = self.lastwait[eng]
        ws = []
        dedup = eng not in "YG"
        for s, v in waits:
            if v > 0 and (not dedup or lw.get(s, 0) < v):
                ws.append((s, v))
                if lw.get(s, 0) < v:
                    lw[s] = v
        if inc is not None:
            self.cnt[inc] += k
        sems = self.sem

        def run(e, ws=tuple(ws), fn=fn, inc=inc, k=k):
            for s, v in ws:
                e.wait_ge(sems[s], v)
            ins = fn(e)
            if inc is not None and ins is not None:
                ins.then_inc(sems[inc], k)
        self.ops[eng].append(run)

    def wait_only(self, eng, waits):
        self.emit(eng, lambda e: None, waits=waits)

    def barrier(self):
        snap = [(n, self.cnt[n]) for n in SEMS if self.cnt[n] > 0]
        for eng in "TVSGY":
            self.wait_only(eng, [(n, v) for n, v in snap if n != eng])

    def c(self, n):
        return self.cnt[n]

    def vv(self):
        return ("V", self.cnt["V"])

    def ss(self):
        return ("S", self.cnt["S"])

    def tt(self):
        return ("T", self.cnt["T"])


def build_program():
    nc = bass.Bass("TRN2", target_bir_lowering=False)

    x_d = nc.dram_tensor("x", [SQ, D], F32, kind="ExternalInput")
    xown_d = nc.dram_tensor("xown", [2048, D], F32, kind="ExternalInput")
    wqT_d = nc.dram_tensor("wqT", [128, DT, D], F32, kind="ExternalInput")
    wkT_d = nc.dram_tensor("wkT", [128, DT, D], F32, kind="ExternalInput")
    wvT_d = nc.dram_tensor("wvT", [128, DT, D], F32, kind="ExternalInput")
    woT_d = nc.dram_tensor("woT", [128, DT, D], F32, kind="ExternalInput")
    bq_d = nc.dram_tensor("bq128", [128, 8], F32, kind="ExternalInput")
    bk_d = nc.dram_tensor("bk128", [128, 8], F32, kind="ExternalInput")
    bv_d = nc.dram_tensor("bvbc", [128, D], F32, kind="ExternalInput")
    bo_d = nc.dram_tensor("bobc", [128, D], F32, kind="ExternalInput")
    rw_d = nc.dram_tensor("rw", [128, DT, 8], F32, kind="ExternalInput")
    gate6_d = nc.dram_tensor("gate6", [8, 5, 128, DT, 512], F32, kind="ExternalInput")
    gate4_d = nc.dram_tensor("gate4", [8, 128, DT, 256], F32, kind="ExternalInput")
    up6_d = nc.dram_tensor("up6", [8, 5, 128, DT, 512], F32, kind="ExternalInput")
    up4_d = nc.dram_tensor("up4", [8, 128, DT, 256], F32, kind="ExternalInput")
    down6_d = nc.dram_tensor("down6", [8, 5, 128, 4, D], F32, kind="ExternalInput")
    down4_d = nc.dram_tensor("down4", [8, 128, 2, D], F32, kind="ExternalInput")
    iota_d = nc.dram_tensor("iota768", [128, 768], F32, kind="ExternalInput")
    trils_d = nc.dram_tensor("trils", [128, 128], BF16, kind="ExternalInput")
    onesf_d = nc.dram_tensor("onesf", [128, 128], BF16, kind="ExternalInput")
    cosK_d = nc.dram_tensor("cosK", [128, SQ], F32, kind="ExternalInput")
    sinK_d = nc.dram_tensor("sinK", [128, SQ], F32, kind="ExternalInput")
    cosQ_d = nc.dram_tensor("cosQ", [128, 2048], F32, kind="ExternalInput")
    sinQ_d = nc.dram_tensor("sinQ", [128, 2048], F32, kind="ExternalInput")
    masks_d = nc.dram_tensor("masks", [128, 8, 512], BF16, kind="ExternalInput")
    perm_d = nc.dram_tensor("perm128", [128, 128], BF16, kind="ExternalInput")
    idb_d = nc.dram_tensor("id128", [128, 128], BF16, kind="ExternalInput")
    sel0_d = nc.dram_tensor("sel0", [128, 128], BF16, kind="ExternalInput")
    out_d = nc.dram_tensor("out", [2048, D], F32, kind="ExternalOutput")

    qT_s = nc.dram_tensor("qTs", [DT, 128, 2048], BF16, kind="Internal")
    hTok_s = nc.dram_tensor("hToks", [2048, D], BF16, kind="Internal")
    x2_s = nc.dram_tensor("x2s", [2048, D], F32, kind="Internal")

    st = ExitStack()
    with st:
        b = Builder(nc, st)
        E = st.enter_context

        perm_sb = E(nc.sbuf_tensor("perm_sb", [128, 128], BF16))
        idb_sb = E(nc.sbuf_tensor("idb_sb", [128, 128], BF16))
        bq_sb = E(nc.sbuf_tensor("bq_sb", [128, 8], F32))
        bk_sb = E(nc.sbuf_tensor("bk_sb", [128, 8], F32))
        w_sb = E(nc.sbuf_tensor("w_sb", [128, OT, 8], F32))
        msel = E(nc.sbuf_tensor("msel", [128, OT, 8], BF16))
        st1 = E(nc.sbuf_tensor("st1", [128, 8], F32))
        st2 = E(nc.sbuf_tensor("st2", [128, 8], F32))
        nwv = E(nc.sbuf_tensor("nwv", [128, 8], F32))
        nwy = E(nc.sbuf_tensor("nwy", [128, 8], F32))
        nwu = E(nc.sbuf_tensor("nwu", [128, 8], F32))
        nmu = E(nc.sbuf_tensor("nmu", [128, 8], F32))

        sKV = ExitStack()
        kT_sb = sKV.enter_context(nc.sbuf_tensor("kT_sb", [128, DT, SQ], BF16))
        v_sb = sKV.enter_context(nc.sbuf_tensor("v_sb", [128, NT, NHEAD * 65], BF16))

        b.emit("Y", lambda e: e.dma_start(perm_sb[:], perm_d[:, :]), inc="dC", k=16)
        b.emit("Y", lambda e: e.dma_start(idb_sb[:], idb_d[:, :]), inc="dC", k=16)
        b.emit("Y", lambda e: e.dma_start(bq_sb[:], bq_d[:, :]), inc="dC", k=16)
        b.emit("Y", lambda e: e.dma_start(bk_sb[:], bk_d[:, :]), inc="dC", k=16)
        b.emit("G", lambda e: e.memset(v_sb[:, :, 64:NHEAD * 65:65], 1.0), inc="G")
        b.emit("S", lambda e: e.preload_act_table(AF.Exp), inc="S")
        dC0 = b.c("dC")

        def ln_stats_newton(ncols, co=0, extra=()):
            sV0 = b.c("V")
            sS0 = b.c("S")
            def SL(t):
                return t[:, co:co + ncols]
            b.emit("V", lambda e: e.tensor_single_scalar(
                SL(nmu), SL(st1), 1.0 / D, ALU.mult),
                waits=[("V", sV0)] + list(extra), inc="V")
            b.emit("V", lambda e: e.tensor_mul(SL(nwu), SL(nmu), SL(nmu)),
                   waits=[b.vv()], inc="V")
            b.emit("V", lambda e: e.scalar_tensor_tensor(
                SL(nwv), SL(st2), 1.0 / D, SL(nwu),
                ALU.mult, ALU.subtract),
                waits=[("S", sS0), b.vv()], inc="V")
            b.emit("V", lambda e: e.tensor_single_scalar(
                SL(nwv), SL(nwv), 1e-5, ALU.add),
                waits=[b.vv()], inc="V")
            b.emit("V", lambda e: e.tensor_single_scalar(
                SL(nwu), SL(nwv), 1.0, ALU.add),
                waits=[b.vv()], inc="V")
            b.emit("V", lambda e: e.reciprocal(SL(nwu), SL(nwu)),
                   waits=[b.vv()], inc="V")
            b.emit("V", lambda e: e.tensor_single_scalar(
                SL(nwy), SL(nwu), 2.0, ALU.mult),
                waits=[b.vv()], inc="V")
            for _ in range(4):
                b.emit("V", lambda e: e.tensor_mul(SL(nwu), SL(nwy), SL(nwy)),
                       waits=[b.vv()], inc="V")
                b.emit("V", lambda e: e.tensor_mul(SL(nwu), SL(nwu), SL(nwv)),
                       waits=[b.vv()], inc="V")
                b.emit("V", lambda e: e.tensor_scalar(
                    SL(nwu), SL(nwu), -0.5, 1.5, ALU.mult, ALU.add),
                    waits=[b.vv()], inc="V")
                b.emit("V", lambda e: e.tensor_mul(SL(nwy), SL(nwy), SL(nwu)),
                       waits=[b.vv()], inc="V")
            return b.c("V")

        with ExitStack() as sA:
            EA = sA.enter_context
            wk_sb = EA(nc.sbuf_tensor("wk_sb", [128, DT, D], BF16))
            wv_sb = EA(nc.sbuf_tensor("wv_sb", [128, DT, D], BF16))
            cosS = EA(nc.sbuf_tensor("cosS", [128, 1, 512], F32))
            sinS = EA(nc.sbuf_tensor("sinS", [128, 1, 512], F32))
            ln1T = EA(nc.sbuf_tensor("ln1T", [128, DT, 512], BF16))
            xbuf = EA(nc.sbuf_tensor("xbuf", [128, 4, D], F32))
            lnb = EA(nc.sbuf_tensor("lnb", [128, D], BF16))
            tmp1 = EA(nc.sbuf_tensor("tmp1", [128, 512], BF16))
            tmp2 = EA(nc.sbuf_tensor("tmp2", [128, 512], BF16))
            bv_sb = EA(nc.sbuf_tensor("bv_sb", [128, D], BF16))
            psT = [EA(nc.psum_tensor(f"psT{i}", [128, 512], BF16)) for i in range(2)]
            kqv = [EA(nc.psum_tensor(f"kqv{i}", [128, 512], F32)) for i in range(2)]
            shf = EA(nc.psum_tensor("shf", [128, 512], F32))
            dump = EA(nc.psum_tensor("dump", [128, 1024], F32))

            b.emit("G", lambda e: e.dma_start(
                wk_sb[:], wkT_d[:, :, :]), inc="dWgt", k=16)
            b.emit("G", lambda e: e.dma_start(
                wv_sb[:], wvT_d[:, :, :]), inc="dWgt", k=16)
            b.emit("G", lambda e: e.dma_start(bv_sb[:], bv_d[:, :]), inc="dWgt", k=16)
            dC1 = b.c("dC")
            dW1 = b.c("dWgt")

            xfree = [None] * 4
            cfree = [None, None]
            stats_pipe = {}

            def ph1_load_stats(c):
                nonlocal dump_free
                co = 4 * (c % 2)
                for t in range(4):
                    tt = 4 * c + t
                    s8 = t
                    xsem = ("dXa", "dXb", "dXc", "dXd")[t]
                    lw = [x for x in [xfree[s8]] if x]
                    b.emit("Y", lambda e, tt=tt, s8=s8: e.dma_start(
                        xbuf[:, s8, :], x_d[tt * 128:(tt + 1) * 128, :]),
                        waits=lw, inc=xsem, k=16)
                    xw = (xsem, b.c(xsem))
                    b.emit("V", lambda e, s8=s8, co=co, t=t: e.reduce_sum(
                        st1[:, co + t:co + t + 1], xbuf[:, s8, :], axis=AX.X, negate=True),
                        waits=[xw], inc="V")
                    dw = [x for x in [dump_free] if x]
                    b.emit("S", lambda e, s8=s8, co=co, t=t: e.activation(
                        dump[:, :], xbuf[:, s8, :], AF.Square,
                        accum_out=st2[:, co + t:co + t + 1]),
                        waits=[xw] + dw, inc="S")
                    dump_free = b.ss()
                stats_pipe[c] = ln_stats_newton(4, co=co)
            lnb_free = None
            psT_free = [None, None]
            kqv_free = [None, None]
            shf_free = None
            ln1T_free = None
            dump_free = None

            ph1_load_stats(0)
            for c in range(NCH):
                cslot = 0
                tsem = "dT" + str(c % 2)
                cw = [x for x in [cfree[cslot]] if x]
                b.emit("Y", lambda e, c=c, cslot=cslot: e.dma_start(
                    cosS[:, cslot, :], cosK_d[:, c * 512:(c + 1) * 512]),
                    waits=cw, inc=tsem, k=16)
                b.emit("Y", lambda e, c=c, cslot=cslot: e.dma_start(
                    sinS[:, cslot, :], sinK_d[:, c * 512:(c + 1) * 512]),
                    inc=tsem, k=16)
                tab_w = (tsem, b.c(tsem))

                co = 4 * (c % 2)
                stats_done = stats_pipe[c]

                for t in range(4):
                    s8 = t
                    lw = [("V", stats_done)] + ([lnb_free] if lnb_free else [])
                    b.emit("V", lambda e, s8=s8, co=co, t=t: e.tensor_scalar(
                        lnb[:, :], xbuf[:, s8, :], nmu[:, co + t:co + t + 1],
                        nwy[:, co + t:co + t + 1], ALU.add, ALU.mult),
                        waits=lw, inc="V")
                    lnw = b.vv()
                    xfree[s8] = lnw
                    for half in range(2):
                        for q in range(4):
                            dt = 4 * half + q
                            wl = [lnw, ("dC", dC1)]
                            if q == 0 and psT_free[half]:
                                wl.append(psT_free[half])
                            b.emit("T", lambda e, half=half, q=q, dt=dt: e.transpose(
                                psT[half][:, q * 128:(q + 1) * 128],
                                lnb[:, dt * 128:(dt + 1) * 128], idb_sb[:]),
                                waits=wl, inc="T" if q == 3 else None)
                        trw = b.tt()
                        wl = [trw]
                        if t == 0 and half == 0 and ln1T_free:
                            wl.append(ln1T_free)
                        b.emit("V", lambda e, half=half, t=t: e.tensor_copy(
                            ln1T[:, 4 * half:4 * half + 4, t * 128:(t + 1) * 128],
                            psT[half][:, :].rearrange("p (a q) -> p a q", a=4)),
                            waits=wl, inc="V")
                        psT_free[half] = b.vv()
                    lnb_free = b.vv()
                lnT_ready = b.vv()
                if c + 1 < NCH:
                    ph1_load_stats(c + 1)

                for ft in range(8):
                    bank = ft % 2
                    wl = [lnT_ready, ("dWgt", dW1)]
                    if kqv_free[bank]:
                        wl.append(kqv_free[bank])
                    for dt in range(DT):
                        b.emit("T", lambda e, bank=bank, ft=ft, dt=dt: e.matmul(
                            kqv[bank][:, :], wk_sb[:, dt, ft * 128:(ft + 1) * 128],
                            ln1T[:, dt, :], start=(dt == 0), stop=(dt == DT - 1)),
                            waits=(wl if dt == 0 else ()), inc=("T" if dt == DT - 1 else None))
                    mmw = b.tt()
                    b.emit("S", lambda e, bank=bank, ft=ft, c=c: e.activation(
                        kT_sb[:, ft, c * 512:(c + 1) * 512], kqv[bank][:, :], AF.Identity,
                        bias=bk_sb[:, ft:ft + 1]),
                        waits=[mmw, ("dC", dC1)], inc="S")
                    kqv_free[bank] = b.ss()
                kevac = b.c("S")

                for t in range(4):
                    tt = 4 * c + t
                    for nh in range(2):
                        bank = (t * 2 + nh) % 2
                        wl = [lnT_ready]
                        if kqv_free[bank]:
                            wl.append(kqv_free[bank])
                        for dt in range(DT):
                            b.emit("T", lambda e, bank=bank, t=t, nh=nh, dt=dt: e.matmul(
                                kqv[bank][:, :], ln1T[:, dt, t * 128:(t + 1) * 128],
                                wv_sb[:, dt, nh * 512:(nh + 1) * 512],
                                start=(dt == 0), stop=(dt == DT - 1)),
                                waits=(wl if dt == 0 else ()), inc=("T" if dt == DT - 1 else None))
                        mmw = b.tt()
                        b.emit("V", lambda e, bank=bank, tt=tt, nh=nh: e.tensor_add(
                            v_sb[:, tt, nh * 520:(nh + 1) * 520]
                            .rearrange("p (h q) -> p h q", h=8)[:, :, 0:64],
                            kqv[bank][:, :].rearrange("p (h q) -> p h q", h=8),
                            bv_sb[:, nh * 512:(nh + 1) * 512]
                            .rearrange("p (h q) -> p h q", h=8)),
                            waits=[mmw, ("dC", dC1), ("G", 1)], inc="V")
                        kqv_free[bank] = b.vv()

                for ft in range(8):
                    wl = [("S", kevac), ("dC", dC0)]
                    if shf_free:
                        wl.append(shf_free)
                    b.emit("T", lambda e, ft=ft, c=c: e.matmul(
                        shf[:, :], perm_sb[:, :], kT_sb[:, ft, c * 512:(c + 1) * 512],
                        start=True, stop=True),
                        waits=wl, inc="T")
                    shw = b.tt()
                    b.emit("V", lambda e, ft=ft, c=c, cslot=cslot: e.tensor_mul(
                        tmp1[:, :], kT_sb[:, ft, c * 512:(c + 1) * 512], cosS[:, cslot, :]),
                        waits=[("S", kevac), tab_w, b.vv()], inc="V")
                    b.emit("V", lambda e, cslot=cslot: e.tensor_mul(
                        tmp2[:, :], shf[:, :], sinS[:, cslot, :]),
                        waits=[shw, b.vv()], inc="V")
                    shf_free = b.vv()
                    b.emit("V", lambda e, ft=ft, c=c: e.tensor_add(
                        kT_sb[:, ft, c * 512:(c + 1) * 512], tmp1[:, :], tmp2[:, :]),
                        waits=[b.vv()], inc="V")
                cfree[cslot] = b.vv()
                ln1T_free = b.tt()

        b.barrier()

        with ExitStack() as sB:
            EB = sB.enter_context
            wq_sb = EB(nc.sbuf_tensor("wq_sb", [128, DT, D], BF16))
            cosQs = EB(nc.sbuf_tensor("cosQs", [128, 2048], F32))
            sinQs = EB(nc.sbuf_tensor("sinQs", [128, 2048], F32))
            lnq = EB(nc.sbuf_tensor("lnq", [128, DT, 256], BF16))
            xq = EB(nc.sbuf_tensor("xq", [128, 2, D], F32))
            lnb2 = EB(nc.sbuf_tensor("lnb2", [128, D], BF16))
            qTq = EB(nc.sbuf_tensor("qTq", [128, DT, 256], BF16))
            tq1 = EB(nc.sbuf_tensor("tq1", [128, 256], F32))
            tq2 = EB(nc.sbuf_tensor("tq2", [128, 256], F32))
            psT2 = [EB(nc.psum_tensor(f"psTa{i}", [128, 512], BF16)) for i in range(2)]
            qps = [EB(nc.psum_tensor(f"qps{i}", [128, 256], F32)) for i in range(2)]
            shq = EB(nc.psum_tensor("shq", [128, 256], F32))
            dump2 = EB(nc.psum_tensor("dump2", [128, 1024], F32))

            b.emit("Y", lambda e: e.dma_start(cosQs[:], cosQ_d[:, :]), inc="dC", k=16)
            b.emit("Y", lambda e: e.dma_start(sinQs[:], sinQ_d[:, :]), inc="dC", k=16)
            b.emit("G", lambda e: e.dma_start(
                wq_sb[:], wqT_d[:, :, :]), inc="dWgt", k=16)
            dC2 = b.c("dC")
            dW2 = b.c("dWgt")

            xqfree = [None, None]
            lnb2_free = None
            psT2_free = [None, None]
            qps_free = [None, None]
            shq_free = None
            lnq_free = None
            qTq_free = None
            dump2_free = None

            def ph2_front(pr):
                nonlocal dump2_free
                for t in range(2):
                    slot = 2 * pr + t
                    xsem = ("dXQa", "dXQb")[t]
                    lw = [x for x in [xqfree[t]] if x]
                    b.emit("Y", lambda e, slot=slot, t=t: e.dma_start(
                        xq[:, t, :], xown_d[slot * 128:(slot + 1) * 128, :]),
                        waits=lw, inc=xsem, k=16)
                    xw = (xsem, b.c(xsem))
                    b.emit("V", lambda e, t=t: e.reduce_sum(
                        st1[:, t:t + 1], xq[:, t, :], axis=AX.X, negate=True),
                        waits=[xw], inc="V")
                    dw = [x for x in [dump2_free] if x]
                    b.emit("S", lambda e, t=t: e.activation(
                        dump2[:, :], xq[:, t, :], AF.Square, accum_out=st2[:, t:t + 1]),
                        waits=[xw] + dw, inc="S")
                    dump2_free = b.ss()
                return ln_stats_newton(2)

            stats_next = ph2_front(0)
            for pr in range(8):
                stats_done = stats_next

                for t in range(2):
                    lw = [("V", stats_done)] + ([lnb2_free] if lnb2_free else [])
                    b.emit("V", lambda e, t=t: e.tensor_scalar(
                        lnb2[:, :], xq[:, t, :], nmu[:, t:t + 1], nwy[:, t:t + 1],
                        ALU.add, ALU.mult),
                        waits=lw, inc="V")
                    lnw = b.vv()
                    xqfree[t] = lnw
                    for half in range(2):
                        for q in range(4):
                            dt = 4 * half + q
                            wl = [lnw]
                            if q == 0 and psT2_free[half]:
                                wl.append(psT2_free[half])
                            wl = wl + [("dC", dC2)]
                            b.emit("T", lambda e, half=half, q=q, dt=dt: e.transpose(
                                psT2[half][:, q * 128:(q + 1) * 128],
                                lnb2[:, dt * 128:(dt + 1) * 128], idb_sb[:]),
                                waits=wl, inc="T" if q == 3 else None)
                        trw = b.tt()
                        wl = [trw]
                        if t == 0 and half == 0 and lnq_free:
                            wl.append(lnq_free)
                        b.emit("V", lambda e, half=half, t=t: e.tensor_copy(
                            lnq[:, 4 * half:4 * half + 4, t * 128:(t + 1) * 128],
                            psT2[half][:, :].rearrange("p (a q) -> p a q", a=4)),
                            waits=wl, inc="V")
                        psT2_free[half] = b.vv()
                    lnb2_free = b.vv()
                lnq_ready = b.vv()
                if pr + 1 < 8:
                    stats_next = ph2_front(pr + 1)

                for ft in range(8):
                    bank = ft % 2
                    wl = [lnq_ready, ("dWgt", dW2)]
                    if qps_free[bank]:
                        wl.append(qps_free[bank])
                    if ft < 2 and qTq_free:
                        wl.append(qTq_free)
                    for dt in range(DT):
                        b.emit("T", lambda e, bank=bank, ft=ft, dt=dt: e.matmul(
                            qps[bank][:, :], wq_sb[:, dt, ft * 128:(ft + 1) * 128],
                            lnq[:, dt, :], start=(dt == 0), stop=(dt == DT - 1)),
                            waits=(wl if dt == 0 else ()), inc=("T" if dt == DT - 1 else None))
                    mmw = b.tt()
                    b.emit("S", lambda e, bank=bank, ft=ft: e.activation(
                        qTq[:, ft, :], qps[bank][:, :], AF.Identity, bias=bq_sb[:, ft:ft + 1]),
                        waits=[mmw, ("dC", dC2)] + ([qTq_free] if qTq_free else []), inc="S")
                    qps_free[bank] = b.ss()
                qevac = b.c("S")
                lnq_free = b.tt()

                for ft in range(8):
                    wl = [("S", qevac)] + ([shq_free] if shq_free else [])
                    b.emit("T", lambda e, ft=ft: e.matmul(
                        shq[:, :], perm_sb[:, :], qTq[:, ft, :], start=True, stop=True),
                        waits=wl + [("dC", dC2)], inc="T")
                    shw = b.tt()
                    b.emit("V", lambda e, ft=ft, pr=pr: e.tensor_mul(
                        tq1[:, :], qTq[:, ft, :], cosQs[:, pr * 256:(pr + 1) * 256]),
                        waits=[("S", qevac), ("dC", dC2), b.vv()], inc="V")
                    b.emit("V", lambda e, pr=pr: e.tensor_mul(
                        tq2[:, :], shq[:, :], sinQs[:, pr * 256:(pr + 1) * 256]),
                        waits=[shw, b.vv()], inc="V")
                    shq_free = b.vv()
                    b.emit("V", lambda e, ft=ft: e.tensor_add(
                        qTq[:, ft, :], tq1[:, :], tq2[:, :]),
                        waits=[b.vv()], inc="V")
                ropeq = b.vv()
                b.emit("Y", lambda e, pr=pr: e.dma_start(
                    qT_s[:, :, pr * 256:(pr + 1) * 256].rearrange("dt p q -> p dt q"),
                    qTq[:, :, :]),
                    waits=[ropeq], inc="dQW", k=16)
                qTq_free = ("dQW", b.c("dQW"))

        b.barrier()
        dQW_all = b.c("dQW")

        with ExitStack() as sC:
            EC = sC.enter_context
            wo_sb = EC(nc.sbuf_tensor("wo_sb", [128, DT, D], BF16))
            qTc = EC(nc.sbuf_tensor("qTc", [128, DT, 512], BF16))
            masks_sb = EC(nc.sbuf_tensor("masks_sb", [128, 8, 512], BF16))
            p_sb = EC(nc.sbuf_tensor("p_sb", [128, 8, 512], BF16))
            aoT = EC(nc.sbuf_tensor("aoT", [128, DT, 512], BF16))
            rcs = EC(nc.sbuf_tensor("rcs", [128, 512], BF16))
            bc_sb = EC(nc.sbuf_tensor("bc_sb", [128, 512], BF16))
            sel_sb = EC(nc.sbuf_tensor("sel_sb", [128, 128], BF16))
            xres = EC(nc.sbuf_tensor("xres", [128, D], F32))
            x2_sb = EC(nc.sbuf_tensor("x2_sb", [128, 2, D], F32))
            ln2b4 = EC(nc.sbuf_tensor("ln2b4", [128, 4, D], BF16))
            hTt = EC(nc.sbuf_tensor("hTt", [128, DT, 128], BF16))
            bo_sb = EC(nc.sbuf_tensor("bo_sb", [128, D], F32))
            rw_sb = EC(nc.sbuf_tensor("rw_sb", [128, DT, 8], BF16))
            pr_sb = EC(nc.sbuf_tensor("pr_sb", [128, 8], F32))
            p2_sb = EC(nc.sbuf_tensor("p2_sb", [128, 8], F32))
            mk1 = EC(nc.sbuf_tensor("mk1", [128, 8], F32))
            mk2 = EC(nc.sbuf_tensor("mk2", [128, 8], F32))
            rr1 = EC(nc.sbuf_tensor("rr1", [128, 1], F32))
            rr2 = EC(nc.sbuf_tensor("rr2", [128, 1], F32))
            rr3 = EC(nc.sbuf_tensor("rr3", [128, 1], F32))
            sps = [EC(nc.psum_tensor(f"sps{i}", [128, 1024], F32)) for i in range(2)]
            aops = [EC(nc.psum_tensor(f"aops{i}", [128, 512], F32)) for i in range(2)]
            bcps = EC(nc.psum_tensor("bcps", [128, 512], F32))
            psT3 = EC(nc.psum_tensor("psT3", [128, 512], BF16))
            ops_ = [sps[0], sps[1]]

            b.emit("Y", lambda e: e.dma_start(masks_sb[:], masks_d[:, :, :]), inc="dC", k=16)
            b.emit("Y", lambda e: e.dma_start(sel_sb[:], sel0_d[:, :]), inc="dC", k=16)
            b.emit("Y", lambda e: e.dma_start(bo_sb[:], bo_d[:, :]), inc="dC", k=16)
            b.emit("G", lambda e: e.dma_start(
                wo_sb[:], woT_d[:, :, :]), inc="dWgt", k=16)
            b.emit("G", lambda e: e.dma_start(
                rw_sb[:], rw_d[:, :, :]), inc="dWgt", k=16)
            b.emit("G", lambda e: e.memset(rcs[:, :], 0.0), inc="G")
            dC3 = b.c("dC")
            dW3 = b.c("dWgt")
            sG3 = b.c("G")

            exp_war = {}
            av_war = {}
            sps_war = {}
            lnfreeY = None
            lnwarT = [None] * 4
            ao_free = [None, None]
            aoT_reuse = None
            rcs_free = None
            ops_free = [None, None]
            x2free = {0: None, 1: None}
            hTt_free = None
            psT3_free = None
            qtc_war = None
            sq_s = None

            for j in range(NJ):
                nk = 8 * j + 8
                bc_war_v = b.vv()
                bc_war_s = b.ss()
                b.emit("Y", lambda e, j=j: e.dma_start(
                    qTc[:, :, :],
                    qT_s[:, :, j * 512:(j + 1) * 512].rearrange("dt p q -> p dt q")),
                    waits=[("dQW", dQW_all), qtc_war if qtc_war else b.tt()],
                    inc="dQR", k=16)
                qw = ("dQR", b.c("dQR"))

                epi_pending = None
                for h in range(NHEAD):
                    po = (h % 2) * 64
                    fth = h // 2
                    ab = h % 2
                    LEAD = 4 if j == 0 else 3
                    npr = nk // 2
                    pready = {}
                    for pi in range(npr + LEAD):
                        if pi == 2 and epi_pending is not None:
                            epi_pending()
                            epi_pending = None
                        if pi < npr:
                            kp = pi
                            pb = kp % 2
                            ps4 = kp % 4
                            wl = [qw, ("dC", dC3)]
                            if exp_war.get(pb) is not None:
                                wl.append(("S", exp_war[pb]))
                            if sps_war.get(pb) is not None:
                                wl.append(sps_war[pb])
                                sps_war[pb] = None
                            for kk2 in range(2):
                                k = 2 * kp + kk2
                                b.emit("T", lambda e, k=k, pb=pb, kk2=kk2, po=po, fth=fth: e.matmul(
                                    sps[pb][:, kk2 * 512:(kk2 + 1) * 512],
                                    kT_sb[po:po + 64, fth, k * 128:(k + 1) * 128],
                                    qTc[po:po + 64, fth, :], start=True, stop=True),
                                    waits=(wl if kk2 == 0 else ()), inc=("T" if kk2 == 1 else None))
                            scw = b.tt()
                            wl = [scw]
                            if av_war.get(ps4) is not None:
                                wl.append(("T", av_war[ps4]))
                            b.emit("S", lambda e, pb=pb, ps4=ps4: e.activation(
                                p_sb[:, 2 * ps4:2 * ps4 + 2, :], sps[pb][:, :],
                                AF.Exp, scale=0.125),
                                waits=wl, inc="S")
                            exp_war[pb] = b.c("S")
                            pw = b.ss()
                            if kp >= 4 * j:
                                r0 = 2 * (kp - 4 * j)
                                b.emit("V", lambda e, ps4=ps4, r0=r0: e.tensor_mul(
                                    p_sb[:, 2 * ps4:2 * ps4 + 2, :],
                                    p_sb[:, 2 * ps4:2 * ps4 + 2, :],
                                    masks_sb[:, r0:r0 + 2, :]),
                                    waits=[pw], inc="V")
                                pw = b.vv()
                            pready[kp] = pw
                        if pi >= LEAD:
                            kp2 = pi - LEAD
                            ps4 = kp2 % 4
                            wl = [pready.pop(kp2), ("G", 1)]
                            if kp2 == 0 and ao_free[ab] is not None:
                                wl.append(ao_free[ab])
                            for kk2 in range(2):
                                k = 2 * kp2 + kk2
                                b.emit("T", lambda e, k=k, h=h, ps4=ps4, kk2=kk2, ab=ab, nk=nk: e.matmul(
                                    aops[ab][0:65, :], v_sb[:, k, h * 65:(h + 1) * 65],
                                    p_sb[:, 2 * ps4 + kk2, :], start=(k == 0), stop=(k == nk - 1)),
                                    waits=(wl if kk2 == 0 else ()), inc=("T" if kk2 == 1 else None))
                            av_war[ps4] = b.c("T")
                    av_last = b.tt()

                    def _epi(av_last=av_last, ab=ab, po=po, fth=fth):
                        nonlocal rcs_free
                        wl = [av_last, ("G", sG3)] + ([rcs_free] if rcs_free else [])
                        def _recip(e, ab=ab):
                            with nc.allow_low_precision(reason="softmax denominator reciprocal cast to bf16 for the selector broadcast matmul"):
                                return e.reciprocal(rcs[0:1, :], aops[ab][64:65, :])
                        b.emit("V", _recip, waits=wl, inc="V")
                        rw_ = b.vv()
                        b.emit("T", lambda e: e.matmul(
                            bcps[:, :], sel_sb[:, :], rcs[:, :], start=True, stop=True),
                            waits=[rw_, ("dC", dC3), bc_war_v, bc_war_s, b.ss()], inc="T")
                        bcw = b.tt()
                        rcs_free = bcw
                        b.emit("S", lambda e: e.activation(
                            bc_sb[0:64, :], bcps[0:64, :], AF.Identity),
                            waits=[bcw, b.ss()], inc="S")
                        bcev = b.ss()
                        wl = [bcev] + ([aoT_reuse] if aoT_reuse else [])
                        b.emit("V", lambda e, ab=ab, po=po, fth=fth: e.tensor_mul(
                            aoT[po:po + 64, fth, :], aops[ab][0:64, :], bc_sb[0:64, :]),
                            waits=wl, inc="V")
                        ao_free[ab] = b.vv()

                    if h == NHEAD - 1:
                        _epi()
                    else:
                        epi_pending = _epi
                aoT_ready = b.vv()
                qtc_war = b.tt()

                lnws = []
                for i in range(4):
                    slot = 4 * j + i
                    xp = i % 2
                    b.emit("Y", lambda e, slot=slot: e.dma_start(
                        xres[:, :], xown_d[slot * 128:(slot + 1) * 128, :]),
                        waits=[b.vv()] + ([sq_s] if sq_s else []), inc="dXR", k=16)
                    xw = ("dXR", b.c("dXR"))
                    wl = [xw, ("dC", dC3), b.vv()]
                    if x2free[xp] is not None:
                        wl.append(x2free[xp])
                    b.emit("V", lambda e, xp=xp: e.tensor_add(
                        x2_sb[:, xp, :], xres[:, :], bo_sb[:, :]),
                        waits=wl, inc="V")
                    xbw = b.vv()
                    for oc in range(2):
                        wl = [aoT_ready, ("dWgt", dW3)]
                        if ops_free[oc]:
                            wl.append(ops_free[oc])
                        if exp_war.get(oc) is not None:
                            wl.append(("S", exp_war[oc]))
                        for dt in range(DT):
                            b.emit("T", lambda e, oc=oc, i=i, dt=dt: e.matmul(
                                ops_[oc][:, 0:512], aoT[:, dt, i * 128:(i + 1) * 128],
                                wo_sb[:, dt, oc * 512:(oc + 1) * 512],
                                start=(dt == 0), stop=(dt == DT - 1)),
                                waits=(wl if dt == 0 else ()), inc=("T" if dt == DT - 1 else None))
                        mmw = b.tt()
                        b.emit("V", lambda e, oc=oc, xp=xp: e.tensor_add(
                            x2_sb[:, xp, oc * 512:(oc + 1) * 512], ops_[oc][:, 0:512],
                            x2_sb[:, xp, oc * 512:(oc + 1) * 512]),
                            waits=[mmw, xbw], inc="V")
                        ops_free[oc] = b.vv()
                    x2w = b.vv()
                    b.emit("V", lambda e, i=i, xp=xp: e.reduce_sum(
                        st1[:, i:i + 1], x2_sb[:, xp, :], axis=AX.X, negate=True),
                        waits=[x2w], inc="V")
                    b.emit("S", lambda e, i=i, xp=xp: e.activation(
                        xres[:, :], x2_sb[:, xp, :], AF.Square, accum_out=st2[:, i:i + 1]),
                        waits=[x2w, b.ss()], inc="S")
                    sq_s = b.ss()
                    stats_done = ln_stats_newton(1, co=i)
                    wl = [("V", stats_done)]
                    if lnwarT[i] is not None:
                        wl.append(lnwarT[i])
                    if lnfreeY is not None:
                        wl.append(lnfreeY)
                    b.emit("V", lambda e, i=i, xp=xp: e.tensor_scalar(
                        ln2b4[:, i, :], x2_sb[:, xp, :], nmu[:, i:i + 1], nwy[:, i:i + 1],
                        ALU.add, ALU.mult),
                        waits=wl, inc="V")
                    lnw = b.vv()
                    lnws.append(lnw)
                    b.emit("Y", lambda e, slot=slot, i=i: e.dma_start(
                        hTok_s[slot * 128:(slot + 1) * 128, :], ln2b4[:, i, :]),
                        waits=[lnw], inc="dHW", k=16)
                    xsm = "dX2W" if xp == 0 else "dX2X"
                    b.emit("Y", lambda e, slot=slot, xp=xp: e.dma_start(
                        x2_s[slot * 128:(slot + 1) * 128, :], x2_sb[:, xp, :]),
                        waits=[lnw], inc=xsm, k=16)
                    x2free[xp] = (xsm, b.c(xsm))
                lnfreeY = ("dHW", b.c("dHW"))
                aoT_reuse = b.tt()
                sps_war[0] = ops_free[0]
                sps_war[1] = ops_free[1]

                for i in range(4):
                    slot = 4 * j + i
                    for half in range(2):
                        for q in range(4):
                            dt = 4 * half + q
                            wl = [lnws[i]]
                            if q == 0 and psT3_free:
                                wl.append(psT3_free)
                            wl = wl + [("dC", dC3)]
                            b.emit("T", lambda e, q=q, dt=dt, i=i: e.transpose(
                                psT3[:, q * 128:(q + 1) * 128],
                                ln2b4[:, i, dt * 128:(dt + 1) * 128], idb_sb[:]),
                                waits=wl, inc="T" if q == 3 else None)
                        trw = b.tt()
                        wl = [trw] + (list(hTt_free) if (half == 0 and hTt_free) else [])
                        b.emit("V", lambda e, half=half: e.tensor_copy(
                            hTt[:, 4 * half:4 * half + 4, :],
                            psT3[:, :].rearrange("p (a q) -> p a q", a=4)),
                            waits=wl, inc="V")
                        psT3_free = b.vv()
                    lnwarT[i] = b.tt()
                    httw = b.vv()
                    for dt in range(DT):
                        b.emit("T", lambda e, dt=dt: e.matmul(
                            bcps[:, 0:8], hTt[:, dt, :], rw_sb[:, dt, :],
                            start=(dt == 0), stop=(dt == DT - 1)),
                            waits=([httw, ("dWgt", dW3), b.vv(), b.ss()] if dt == 0 else ()),
                            inc=("T" if dt == DT - 1 else None))
                    rmm = b.tt()
                    hTt_free = [rmm]
                    b.emit("V", lambda e: e.reduce_max(
                        rr1[:, 0:1], bcps[:, 0:8], axis=AX.X, negate=True),
                        waits=[rmm], inc="V")
                    b.emit("S", lambda e: e.activation(
                        pr_sb[:, :], bcps[:, 0:8], AF.Exp, bias=rr1[:, 0:1]),
                        waits=[b.vv(), b.ss()], inc="S")
                    spr = b.ss()
                    b.emit("V", lambda e: e.reduce_max(rr1[:, 0:1], pr_sb[:, :], axis=AX.X),
                           waits=[spr, b.vv()], inc="V")
                    b.emit("V", lambda e: e.tensor_single_scalar(
                        mk1[:, :], pr_sb[:, :], rr1[:, 0:1], ALU.is_ge),
                        waits=[b.vv()], inc="V")
                    b.emit("V", lambda e: e.scalar_tensor_tensor(
                        p2_sb[:, :], mk1[:, :], -4.0, pr_sb[:, :], ALU.mult, ALU.add),
                        waits=[b.vv()], inc="V")
                    b.emit("V", lambda e: e.reduce_max(rr2[:, 0:1], p2_sb[:, :], axis=AX.X),
                           waits=[b.vv()], inc="V")
                    b.emit("V", lambda e: e.tensor_single_scalar(
                        mk2[:, :], p2_sb[:, :], rr2[:, 0:1], ALU.is_ge),
                        waits=[b.vv()], inc="V")
                    b.emit("V", lambda e: e.tensor_add(rr3[:, 0:1], rr1[:, 0:1], rr2[:, 0:1]),
                           waits=[b.vv()], inc="V")
                    b.emit("V", lambda e: e.tensor_single_scalar(
                        rr3[:, 0:1], rr3[:, 0:1], 2.0, ALU.mult),
                        waits=[b.vv()], inc="V")
                    b.emit("V", lambda e: e.reciprocal(rr3[:, 0:1], rr3[:, 0:1]),
                           waits=[b.vv()], inc="V")
                    b.emit("V", lambda e, slot=slot: e.tensor_add(
                        msel[:, slot, :], mk1[:, :], mk2[:, :]),
                        waits=[b.vv()], inc="V")
                    b.emit("V", lambda e, slot=slot: e.tensor_mul(
                        mk1[:, :], msel[:, slot, :], pr_sb[:, :]),
                        waits=[b.vv()], inc="V")
                    b.emit("V", lambda e, slot=slot: e.tensor_single_scalar(
                        w_sb[:, slot, :], mk1[:, :], rr3[:, 0:1], ALU.mult),
                        waits=[b.vv()], inc="V")
            dX2W_all = b.c("dX2W")
            dX2X_all = b.c("dX2X")
            dHW_all = b.c("dHW")
            router_done = b.c("V")

        sKV.close()
        b.barrier()

        with ExitStack() as sD:
            ED = sD.enter_context
            hTok = ED(nc.sbuf_tensor("hTok", [128, OT, D], BF16))
            acc = ED(nc.sbuf_tensor("acc", [128, OT, D], BF16))
            gbuf = [ED(nc.sbuf_tensor(f"gbuf{i}", [128, DT, 512], BF16)) for i in range(2)]
            ubuf = [ED(nc.sbuf_tensor(f"ubuf{i}", [128, DT, 512], BF16)) for i in range(2)]
            dbuf = [ED(nc.sbuf_tensor(f"dbuf{i}", [128, 4, D], BF16)) for i in range(2)]
            gT = ED(nc.sbuf_tensor("gT", [128, OT, CAP], BF16))
            gS = ED(nc.sbuf_tensor("gS", [128, CT, 2048], BF16))
            hgT = ED(nc.sbuf_tensor("hgT", [128, DT, CAP], BF16))
            act_sb = ED(nc.sbuf_tensor("act_sb", [128, 4, CAP], BF16))
            dwacc = ED(nc.sbuf_tensor("dwacc", [128, CT, D], BF16))
            t_sb = ED(nc.sbuf_tensor("t_sb", [128, CAP], BF16))
            gp_sb = ED(nc.sbuf_tensor("gp_sb", [128, CAP], BF16))
            x2t = [ED(nc.sbuf_tensor(f"x2t{i}", [128, D], F32)) for i in range(2)]
            iota_sb = ED(nc.sbuf_tensor("iota_sb", [128, CAP], F32))
            trils_sb = ED(nc.sbuf_tensor("trils_sb", [128, 128], BF16))
            onesf_sb = ED(nc.sbuf_tensor("onesf_sb", [128, 128], BF16))
            rank_sb = ED(nc.sbuf_tensor("rank_sb", [128, OT, 8], F32))
            pA = [ED(nc.psum_tensor(f"pA{i}", [128, 512], F32)) for i in range(4)]
            pB = [ED(nc.psum_tensor(f"pB{i}", [128, 512], F32)) for i in range(3)]

            for nm, dd in ((iota_sb, iota_d),
                           (trils_sb, trils_d), (onesf_sb, onesf_d)):
                b.emit("Y", lambda e, nm=nm, dd=dd: e.dma_start(nm[:], dd[:]), inc="dC", k=16)
            b.emit("Y", lambda e: e.dma_start(
                hTok[:, :, :], hTok_s[:, :].rearrange("(t p) d -> p t d", p=128)),
                waits=[("dHW", dHW_all)], inc="dHR", k=16)
            dC4 = b.c("dC")
            hTw = ("dHR", b.c("dHR"))
            sG4 = b.c("G")

            rk_done = []
            for i in range(OT):
                bank = pA[i % 2]
                wl = [("V", router_done), ("dC", dC4)]
                if i >= 2:
                    wl.append(rk_done[i - 2])
                for jj in range(i + 1):
                    b.emit("T", lambda e, i=i, jj=jj, bank=bank: e.matmul(
                        bank[:, 0:8], trils_sb[:, :] if jj == i else onesf_sb[:, :],
                        msel[:, jj, :], start=(jj == 0), stop=(jj == i)),
                        waits=(wl if jj == 0 else ()), inc=("T" if jj == i else None))
                mmw = b.tt()
                b.emit("V", lambda e, i=i, bank=bank: e.scalar_tensor_tensor(
                    rank_sb[:, i, :], msel[:, i, :], 4096.0, bank[:, 0:8],
                    ALU.mult, ALU.add),
                    waits=[mmw, b.vv()], inc="V")
                b.emit("V", lambda e, i=i: e.tensor_single_scalar(
                    rank_sb[:, i, :], rank_sb[:, i, :], -4096.0, ALU.add),
                    waits=[b.vv()], inc="V")
                rk_done.append(b.vv())
            rank_ready = b.vv()


            wsem = ["dMA", "dMB"]
            wload = {}
            wfree = {0: None, 1: None}

            def load_chunk(stp):
                ei, ci = stp // 6, stp % 6
                f0, nf = FCH[ci]
                buf = stp % 2
                sname = wsem[buf]
                wl = [x for x in [wfree[buf]] if x]
                if ci < 5:
                    gsrc = lambda e, ei=ei, ci=ci: gate6_d[ei, ci, :, :, :]
                    usrc = lambda e, ei=ei, ci=ci: up6_d[ei, ci, :, :, :]
                    dsrc = lambda e, ei=ei, ci=ci: down6_d[ei, ci, :, :, :]
                else:
                    gsrc = lambda e, ei=ei: gate4_d[ei, :, :, :]
                    usrc = lambda e, ei=ei: up4_d[ei, :, :, :]
                    dsrc = lambda e, ei=ei: down4_d[ei, :, :, :]
                b.emit("G", lambda e, nf=nf, buf=buf, gsrc=gsrc: e.dma_start(
                    gbuf[buf][:, :, 0:nf * 128], gsrc(e)), waits=wl, inc=sname, k=16)
                b.emit("G", lambda e, nf=nf, buf=buf, usrc=usrc: e.dma_start(
                    ubuf[buf][:, :, 0:nf * 128], usrc(e)), inc=sname, k=16)
                wload[stp] = (sname, b.c(sname))

            dfree = {0: None, 1: None}

            def load_dchunk(stp):
                ei, ci = stp // 6, stp % 6
                f0, nf = FCH[ci]
                dbf = stp % 2
                if ci < 5:
                    dsrc = lambda e, ei=ei, ci=ci: down6_d[ei, ci, :, :, :]
                else:
                    dsrc = lambda e, ei=ei: down4_d[ei, :, :, :]
                dsem = "dMD" if dbf == 0 else "dME"
                wl = [x for x in [dfree[dbf]] if x]
                b.emit("G", lambda e, nf=nf, dbf=dbf, dsrc=dsrc: e.dma_start(
                    dbuf[dbf][:, 0:nf, :], dsrc(e)), waits=wl, inc=dsem, k=16)
                wload[("d", stp)] = (dsem, b.c(dsem))

            load_chunk(0)
            load_chunk(1)
            load_dchunk(0)
            load_dchunk(1)

            gather_T = None
            hg_S = {}
            gu_war = {}
            dw_war = {}
            act_T = None
            t_free = None
            rflush = {}
            s_ffn = None
            xld = {}

            gT_pipe = {}

            def build_gT(ei):
                wl = [rank_ready, ("dC", dC4)]
                if gather_T is not None:
                    wl.append(gather_T)
                for ti in range(OT):
                    b.emit("V", lambda e, ti=ti, ei=ei: e.tensor_single_scalar(
                        gT[:, ti, :], iota_sb[:, :], rank_sb[:, ti, ei:ei + 1],
                        ALU.is_equal),
                        waits=wl + [b.vv()], inc="V")
                gT_pipe[ei] = b.vv()

            build_gT(0)
            for ei in range(8):
                gT_ready = gT_pipe[ei]

                with nc.psum_tensor(f"psT4_{ei}", [128, 1024], BF16) as psT4:
                    psT4_free = None
                    gidx = 0
                    for ct in range(CT):
                        for tp in range(2):
                            if gidx == 0:
                                if ei == 0:
                                    wl2 = [gT_ready, b.vv(), b.ss(), ("dC", dC4)]
                                else:
                                    wl2 = [gT_ready, rflush[6], s_ffn, ("dC", dC4)]
                            else:
                                wl2 = [gT_ready, psT4_free]
                            for q8 in range(8):
                                ti = tp * 8 + q8
                                b.emit("T", lambda e, ti=ti, ct=ct, q8=q8, psT4=psT4: e.transpose(
                                    psT4[:, q8 * 128:(q8 + 1) * 128],
                                    gT[:, ti, ct * 128:(ct + 1) * 128], idb_sb[:]),
                                    waits=(wl2 if q8 == 0 else ()),
                                    inc=("T" if q8 == 7 else None))
                            trw = b.tt()
                            b.emit("V", lambda e, ct=ct, tp=tp, psT4=psT4: e.tensor_copy(
                                gS[:, ct, tp * 1024:(tp + 1) * 1024], psT4[:, :]),
                                waits=[trw, b.vv()], inc="V")
                            psT4_free = b.vv()
                            gidx += 1
                gs_wl = [psT4_free, psT4_free]

                with nc.psum_tensor(f"pg_{ei}", [128, 512], F32) as pg256:
                    for dh2 in range(2):
                        for q in range(4):
                            dt = 4 * dh2 + q
                            pbq = pB[q] if q < 3 else pg256
                            if dh2 == 0:
                                if ei == 0:
                                    wl = [hTw, b.vv(), b.ss()]
                                else:
                                    wl = [hTw, rflush[7], s_ffn]
                                if q == 3:
                                    wl = wl + gs_wl
                            else:
                                wl = [hTw, ("S", hg_S[(0, q)])]
                            for ti in range(OT):
                                b.emit("T", lambda e, ti=ti, dt=dt, q=q: e.matmul(
                                    pA[q][:, :], hTok[:, ti, dt * 128:(dt + 1) * 128],
                                    gT[:, ti, 0:512], start=(ti == 0), stop=(ti == OT - 1)),
                                    waits=(wl if ti == 0 else ()), inc=("T" if ti == OT - 1 else None))
                                b.emit("T", lambda e, ti=ti, dt=dt, pbq=pbq: e.matmul(
                                    pbq[:, 0:256], hTok[:, ti, dt * 128:(dt + 1) * 128],
                                    gT[:, ti, 512:768], start=(ti == 0), stop=(ti == OT - 1)),
                                    inc=("T" if ti == OT - 1 else None))
                            mmw = b.tt()
                            b.emit("S", lambda e, dt=dt, q=q: e.activation(
                                hgT[:, dt, 0:512], pA[q][:, :], AF.Identity),
                                waits=[mmw, b.ss()], inc="S")
                            b.emit("S", lambda e, dt=dt, pbq=pbq: e.activation(
                                hgT[:, dt, 512:768], pbq[:, 0:256], AF.Identity),
                                waits=[b.ss()], inc="S")
                            hg_S[(dh2, q)] = b.c("S")
                gather_T = b.tt()
                hg_ready = [("S", b.c("S"))]
                if ei + 1 < 8:
                    build_gT(ei + 1)

                for ci in range(6):
                    stp = ei * 6 + ci
                    f0, nf = FCH[ci]
                    buf = stp % 2
                    lw = wload[stp]
                    act_w = {}
                    tw_bank = {}
                    for fl in range(nf):
                        gb = fl % 2
                        wl = [lw] + hg_ready
                        if gu_war.get(("g", gb)) is not None:
                            wl.extend(gu_war[("g", gb)])
                        for half, (pp, w0, w1) in enumerate(
                                ((pA, 0, 512), (pB, 512, 768))):
                            for dt in range(DT):
                                b.emit("T", lambda e, gb=gb, buf=buf, fl=fl, dt=dt, pp=pp, w0=w0, w1=w1: e.matmul(
                                    pp[gb][:, 0:w1 - w0], gbuf[buf][:, dt, fl * 128:(fl + 1) * 128],
                                    hgT[:, dt, w0:w1], start=(dt == 0), stop=(dt == DT - 1)),
                                    waits=(wl if (dt == 0 and half == 0) else ()),
                                    inc=("T" if dt == DT - 1 else None))
                        gmm = b.tt()
                        wl = [lw]
                        if gu_war.get(("u", gb)) is not None:
                            wl.extend(gu_war[("u", gb)])
                        if gu_war.get(("ub",)) is not None:
                            wl.extend(gu_war[("ub",)])
                        for half, (upp, w0, w1) in enumerate(
                                ((pA[2 + gb], 0, 512), (pB[2], 512, 768))):
                            for dt in range(DT):
                                b.emit("T", lambda e, buf=buf, fl=fl, dt=dt, upp=upp, w0=w0, w1=w1: e.matmul(
                                    upp[:, 0:w1 - w0], ubuf[buf][:, dt, fl * 128:(fl + 1) * 128],
                                    hgT[:, dt, w0:w1], start=(dt == 0), stop=(dt == DT - 1)),
                                    waits=(wl if (dt == 0 and half == 0) else ()),
                                    inc=("T" if dt == DT - 1 else None))
                        umm = b.tt()
                        wl = [gmm, b.ss()]
                        if t_free is not None:
                            wl.append(t_free)
                        b.emit("S", lambda e, gb=gb: e.activation(
                            t_sb[:, 0:512], pA[gb][:, :], AF.Tanh, scale=0.5),
                            waits=wl, inc="S")
                        b.emit("S", lambda e, gb=gb: e.activation(
                            t_sb[:, 512:768], pB[gb][:, 0:256], AF.Tanh, scale=0.5),
                            waits=[b.ss()], inc="S")
                        tw = b.ss()
                        b.emit("V", lambda e, gb=gb: e.scalar_tensor_tensor(
                            gp_sb[:, 0:512], t_sb[:, 0:512], 1.0, pA[gb][:, :],
                            ALU.add, ALU.mult),
                            waits=[gmm, tw, b.vv()], inc="V")
                        b.emit("V", lambda e, gb=gb: e.scalar_tensor_tensor(
                            gp_sb[:, 512:768], t_sb[:, 512:768], 1.0, pB[gb][:, 0:256],
                            ALU.add, ALU.mult),
                            waits=[b.vv()], inc="V")
                        gu_war[("g", gb)] = [b.vv(), tw]
                        tw_bank[gb] = tw
                        t_free = b.vv()
                        b.emit("V", lambda e, gb=gb, fl=fl: e.tensor_mul(
                            act_sb[:, fl, 0:512], gp_sb[:, 0:512], pA[2 + gb][:, :]),
                            waits=[umm, b.vv()] + ([act_T] if (fl == 0 and act_T) else []),
                            inc="V")
                        act_h0 = b.vv()
                        b.emit("V", lambda e, fl=fl: e.tensor_mul(
                            act_sb[:, fl, 512:768], gp_sb[:, 512:768], pB[2][:, 0:256]),
                            waits=[b.vv()], inc="V")
                        gu_war[("u", gb)] = [b.vv()]
                        gu_war[("ub",)] = [b.vv()]
                        act_w[fl] = b.vv()
                    act_ready = act_w[nf - 1]
                    for sp in range(3):
                        for q2 in range(2):
                            stile = sp * 2 + q2
                            for dh in range(2):
                                bank = q2 * 2 + dh
                                wl = [act_h0 if sp < 2 else act_ready, lw]
                                if dw_war.get(bank) is not None:
                                    wl.append(dw_war[bank])
                                wl = wl + [wload[("d", stp)]]
                                for fl in range(nf):
                                    b.emit("T", lambda e, bank=bank, fl=fl, stile=stile, dh=dh, nf=nf, buf=buf: e.matmul(
                                        pA[bank][:, :],
                                        act_sb[:, fl, stile * 128:(stile + 1) * 128],
                                        dbuf[buf][:, fl, dh * 512:(dh + 1) * 512],
                                        start=(fl == 0), stop=(fl == nf - 1)),
                                        waits=(wl if fl == 0 else ()),
                                        inc=("T" if fl == nf - 1 else None))
                                dmm = b.tt()
                                if ci == 0:
                                    b.emit("V", lambda e, bank=bank, stile=stile, dh=dh: e.tensor_copy(
                                        dwacc[:, stile, dh * 512:(dh + 1) * 512], pA[bank][:, :]),
                                        waits=[dmm, b.vv()], inc="V")
                                else:
                                    b.emit("V", lambda e, bank=bank, stile=stile, dh=dh: e.tensor_add(
                                        dwacc[:, stile, dh * 512:(dh + 1) * 512], pA[bank][:, :],
                                        dwacc[:, stile, dh * 512:(dh + 1) * 512]),
                                        waits=[dmm, b.vv()], inc="V")
                                dw_war[bank] = b.vv()
                    act_T = b.tt()
                    for _gb in (0, 1):
                        gu_war[("g", _gb)] = [dw_war[_gb], tw_bank[_gb]]
                        gu_war[("u", _gb)] = [dw_war[2 + _gb]]
                    gu_war[("ub",)] = [act_w[nf - 1]]
                    wfree[buf] = b.tt()
                    dfree[stp % 2] = b.tt()
                    if stp + 2 < 48:
                        load_chunk(stp + 2)
                        load_dchunk(stp + 2)
                dw_ready = b.vv()
                s_ffn = b.ss()

                with nc.psum_tensor(f"psc_{ei}", [128, 512], F32) as psc:
                    setA = [pA[0], pA[1], pA[2], pA[3]]
                    setB = [pB[0], pB[1], pB[2], psc]
                    stfree = {0: None, 1: None}
                    for rnd in range(8):
                        banks = setB if rnd % 2 == 0 else setA
                        if ei == 7:
                            for k2 in range(2):
                                ti = rnd * 2 + k2
                                wlx = [("dX2W", dX2W_all), ("dX2X", dX2X_all)]
                                if stfree[k2] is not None:
                                    wlx.append(stfree[k2])
                                xsem = "dX2R" if k2 == 0 else "dX2S"
                                b.emit("Y", lambda e, ti=ti, k2=k2: e.dma_start(
                                    x2t[k2][:, :], x2_s[ti * 128:(ti + 1) * 128, :]),
                                    waits=wlx, inc=xsem, k=16)
                                xld[ti] = (xsem, b.c(xsem))
                        if rnd < 2:
                            wl = gs_wl + [dw_ready, s_ffn]
                        else:
                            wl = [rflush[rnd - 2]]
                        for ct in range(CT):
                            for tg in range(4):
                                ti = rnd * 2 + tg // 2
                                dh = tg % 2
                                pp = banks[tg]
                                b.emit("T", lambda e, ct=ct, ti=ti, dh=dh, pp=pp: e.matmul(
                                    pp[:, :], gS[:, ct, ti * 128:(ti + 1) * 128],
                                    dwacc[:, ct, dh * 512:(dh + 1) * 512],
                                    start=(ct == 0), stop=(ct == CT - 1)),
                                    waits=(wl if (ct == 0 and tg == 0) else ()),
                                    inc=("T" if (ct == CT - 1 and tg == 3) else None))
                        smm = b.tt()
                        for tg in range(4):
                            ti = rnd * 2 + tg // 2
                            dh = tg % 2
                            pp = banks[tg]
                            if ei == 0:
                                b.emit("V", lambda e, ti=ti, dh=dh, pp=pp, ei=ei: e.tensor_single_scalar(
                                    acc[:, ti, dh * 512:(dh + 1) * 512], pp[:, :],
                                    w_sb[:, ti, ei:ei + 1], ALU.mult),
                                    waits=[smm, b.vv()], inc="V")
                            else:
                                b.emit("V", lambda e, ti=ti, dh=dh, pp=pp, ei=ei: e.scalar_tensor_tensor(
                                    acc[:, ti, dh * 512:(dh + 1) * 512], pp[:, :],
                                    w_sb[:, ti, ei:ei + 1],
                                    acc[:, ti, dh * 512:(dh + 1) * 512],
                                    ALU.mult, ALU.add),
                                    waits=[smm, b.vv()], inc="V")
                        rflush[rnd] = b.vv()
                        if ei == 7:
                            for k2 in range(2):
                                ti = rnd * 2 + k2
                                b.emit("V", lambda e, ti=ti, k2=k2: e.tensor_add(
                                    x2t[k2][:, :], acc[:, ti, :], x2t[k2][:, :]),
                                    waits=[xld[ti], b.vv()], inc="V")
                                ow = b.vv()
                                b.emit("Y", lambda e, ti=ti, k2=k2: e.dma_start(
                                    out_d[ti * 128:(ti + 1) * 128, :], x2t[k2][:, :]),
                                    waits=[ow], inc=("dOut" if k2 == 0 else "dOu2"), k=16)
                                stfree[k2] = (("dOut" if k2 == 0 else "dOu2"),
                                              b.c("dOut" if k2 == 0 else "dOu2"))
                for q in range(4):
                    hg_S[(0, q)] = None
                    hg_S[(1, q)] = None
                gu_war = {}
                dw_war = {}
                hg_ready = []
                act_T = None
                hTw = ("dHR", b.c("dHR"))
            b.wait_only("Y", [("dOut", b.c("dOut")), ("dOu2", b.c("dOu2"))])

        blk = E(nc.Block())
        opsd = b.ops
        blk.tensor(lambda e: [r(e) for r in opsd["T"]] and None)
        blk.vector(lambda e: [r(e) for r in opsd["V"]] and None)
        blk.scalar(lambda e: [r(e) for r in opsd["S"]] and None)
        blk.gpsimd(lambda e: [r(e) for r in opsd["G"]] and None)
        blk.sync(lambda e: [r(e) for r in opsd["Y"]] and None)
    return nc


_NC_CACHE = {}


def _enable_ldw_opt():
    try:
        from concourse.compiler_utils import get_compiler_flags, set_compiler_flags
        flags = get_compiler_flags()
        nf = [f.replace('--enable-ldw-opt=false', '--enable-ldw-opt=true') for f in flags]
        if nf != flags:
            set_compiler_flags(nf)
    except Exception:
        pass


def _host_consts(p):
    hd = 64
    j = np.arange(32)
    invf = (1.0 / THETA ** (2 * j / hd)).astype(np.float64)
    t = np.arange(SQ, dtype=np.float64)
    ang = t[None, :] * invf[:, None]
    cos = np.cos(ang)
    sin = np.sin(ang)
    pp = np.arange(128)
    jj = pp & 31
    lower = (pp & 63) < 32
    cosK = cos[jj, :]
    sinK = np.where(lower[:, None], -sin[jj, :], sin[jj, :])
    own_tok = ((2 * np.arange(OT) + p)[:, None] * 128 + np.arange(128)[None, :]).reshape(-1)
    cosQ = cosK[:, own_tok]
    sinQ = sinK[:, own_tok]

    masks = np.zeros((128, 8, 512), dtype=np.float32)
    kk = np.arange(128)
    tri = (kk[:, None] <= kk[None, :]).astype(np.float32)
    for r in range(8):
        for i in range(4):
            lim = 2 * i + p
            if r < lim:
                masks[:, r, i * 128:(i + 1) * 128] = 1.0
            elif r == lim:
                masks[:, r, i * 128:(i + 1) * 128] = tri

    permM = np.zeros((128, 128), dtype=np.float32)
    for m in range(128):
        src = m + 32 if (m & 63) < 32 else m - 32
        permM[src, m] = 1.0
    sel0 = np.zeros((128, 128), dtype=np.float32)
    sel0[0, :] = 1.0
    return (np.ascontiguousarray(cosK, dtype=np.float32), np.ascontiguousarray(sinK, dtype=np.float32),
            np.ascontiguousarray(cosQ, dtype=np.float32), np.ascontiguousarray(sinQ, dtype=np.float32),
            np.ascontiguousarray(masks, dtype=BFD), np.ascontiguousarray(permM, dtype=BFD),
            np.eye(128, dtype=BFD), np.ascontiguousarray(sel0, dtype=BFD))


def kernel(L0_attn_x, L0_attn_w_qkv, L0_attn_b_qkv, L0_attn_w_out, L0_attn_b_out,
           L0_ffn_router_w, L0_ffn_gate_proj, L0_ffn_up_proj, L0_ffn_down_proj):
    x = np.ascontiguousarray(np.asarray(L0_attn_x), dtype=np.float32)
    wqkv = np.asarray(L0_attn_w_qkv, dtype=np.float32)
    bqkv = np.asarray(L0_attn_b_qkv, dtype=np.float32)
    wout = np.asarray(L0_attn_w_out, dtype=np.float32)
    bout = np.asarray(L0_attn_b_out, dtype=np.float32)
    rw = np.ascontiguousarray(np.asarray(L0_ffn_router_w), dtype=np.float32)
    gate = np.ascontiguousarray(np.asarray(L0_ffn_gate_proj), dtype=np.float32)
    up = np.ascontiguousarray(np.asarray(L0_ffn_up_proj), dtype=np.float32)
    down = np.ascontiguousarray(np.asarray(L0_ffn_down_proj), dtype=np.float32)

    perm = np.concatenate([64 * h + np.r_[0:64:2, 1:64:2] for h in range(NHEAD)])

    def wprep(w):
        return np.ascontiguousarray(w.T.reshape(DT, 128, D).transpose(1, 0, 2))

    wqT = wprep(wqkv[0:D][perm])
    wkT = wprep(wqkv[D:2 * D][perm])
    wvT = wprep(wqkv[2 * D:3 * D])
    woT = wprep(wout)

    def fprep6(w):
        out = np.empty((8, 5, 128, DT, 512), np.float32)
        for ci in range(5):
            blk = w[:, :, ci * 512:(ci + 1) * 512]
            out[:, ci] = blk.reshape(8, DT, 128, 512).transpose(0, 2, 1, 3)
        return np.ascontiguousarray(out)

    def fprep4(w):
        blk = w[:, :, 2560:2816]
        return np.ascontiguousarray(blk.reshape(8, DT, 128, 256).transpose(0, 2, 1, 3))

    def dprep6(w):
        out = np.empty((8, 5, 128, 4, D), np.float32)
        for ci in range(5):
            blk = w[:, ci * 512:(ci + 1) * 512, :]
            out[:, ci] = blk.reshape(8, 4, 128, D).transpose(0, 2, 1, 3)
        return np.ascontiguousarray(out)

    def dprep4(w):
        blk = w[:, 2560:2816, :]
        return np.ascontiguousarray(blk.reshape(8, 2, 128, D).transpose(0, 2, 1, 3))

    iota768 = np.ascontiguousarray(np.tile(np.arange(768, dtype=np.float32), (128, 1)))
    _k = np.arange(128)
    trils = (_k[:, None] < _k[None, :]).astype(BFD)
    onesf = np.ones((128, 128), BFD)
    gate6, gate4 = fprep6(gate), fprep4(gate)
    up6, up4 = fprep6(up), fprep4(up)
    down6, down4 = dprep6(down), dprep4(down)
    rwR = np.ascontiguousarray(rw.reshape(DT, 128, 8).transpose(1, 0, 2))
    bq128 = np.ascontiguousarray(bqkv[0:D][perm].reshape(8, 128).T)
    bk128 = np.ascontiguousarray(bqkv[D:2 * D][perm].reshape(8, 128).T)
    bvbc = np.ascontiguousarray(np.broadcast_to(bqkv[2 * D:3 * D], (128, D)))
    bobc = np.ascontiguousarray(np.broadcast_to(bout, (128, D)))

    _enable_ldw_opt()
    if "nc" not in _NC_CACHE:
        _NC_CACHE["nc"] = build_program()
    nc = _NC_CACHE["nc"]

    in_maps = []
    for core in range(8):
        bb, p = core // 2, core % 2
        cosK, sinK, cosQ, sinQ, masks, permM, id128, sel0 = _host_consts(p)
        xb = np.ascontiguousarray(x[bb])
        own = 2 * np.arange(OT) + p
        xown = np.ascontiguousarray(xb.reshape(NT, 128, D)[own].reshape(2048, D))
        in_maps.append(dict(
            x=xb, xown=xown, wqT=wqT, wkT=wkT, wvT=wvT, woT=woT,
            bq128=bq128, bk128=bk128, bvbc=bvbc, bobc=bobc, rw=rwR,
            gate6=gate6, gate4=gate4, up6=up6, up4=up4, down6=down6, down4=down4,
            cosK=cosK, sinK=sinK, cosQ=cosQ, sinQ=sinQ, masks=masks,
            perm128=permM, id128=id128, sel0=sel0,
            iota768=iota768, trils=trils, onesf=onesf))

    res = run_bass_kernel_spmd(nc, in_maps, core_ids=list(range(8)))
    full = np.empty((4, SQ, D), dtype=np.float32)
    for core in range(8):
        bb, p = core // 2, core % 2
        o = np.asarray(res.results[core]["out"]).reshape(OT, 128, D)
        full[bb].reshape(NT, 128, D)[2 * np.arange(OT) + p] = o
    return full
